# version 01
# baseline (speedup 1.0000x reference)
import numpy as np
import contextlib
import concourse.bass as bass
import concourse.mybir as mybir
from concourse.bass_utils import run_bass_kernel_spmd

F32 = mybir.dt.float32
BF16 = mybir.dt.bfloat16
AF = mybir.ActivationFunctionType
ALU = mybir.AluOpType
AX = mybir.AxisListType

HID, NH, DK, DV = 1024, 4, 256, 256
L, HALF = 4096, 2048
SEG = 256
NSEG = L // SEG
NT = SEG // 128
NE, EI = 32, 768
TOK = HALF
PB_TOK = 1024
PB_CH = 512


class Ctr:
    def __init__(self, sem):
        self.sem = sem
        self.n = 0

    def tick(self, instr, k=1):
        instr.then_inc(self.sem, k)
        self.n += k
        return self.n


def build_program(nseg=NSEG, nexp=NE, taps=(), own0=NSEG // 2, npass=2):
    nc = bass.Bass("TRN2", target_bir_lowering=False, detect_race_conditions=False)

    dx = nc.dram_tensor("x", [L, HID], F32, kind="ExternalInput").ap()
    dmask = nc.dram_tensor("tokmask", [128, L // 128], F32, kind="ExternalInput").ap()
    dWq = nc.dram_tensor("Wq", [HID, HID], F32, kind="ExternalInput").ap()
    dWk = nc.dram_tensor("Wk", [HID, HID], F32, kind="ExternalInput").ap()
    dWv = nc.dram_tensor("Wv", [HID, HID], F32, kind="ExternalInput").ap()
    dWb = nc.dram_tensor("Wb", [HID, NH], F32, kind="ExternalInput").ap()
    dWo = nc.dram_tensor("Wo", [HID, HID], F32, kind="ExternalInput").ap()
    dconv = nc.dram_tensor("convtaps", [128, 3, 8, 4], F32, kind="ExternalInput").ap()
    dfirs = nc.dram_tensor("firs", [128, 8, 3], F32, kind="ExternalInput").ap()
    dfirl = nc.dram_tensor("firl", [128, 8, 31], F32, kind="ExternalInput").ap()
    dgW1 = nc.dram_tensor("gW1", [5120, 512], F32, kind="ExternalInput").ap()
    dgb1 = nc.dram_tensor("gb1", [1, 512], F32, kind="ExternalInput").ap()
    dgW2 = nc.dram_tensor("gW2", [512, 4], F32, kind="ExternalInput").ap()
    dgb2 = nc.dram_tensor("gb2", [1, 4], F32, kind="ExternalInput").ap()
    dtemp = nc.dram_tensor("temp", [1, 4], F32, kind="ExternalInput").ap()
    dfloor = nc.dram_tensor("floorv", [1, 16], F32, kind="ExternalInput").ap()
    donorm = nc.dram_tensor("onorm", [1, DV], F32, kind="ExternalInput").ap()
    dcT = nc.dram_tensor("c_bT", [128, 8], F32, kind="ExternalInput").ap()
    dwmod = nc.dram_tensor("wmod", [HID, 6 * HID], F32, kind="ExternalInput").ap()
    dgatew = nc.dram_tensor("gate_wT", [HID, NE], F32, kind="ExternalInput").ap()
    debias = nc.dram_tensor("e_bias", [1, NE], F32, kind="ExternalInput").ap()
    dw1 = nc.dram_tensor("w1", [NE, HID, EI], F32, kind="ExternalInput").ap()
    dw3 = nc.dram_tensor("w3", [NE, HID, EI], F32, kind="ExternalInput").ap()
    dw2 = nc.dram_tensor("w2", [NE, EI, HID], F32, kind="ExternalInput").ap()
    dident = nc.dram_tensor("identity", [128, 128], BF16, kind="ExternalInput").ap()
    didentf = nc.dram_tensor("identityf", [128, 128], F32, kind="ExternalInput").ap()
    dmaskA = nc.dram_tensor("maskA", [128, 128], F32, kind="ExternalInput").ap()
    dmaskAU = nc.dram_tensor("maskAU", [128, 128], F32, kind="ExternalInput").ap()
    dmaskT = nc.dram_tensor("maskT", [128, 128], F32, kind="ExternalInput").ap()
    dout = nc.dram_tensor("out", [TOK, HID], F32, kind="ExternalOutput").ap()
    d_xmid = nc.dram_tensor("xmid_scr", [TOK, HID], F32, kind="Internal").ap()
    d_h2 = nc.dram_tensor("h2_scr", [TOK, HID], BF16, kind="Internal").ap()

    TAPSPEC = {
        "h": ([L, HID], BF16), "q": ([L, HID], BF16), "k": ([L, HID], BF16),
        "v": ([L, HID], BF16), "beta": ([L, NH], F32),
        "qn": ([L, HID], BF16), "kn": ([L, HID], BF16),
        "uw": ([L // 128, 128, NH, 512], BF16), "delta_o": ([L, HID], BF16),
        "fir_s": ([TOK, HID], BF16), "fir_l": ([TOK, HID], BF16),
        "stat": ([TOK // 128, 128, NH, 12], F32), "wgt": ([TOK // 128, 128, NH, 4], F32),
        "o_mix": ([TOK, HID], BF16), "x_mid": ([TOK, HID], F32), "h2": ([TOK, HID], BF16),
        "combine": ([TOK, NE], F32),
        "setup": ([8, 128, HID], F32),
    }
    dbg = {}
    for t in taps:
        shp, dt = TAPSPEC[t]
        dbg[t] = nc.dram_tensor("dbg_" + t, shp, dt, kind="ExternalOutput").ap()

    es = contextlib.ExitStack()

    def sbt(name, shape, dt=F32):
        return es.enter_context(nc.sbuf_tensor(name, shape, dt))

    def mksem(name):
        return Ctr(es.enter_context(nc.semaphore(name)))

    cT = mksem("cT")
    cV = mksem("cV")
    cS = mksem("cS")
    cG = mksem("cG")
    dS = mksem("dS")
    dGm = mksem("dGm")
    dA = mksem("dA")

    ident = sbt("ident", [128, 128], BF16)
    identf = sbt("identf", [128, 128], F32)
    maskA_s = sbt("maskA_s", [128, 128])
    maskAU_s = sbt("maskAU_s", [128, 128])
    maskT_s = sbt("maskT_s", [128, 128])
    convt = sbt("convt", [128, 3, 8, 4])
    firs_t = sbt("firs_t", [128, 8, 3])
    firl_t = sbt("firl_t", [128, 8, 31])
    mask_sb = sbt("mask_sb", [128, L // 128])
    Wb_s = sbt("Wb_s", [128, 8, NH], BF16)
    W1s = sbt("W1s", [12, 512], BF16)
    gW2h = sbt("gW2h", [128, 4, 4 * NH], BF16)
    gb2h = sbt("gb2h", [1, 4 * NH], BF16)
    gb1_b = sbt("gb1_b", [1, 512], BF16)
    floorb = sbt("floorb", [128, NH * 4])
    onormb = sbt("onormb", [128, DV])
    ebias_b = sbt("ebias_b", [128, NE])
    gatewT = sbt("gatewT", [128, 8, NE], BF16)
    invt_c = sbt("invt_c", [128, 4])
    ones1 = sbt("ones1", [1, 512], BF16)
    onec = sbt("onec", [128, 1], BF16)
    zrow = sbt("zrow", [1, 512], BF16)
    sa_b = sbt("sa_b", [128, HID], BF16)
    ca1_b = sbt("ca1_b", [128, HID], BF16)
    ga_b = sbt("ga_b", [128, HID], BF16)
    sbv_b = sbt("sbv_b", [128, HID], BF16)
    cb1_b = sbt("cb1_b", [128, HID], BF16)
    gb_b = sbt("gb_b", [128, HID], BF16)

    block = es.enter_context(nc.Block())

    def E(name):
        def deco(f):
            getattr(block, name)(f)
        return deco

    def dma(eng, ctr, out, in_, **kw):
        ctr.tick(eng.dma_start(out=out, in_=in_, **kw), 16)

    KT = lambda d: d.rearrange("(k p) e -> p k e", p=128)

    sc = contextlib.ExitStack()

    def sbl(name, shape, dt=F32):
        return sc.enter_context(nc.sbuf_tensor(name, shape, dt))

    modrow = sbl("modrow", [1, 6 * HID])
    modrow_b = sbl("modrow_b", [1, 6 * HID], BF16)
    c_sb = sbl("c_sb", [128, 8])
    silc = sbl("silc", [128, 8], BF16)
    wmbuf = sbl("wmbuf", [128, 2, 8, 512], BF16)
    g1stg = sbl("g1stg", [128, 2, 512], BF16)
    gW2f = sbl("gW2f", [128, 4, 4])
    rows_s = sbl("rows_s", [1, 64])
    rows_b = sbl("rows_b", [1, 64], BF16)
    onr = sbl("onr", [1, DV])
    onr_b = sbl("onr_b", [1, DV], BF16)
    ebr = sbl("ebr", [1, NE])
    ebr_b = sbl("ebr_b", [1, NE], BF16)
    clampt = sbl("clampt", [128, HID])
    clampt2 = sbl("clampt2", [128, HID])
    w1row = sbl("w1row", [1, 512], BF16)
    pSu = sc.enter_context(nc.psum_tensor("pSu", [128, 512], F32))
    pRu = sc.enter_context(nc.psum_tensor("pRu", [1, 512], F32))

    @E("sync")
    def _(sy):
        for o, i in [(ident[:], dident), (identf[:], didentf), (maskA_s[:], dmaskA),
                     (maskAU_s[:], dmaskAU), (maskT_s[:], dmaskT), (convt[:], dconv),
                     (firs_t[:], dfirs), (firl_t[:], dfirl), (mask_sb[:], dmask),
                     (c_sb[:], dcT), (gW2f[:], KT(dgW2)),
                     (rows_s[0:1, 0:4], dgb2), (rows_s[0:1, 4:8], dtemp),
                     (rows_s[0:1, 16:32], dfloor), (onr[:], donorm), (ebr[:], debias)]:
            dma(sy, dS, o, i)
    ld_smalls = dS.n

    @E("gpsimd")
    def _(g):
        dma(g, dGm, Wb_s[:], KT(dWb))
        dma(g, dGm, gatewT[:], KT(dgatew))
        dma(g, dGm, gb1_b[:], dgb1)
    n_pre_g = 3

    @E("vector")
    def _(v):
        v.memset(ones1[:], 1.0)
        v.memset(onec[:], 1.0)
        cV.tick(v.memset(zrow[:], 0.0))
    ones_done = cV.n

    @E("scalar")
    def _(s):
        s.wait_ge(dS.sem, ld_smalls)
        cS.tick(s.activation(silc[:], c_sb[:], AF.Silu))
    silc_done = cS.n

    mod_evac = {}
    for n in range(12):
        @E("gpsimd")
        def _(g, n=n):
            if n >= 2:
                g.wait_ge(cS.sem, mod_evac[n - 2])
            dma(g, dGm, wmbuf[:, n % 2], KT(dwmod[:, 512 * n:512 * (n + 1)]))

        @E("tensor")
        def _(t, n=n):
            t.wait_ge(dGm.sem, dGm.n)
            if n == 0:
                t.wait_ge(cS.sem, silc_done)
            else:
                t.wait_ge(cS.sem, mod_evac[n - 1])
            for k in range(8):
                ins = t.matmul(pRu[:], silc[:, k:k + 1], wmbuf[:, n % 2, k, :],
                               start=(k == 0), stop=(k == 7))
            cT.tick(ins)

        @E("scalar")
        def _(s, n=n):
            s.wait_ge(cT.sem, cT.n)
            cS.tick(s.copy(modrow[0:1, 512 * n:512 * (n + 1)], pRu[:]))
        mod_evac[n] = cS.n

    @E("vector")
    def _(v):
        v.wait_ge(cS.sem, mod_evac[11])
        cV.tick(v.tensor_copy(modrow_b[:], modrow[:]))
    modb_done = cV.n

    vec_tiles = [sa_b, ca1_b, ga_b, sbv_b, cb1_b, gb_b]
    for vi in range(6):
        for half in range(2):
            @E("tensor")
            def _(t, vi=vi, half=half):
                t.wait_ge(cV.sem, modb_done if (vi == 0 and half == 0) else cV.n)
                cT.tick(t.matmul(pSu[:], ones1[0:1, 0:128],
                                 modrow_b[0:1, HID * vi + 512 * half: HID * vi + 512 * (half + 1)],
                                 start=True, stop=True))

            @E("vector")
            def _(v, vi=vi, half=half):
                v.wait_ge(cT.sem, cT.n)
                dst = vec_tiles[vi][:, 512 * half:512 * (half + 1)]
                if vi in (1, 4):
                    cV.tick(v.tensor_scalar(dst, pSu[:], 1.0, None, ALU.add))
                else:
                    cV.tick(v.tensor_copy(dst, pSu[:]))

    @E("vector")
    def _(v):
        v.wait_ge(dS.sem, ld_smalls)
        cV.tick(v.tensor_copy(rows_b[0:1, 32:33], rows_s[0:1, 16:17]))

    for gt in (ga_b, gb_b):
        @E("scalar")
        def _(s, gt=gt):
            s.wait_ge(cV.sem, cV.n)
            cS.tick(s.activation(clampt2[:], gt[:], AF.Abs))

        @E("vector")
        def _(v, gt=gt):
            v.wait_ge(cS.sem, cS.n)
            v.tensor_scalar(clampt[:], gt[:], 0.0, None, ALU.is_ge)
            v.tensor_scalar(clampt[:], clampt[:], 2.0, -1.0, ALU.mult, ALU.add)
            v.tensor_scalar(clampt2[:], clampt2[:], 0.25, None, ALU.max)
            cV.tick(v.tensor_tensor(gt[:], clampt[:], clampt2[:], ALU.mult))

    @E("vector")
    def _(v):
        v.wait_ge(cV.sem, cV.n)
        v.tensor_scalar(rows_s[0:1, 8:12], rows_s[0:1, 4:8], 0.2, 10.0, ALU.max, ALU.min)
        v.drain()
        v.reciprocal(rows_s[0:1, 12:16], rows_s[0:1, 8:12])
        v.drain()
        v.tensor_scalar(rows_s[0:1, 16:32], rows_s[0:1, 16:32], 1e-7, 0.1, ALU.max, ALU.min)
        v.drain()
        v.tensor_copy(rows_b[0:1, 0:32], rows_s[0:1, 0:32])
        v.tensor_copy(onr_b[:], onr[:])
        cV.tick(v.tensor_copy(ebr_b[:], ebr[:]))

    for src, dst, wid in [(rows_b[0:1, 12:16], invt_c[:], 4),
                          (rows_b[0:1, 16:32], floorb[:], 16),
                          (ebr_b[0:1, :], ebias_b[:], NE),
                          (onr_b[0:1, :], onormb[:], DV)]:
        @E("tensor")
        def _(t, src=src, wid=wid):
            t.wait_ge(cV.sem, cV.n)
            cT.tick(t.matmul(pSu[:, 0:wid], ones1[0:1, 0:128], src, start=True, stop=True))

        @E("vector")
        def _(v, dst=dst, wid=wid):
            v.wait_ge(cT.sem, cT.n)
            cV.tick(v.tensor_copy(dst, pSu[:, 0:wid]))

    @E("vector")
    def _(v):
        v.wait_ge(cV.sem, cV.n)
        for h in range(NH):
            v.tensor_scalar(gW2h[:, :, 4 * h:4 * (h + 1)], gW2f[:], invt_c[:, h:h + 1], None, ALU.mult)
            ins = v.tensor_scalar(gb2h[0:1, 4 * h:4 * (h + 1)], rows_s[0:1, 0:4],
                                  rows_s[0:1, 12 + h:13 + h], None, ALU.mult)
        cV.tick(ins)

    w1s_evac = {}
    for j in range(24):
        @E("gpsimd")
        def _(g, j=j):
            if j >= 2:
                g.wait_ge(cS.sem, w1s_evac[j - 2])
            dma(g, dGm, g1stg[:, j % 2, :], dgW1[HID + 128 * j: HID + 128 * (j + 1), :])

        @E("tensor")
        def _(t, j=j):
            t.wait_ge(dGm.sem, dGm.n)
            if j >= 2:
                t.wait_ge(cS.sem, w1s_evac[j - 1])
            cT.tick(t.matmul(pRu[:], onec[:], g1stg[:, j % 2, :], start=(j % 2 == 0), stop=(j % 2 == 1)))

        if j % 2 == 1:
            @E("scalar")
            def _(s, j=j):
                s.wait_ge(cT.sem, cT.n)
                if j >= 3:
                    s.wait_ge(dS.sem, dS.n)
                grp = j // 2
                scale = (1.0 / 256.0) if (grp % 3 == 0) else 1.0
                cS.tick(s.activation(w1row[:], pRu[:], AF.Copy, scale=scale))

            @E("sync")
            def _(sy, j=j):
                sy.wait_ge(cS.sem, cS.n)
                dma(sy, dS, W1s[j // 2:j // 2 + 1, :], w1row[:])
        w1s_evac[j] = cS.n

    if "setup" in dbg:
        @E("gpsimd")
        def _(sy):
            sy.wait_ge(cS.sem, cS.n)
            sy.wait_ge(cV.sem, cV.n)
            for j, tsr in enumerate([sa_b, ca1_b, ga_b, sbv_b, cb1_b, gb_b]):
                dma(sy, dGm, dbg["setup"][j], tsr[:])
            dma(sy, dGm, dbg["setup"][6, 0:12, 0:512], W1s[:])
            dma(sy, dGm, dbg["setup"][7, :, 0:4], invt_c[:])
            dma(sy, dGm, dbg["setup"][7, :, 4:20], floorb[:])
            dma(sy, dGm, dbg["setup"][7, :, 20:276], onormb[:])
            dma(sy, dGm, dbg["setup"][7, 0:1, 276:340], rows_s[:])
            dma(sy, dGm, dbg["setup"][7, 1:2, 276:340], rows_b[:])
    w1s_dma = dS.n
    sc.close()

    pa = contextlib.ExitStack()

    def sba(name, shape, dt=F32):
        return pa.enter_context(nc.sbuf_tensor(name, shape, dt))

    x_sb = [sba("x_sb0", [128, NT, HID]), sba("x_sb1", [128, NT, HID])]
    h_bf = sba("h_bf", [128, NT, HID], BF16)
    hT = sba("hT", [128, 8, SEG], BF16)
    SCR = sba("SCR", [128, NT, HID])
    CONVB = sba("CONVB", [128, 8, SEG])
    tail3 = sba("tail3", [128, 3, 8, 3])
    qTc = sba("qTc", [128, 8, SEG], BF16)
    kTc = sba("kTc", [128, 8, SEG], BF16)
    vTc = [sba("vTc0", [128, 8, SEG], BF16), sba("vTc1", [128, 8, SEG], BF16)]
    TOKA = sba("TOKA", [128, NT, HID], BF16)
    TOKB = sba("TOKB", [128, NT, HID], BF16)
    v_tok = sba("v_tok", [128, NT, HID], BF16)
    kn_tok = sba("kn_tok", [128, NT, HID], BF16)
    TOKC = sba("TOKC", [128, NT, HID], BF16)
    Xu = sba("Xu", [128, NT, NH, 2 * DV], BF16)
    qnT = sba("qnT", [128, 8, SEG], BF16)
    knT = sba("knT", [128, 8, SEG], BF16)
    kbnT = sba("kbnT", [128, 8, SEG], BF16)
    wT_sb = sba("wT_sb", [128, 8, SEG], BF16)
    attnT = sba("attnT", [128, NT * NH, 128], BF16)
    S_bf = sba("S_bf", [128, NH, 2 * DV], BF16)
    TpP = sba("TpP", [128, 128], BF16)
    TpB5 = sba("TpB5", [128, 5, 128], BF16)
    beta_sb = sba("beta_sb", [128, NT, NH])
    statv = sba("statv", [128, NT, NH, 12])
    svT = sba("svT", [12, NH, SEG], BF16)
    hmG = sba("hmG", [128, 4, SEG], BF16)
    wgt = sba("wgt", [128, NT, NH, 4])
    h2_bf = sba("h2_bf", [128, NT, HID], BF16)
    stat_s = sba("stat_s", [128, 32])
    sqd = sba("sqd", [128, HID])
    vwin = sba("vwin", [128, 8, 286], BF16)
    W_ring = sba("W_ring", [128, 24, HID], BF16)
    ez_sb = sba("ez_sb", [128, 4])

    S_ps = [pa.enter_context(nc.psum_tensor(f"S_ps{h}", [128, 512], F32)) for h in range(NH)]
    pA = pa.enter_context(nc.psum_tensor("pA", [128, 512], F32))
    pB = pa.enter_context(nc.psum_tensor("pB", [128, 512], F32))

    @E("sync")
    def _(sy):
        dma(sy, dS, x_sb[0][:], dx[0:SEG, :].rearrange("(T p) d -> p T d", p=128))
    x_ld = {0: dS.n}

    @E("gpsimd")
    def _(g):
        dma(g, dGm, W_ring[:, 0:8, :], KT(dWq))
        dma(g, dGm, W_ring[:, 8:16, :], KT(dWk))
        dma(g, dGm, W_ring[:, 16:24, :], KT(dWv))
    w1_ld = {0: dGm.n}

    @E("vector")
    def _(v):
        v.memset(S_bf[:], 0.0)
        cV.tick(v.memset(tail3[:], 0.0))
    sbf_zero = cV.n

    setup_cS, setup_cV, setup_dS = cS.n, cV.n, dS.n

    @E("tensor")
    def _(t):
        t.wait_ge(cV.sem, max(ones_done, setup_cV))
        t.wait_ge(cS.sem, setup_cS)
        t.wait_ge(dS.sem, setup_dS)
        for h in range(NH):
            ins = t.matmul(S_ps[h][:], zrow[0:1, 0:128], zrow[0:1, 0:512],
                           start=True, stop=False, skip_group_check=True)
        cT.tick(ins)

    x_free = {0: (cV, 0), 1: (cV, 0)}
    scast = {h: (cS, 0) for h in range(NH)}
    vwin_guard = 0
    w4_ld = {}

    for i in range(nseg):
        buf = i % 2
        own = i >= own0
        orow = (i - own0) * SEG

        if i + 1 < nseg:
            @E("sync")
            def _(sy, i=i):
                ctr, val = x_free[(i + 1) % 2]
                if val:
                    sy.wait_ge(ctr.sem, val)
                dma(sy, dS, x_sb[(i + 1) % 2][:],
                    dx[SEG * (i + 1):SEG * (i + 2), :].rearrange("(T p) d -> p T d", p=128))
            x_ld[i + 1] = dS.n

        for T in range(NT):
            @E("vector")
            def _(v, i=i, T=T):
                v.wait_ge(dS.sem, x_ld[i])
                if i == 0 and T == 0:
                    v.wait_ge(cV.sem, sbf_zero)
                cV.tick(v.reduce_sum(stat_s[:, 0:1], x_sb[i % 2][:, T, :], axis=AX.X))

            @E("scalar")
            def _(s, i=i, T=T):
                s.wait_ge(dS.sem, x_ld[i])
                s.wait_ge(cV.sem, cV.n)
                cS.tick(s.activation(sqd[:], x_sb[i % 2][:, T, :], AF.Square, accum_out=stat_s[:, 1:2]))

            @E("vector")
            def _(v):
                v.wait_ge(cS.sem, cS.n)
                v.tensor_scalar(stat_s[:, 2:3], stat_s[:, 0:1], 1.0 / HID, None, ALU.mult)
                v.drain()
                v.tensor_tensor(stat_s[:, 3:4], stat_s[:, 2:3], stat_s[:, 2:3], ALU.mult)
                v.tensor_scalar(stat_s[:, 4:5], stat_s[:, 1:2], 1.0 / HID, 1e-5, ALU.mult, ALU.add)
                v.drain()
                cV.tick(v.tensor_tensor(stat_s[:, 4:5], stat_s[:, 4:5], stat_s[:, 3:4], ALU.subtract))

            @E("scalar")
            def _(s):
                s.wait_ge(cV.sem, cV.n)
                cS.tick(s.activation(stat_s[:, 5:6], stat_s[:, 4:5], AF.Sqrt))

            @E("vector")
            def _(v, i=i, T=T):
                v.wait_ge(cS.sem, cS.n)
                v.reciprocal(stat_s[:, 6:7], stat_s[:, 5:6])
                v.drain()
                v.tensor_scalar(sqd[:], x_sb[i % 2][:, T, :], stat_s[:, 2:3], stat_s[:, 6:7],
                                ALU.subtract, ALU.mult)
                v.tensor_tensor(sqd[:], sqd[:], ca1_b[:], ALU.mult)
                v.tensor_tensor(sqd[:], sqd[:], sa_b[:], ALU.add)
                cV.tick(v.tensor_scalar(h_bf[:, T, :], sqd[:],
                                        mask_sb[:, i * NT + T:i * NT + T + 1], None, ALU.mult))
        h_done = cV.n

        pTt_cm = nc.psum_tensor("pTt%d" % i, [128, 1024], BF16)
        with pTt_cm as pTt:
            for T in range(NT):
                @E("tensor")
                def _(t, T=T):
                    t.wait_ge(cV.sem, h_done if T == 0 else cV.n)
                    for k in range(8):
                        ins = t.transpose(pTt[:, 128 * k:128 * (k + 1)],
                                          h_bf[:, T, 128 * k:128 * (k + 1)], ident[:])
                    cT.tick(ins)

                @E("vector")
                def _(v, T=T):
                    v.wait_ge(cT.sem, cT.n)
                    cV.tick(v.tensor_copy(hT[:, :, 128 * T:128 * (T + 1)],
                                          pTt[:].rearrange("p (k t) -> p k t", t=128)))
            hT_done = cV.n

            @E("tensor")
            def _(t, i=i):
                t.wait_ge(cV.sem, hT_done)
                t.wait_ge(dGm.sem, w1_ld[i])
                t.wait_ge(cS.sem, scast[NH - 1][1])
                for T in range(NT):
                    for k in range(8):
                        ins = t.matmul(pB[:, 4 * T:4 * (T + 1)], hT[:, k, 128 * T:128 * (T + 1)],
                                       Wb_s[:, k, :], start=(T == 0 and k == 0),
                                       stop=(T == NT - 1 and k == 7))
                cT.tick(ins)

            @E("scalar")
            def _(s):
                s.wait_ge(cT.sem, cT.n)
                cS.tick(s.activation(beta_sb[:].rearrange("p T h -> p (T h)"),
                                     pB[:, 0:4 * NT], AF.Sigmoid))
            beta_done = cS.n

            q_on = own or (i == own0 - 1)
            tensors3 = [(qTc, 0, 0), (kTc, 8, 1), (vTc[buf], 16, 2)] if q_on else \
                       [(kTc, 8, 1), (vTc[buf], 16, 2)]
            for dst, base, wi in tensors3:
                for f in range(8):
                    @E("tensor")
                    def _(t, base=base, f=f):
                        t.wait_ge(cS.sem, cS.n)
                        for k in range(8):
                            ins = t.matmul(pA[:, 256 * (f % 2):256 * (f % 2 + 1)],
                                           W_ring[:, base + k, 128 * f:128 * (f + 1)],
                                           hT[:, k, :], start=(k == 0), stop=(k == 7))
                        cT.tick(ins)

                    @E("scalar")
                    def _(s, f=f):
                        s.wait_ge(cT.sem, cT.n)
                        scr8 = SCR[:].rearrange("p T (k2 z) -> p (T k2) z", z=SEG)
                        cS.tick(s.copy(scr8[:, f, :], pA[:, 256 * (f % 2):256 * (f % 2 + 1)]))
                raw_done = cS.n

                @E("vector")
                def _(v, wi=wi, i=i, raw_done=raw_done):
                    v.wait_ge(cS.sem, raw_done)
                    scr8 = SCR[:].rearrange("p T (k2 z) -> p (T k2) z", z=SEG)
                    for dt_ in range(8):
                        tp = lambda j: convt[:, wi, dt_, j:j + 1]
                        v.tensor_scalar(CONVB[:, dt_, :], scr8[:, dt_, :], tp(3), None, ALU.mult)
                        for sh in (1, 2, 3):
                            v.scalar_tensor_tensor(CONVB[:, dt_, sh:SEG], scr8[:, dt_, 0:SEG - sh],
                                                   tp(3 - sh), CONVB[:, dt_, sh:SEG], ALU.mult, ALU.add)
                            v.scalar_tensor_tensor(CONVB[:, dt_, 0:sh], tail3[:, wi, dt_, 3 - sh:3],
                                                   tp(3 - sh), CONVB[:, dt_, 0:sh], ALU.mult, ALU.add)
                    cV.tick(v.tensor_copy(tail3[:, wi, :, :], scr8[:, :, SEG - 3:SEG]))

                @E("scalar")
                def _(s, dst=dst, wi=wi):
                    s.wait_ge(cV.sem, cV.n)
                    if wi == 2 and vwin_guard:
                        s.wait_ge(cV.sem, vwin_guard)
                    cS.tick(s.activation(dst[:].rearrange("p k z -> p (k z)"),
                                         CONVB[:].rearrange("p k z -> p (k z)"), AF.Silu))
            qkv_done = cS.n

            if own:
                @E("gpsimd")
                def _(g):
                    g.wait_ge(cS.sem, qkv_done)
                    dma(g, dGm, W_ring[:, 0:8, :], KT(dWo))
                    dma(g, dGm, W_ring[:, 8:16, 0:512], KT(dgW1[0:HID, :]))
                    dma(g, dGm, W_ring[:, 16:24, 0:512], KT(dgW1[HID + 3 * DV * NH:5120, :]))
                w4_ld[i] = dGm.n

            srcdst = ([(qTc, TOKA)] if own else []) + [(kTc, TOKB), (vTc[buf], v_tok)]
            for src, dst2 in srcdst:
                for T in range(NT):
                    @E("tensor")
                    def _(t, src=src, T=T):
                        t.wait_ge(cS.sem, qkv_done)
                        t.wait_ge(cV.sem, cV.n)
                        for k in range(8):
                            ins = t.transpose(pTt[:, 128 * k:128 * (k + 1)],
                                              src[:, k, 128 * T:128 * (T + 1)], ident[:])
                        cT.tick(ins)

                    @E("vector")
                    def _(v, dst2=dst2, T=T):
                        v.wait_ge(cT.sem, cT.n)
                        cV.tick(v.tensor_copy(dst2[:, T, :],
                                              pTt[:].rearrange("p (k t) -> p k t", t=128)))
            tok_done = cV.n

            for T in range(NT):
                @E("scalar")
                def _(s, T=T, own=own):
                    s.wait_ge(cV.sem, tok_done)
                    for h in range(NH):
                        if own:
                            s.activation(sqd[:, 0:DV], TOKA[:, T, DV * h:DV * (h + 1)], AF.Square,
                                         accum_out=stat_s[:, 8 + h:9 + h])
                        ins = s.activation(sqd[:, 0:DV], TOKB[:, T, DV * h:DV * (h + 1)], AF.Square,
                                           accum_out=stat_s[:, 12 + h:13 + h])
                    cS.tick(ins)

                lo = 8 if own else 12

                @E("vector")
                def _(v, lo=lo):
                    v.wait_ge(cS.sem, cS.n)
                    cV.tick(v.tensor_scalar(stat_s[:, lo:16], stat_s[:, lo:16], 1e-6, None, ALU.add))

                @E("scalar")
                def _(s, lo=lo):
                    s.wait_ge(cV.sem, cV.n)
                    cS.tick(s.activation(stat_s[:, lo + 8:24], stat_s[:, lo:16], AF.Sqrt))

                @E("vector")
                def _(v, T=T, own=own, lo=lo):
                    v.wait_ge(cS.sem, cS.n)
                    v.reciprocal(stat_s[:, lo + 16:32], stat_s[:, lo + 8:24])
                    v.drain()
                    for h in range(NH):
                        if own:
                            v.tensor_scalar(TOKC[:, T, DV * h:DV * (h + 1)],
                                            TOKA[:, T, DV * h:DV * (h + 1)],
                                            stat_s[:, 24 + h:25 + h], None, ALU.mult)
                        v.tensor_scalar(kn_tok[:, T, DV * h:DV * (h + 1)],
                                        TOKB[:, T, DV * h:DV * (h + 1)],
                                        stat_s[:, 28 + h:29 + h], None, ALU.mult)
                        v.tensor_scalar(Xu[:, T, h, 0:DV], v_tok[:, T, DV * h:DV * (h + 1)],
                                        beta_sb[:, T, h:h + 1], None, ALU.mult)
                        ins = v.tensor_scalar(Xu[:, T, h, DV:2 * DV],
                                              kn_tok[:, T, DV * h:DV * (h + 1)],
                                              beta_sb[:, T, h:h + 1], None, ALU.mult)
                    cV.tick(ins)
            norm_done = cV.n

            tlist = ([("qn", qnT)] if own else []) + [("kn", knT), ("kb", kbnT)]
            for srcf, dstt in tlist:
                for T in range(NT):
                    @E("tensor")
                    def _(t, srcf=srcf, T=T):
                        t.wait_ge(cV.sem, cV.n)
                        for hh in range(8):
                            h, half = hh // 2, hh % 2
                            if srcf == "qn":
                                src_ap = TOKC[:, T, DV * h + 128 * half:DV * h + 128 * (half + 1)]
                            elif srcf == "kn":
                                src_ap = kn_tok[:, T, DV * h + 128 * half:DV * h + 128 * (half + 1)]
                            else:
                                src_ap = Xu[:, T, h, DV + 128 * half:DV + 128 * (half + 1)]
                            ins = t.transpose(pTt[:, 128 * hh:128 * (hh + 1)], src_ap, ident[:])
                        cT.tick(ins)

                    @E("vector")
                    def _(v, dstt=dstt, T=T):
                        v.wait_ge(cT.sem, cT.n)
                        cV.tick(v.tensor_copy(dstt[:, :, 128 * T:128 * (T + 1)],
                                              pTt[:].rearrange("p (k t) -> p k t", t=128)))
            tposed_done = cV.n

        with nc.psum_tensor("pTb%d" % i, [128, 512], BF16) as pTb:
            for T in range(NT):
                for h in range(NH):
                    @E("tensor")
                    def _(t, T=T, h=h, own=own):
                        t.wait_ge(cV.sem, cV.n)
                        for half in range(2):
                            t.matmul(pA[:, 0:128], kbnT[:, 2 * h + half, 128 * T:128 * (T + 1)],
                                     knT[:, 2 * h + half, 128 * T:128 * (T + 1)],
                                     start=(half == 0), stop=(half == 1))
                        for half in range(2):
                            ins = t.matmul(pA[:, 128:256], knT[:, 2 * h + half, 128 * T:128 * (T + 1)],
                                           kbnT[:, 2 * h + half, 128 * T:128 * (T + 1)],
                                           start=(half == 0), stop=(half == 1))
                        if own:
                            for half in range(2):
                                ins = t.matmul(pA[:, 256:384], knT[:, 2 * h + half, 128 * T:128 * (T + 1)],
                                               qnT[:, 2 * h + half, 128 * T:128 * (T + 1)],
                                               start=(half == 0), stop=(half == 1))
                        cT.tick(ins)

                    @E("vector")
                    def _(v, T=T, h=h, own=own):
                        v.wait_ge(cT.sem, cT.n)
                        v.tensor_tensor(TpP[:], pA[:, 0:128], maskA_s[:], ALU.mult)
                        ins = v.tensor_tensor(TpB5[:, 0, :], pA[:, 128:256], maskAU_s[:], ALU.mult)
                        if own:
                            ins = v.tensor_tensor(attnT[:, NH * T + h, :], pA[:, 256:384],
                                                  maskT_s[:], ALU.mult)
                        cV.tick(ins)

                    for j in range(1, 5):
                        @E("tensor")
                        def _(t, j=j):
                            t.wait_ge(cV.sem, cV.n)
                            t.wait_ge(cS.sem, cS.n)
                            cT.tick(t.matmul(pB[:, 0:128], TpB5[:, j - 1, :], TpP[:],
                                             start=True, stop=True))

                        @E("scalar")
                        def _(s):
                            s.wait_ge(cT.sem, cT.n)
                            cS.tick(s.copy(TpP[:], pB[:, 0:128]))

                        @E("vector")
                        def _(v, j=j):
                            v.wait_ge(cS.sem, cS.n)
                            cV.tick(v.transpose(TpB5[:, j, :], TpP[:]))

                    for j in range(4, -1, -1):
                        @E("tensor")
                        def _(t, T=T, h=h, j=j):
                            t.wait_ge(cV.sem, cV.n)
                            cT.tick(t.matmul(pB[:], TpB5[:, j, :], Xu[:, T, h, :], start=True, stop=True))

                        @E("vector")
                        def _(v, T=T, h=h):
                            v.wait_ge(cT.sem, cT.n)
                            cV.tick(v.tensor_tensor(Xu[:, T, h, :], Xu[:, T, h, :], pB[:], ALU.add))
            tchain_done = cV.n

            for T in range(NT):
                for grp in range(2):
                    @E("tensor")
                    def _(t, T=T, grp=grp):
                        t.wait_ge(cV.sem, cV.n)
                        for j in range(4):
                            hh = 4 * grp + j
                            h, half = hh // 2, hh % 2
                            ins = t.transpose(pTb[:, 128 * j:128 * (j + 1)],
                                              Xu[:, T, h, DV + 128 * half:DV + 128 * (half + 1)],
                                              ident[:])
                        cT.tick(ins)

                    @E("vector")
                    def _(v, T=T, grp=grp):
                        v.wait_ge(cT.sem, cT.n)
                        cV.tick(v.tensor_copy(wT_sb[:, 4 * grp:4 * (grp + 1), 128 * T:128 * (T + 1)],
                                              pTb[:, 0:512].rearrange("p (k t) -> p k t", t=128)))
            wT_done = cV.n

        with nc.psum_tensor("wSa%d" % i, [128, 512], F32) as wSa, \
             nc.psum_tensor("wSb%d" % i, [128, 512], F32) as wSb:
            for T in range(NT):
                for c in range(4):
                    for h in range(NH):
                        wbank = wSa if h % 2 == 0 else wSb
                        obank = pA if h < 2 else pB
                        ocol = 256 * (h % 2)
                        r0, r1 = 32 * c, 32 * (c + 1)

                        @E("tensor")
                        def _(t, T=T, c=c, h=h, wbank=wbank, obank=obank, ocol=ocol,
                              r0=r0, r1=r1, own=own):
                            t.wait_ge(cS.sem, max(scast[h][1], cS.n if (c == 0 and h == 0) else 0))
                            if T == 0 and c == 0 and h == 0:
                                t.wait_ge(cV.sem, wT_done)
                            if h >= 2:
                                t.wait_ge(cV.sem, cV.n)
                            for half in range(2):
                                ins = t.matmul(wbank[r0:r1, 0:256],
                                               wT_sb[:, 2 * h + half, 128 * T + r0:128 * T + r1],
                                               S_bf[:, h, 256 * half:256 * (half + 1)],
                                               start=(half == 0), stop=(half == 1),
                                               tile_position=(0, r0), skip_group_check=True)
                            if own:
                                for half in range(2):
                                    ins = t.matmul(obank[r0:r1, ocol:ocol + 256],
                                                   qnT[:, 2 * h + half, 128 * T + r0:128 * T + r1],
                                                   S_bf[:, h, 256 * half:256 * (half + 1)],
                                                   start=(half == 0 and h % 2 == 0), stop=False,
                                                   tile_position=(0, r0), skip_group_check=True)
                            cT.tick(ins)

                        @E("vector")
                        def _(v, T=T, h=h, wbank=wbank, r0=r0, r1=r1):
                            v.wait_ge(cT.sem, cT.n)
                            cV.tick(v.tensor_tensor(Xu[r0:r1, T, h, 0:DV], Xu[r0:r1, T, h, 0:DV],
                                                    wbank[r0:r1, 0:256], ALU.subtract))

                        @E("tensor")
                        def _(t, T=T, c=c, h=h, r0=r0, r1=r1):
                            t.wait_ge(cV.sem, cV.n)
                            for mc in range(2):
                                ins = t.matmul(S_ps[h][:, 256 * mc:256 * (mc + 1)],
                                               kn_tok[r0:r1, T, DV * h + 128 * mc:DV * h + 128 * (mc + 1)],
                                               Xu[r0:r1, T, h, 0:DV],
                                               start=False, stop=False,
                                               tile_position=(r0, 0), skip_group_check=True)
                            cT.tick(ins)

                        @E("scalar")
                        def _(s, h=h):
                            s.wait_ge(cT.sem, cT.n)
                            cS.tick(s.copy(S_bf[:, h, :], S_ps[h][:]))
                        scast[h] = (cS, cS.n)

                if own:
                    for h in range(NH):
                        obank = pA if h < 2 else pB
                        ocol = 256 * (h % 2)

                        @E("tensor")
                        def _(t, T=T, h=h, obank=obank, ocol=ocol):
                            t.wait_ge(cV.sem, cV.n)
                            cT.tick(t.matmul(obank[:, ocol:ocol + 256], attnT[:, NH * T + h, :],
                                             Xu[:, T, h, 0:DV], start=False, stop=(h % 2 == 1),
                                             skip_group_check=True))
                        if h % 2 == 1:
                            @E("scalar")
                            def _(s, T=T, h=h, obank=obank):
                                s.wait_ge(cT.sem, cT.n)
                                cS.tick(s.copy(TOKA[:, T, 512 * (h // 2):512 * (h // 2 + 1)], obank[:]))
            scan_done = cS.n
        if not own:
            x_free[buf] = (cS, scan_done)

        if taps:
            @E("sync")
            def _(sy, i=i, h_done=h_done, tok_done=tok_done, beta_done=beta_done,
                  norm_done=norm_done, scan_done=scan_done, own=own):
                rr = lambda d: d.rearrange("(T p) d -> p T d", p=128)
                tap_rows = slice(SEG * i, SEG * (i + 1))
                if "h" in dbg:
                    sy.wait_ge(cV.sem, h_done)
                    dma(sy, dS, rr(dbg["h"][tap_rows, :]), h_bf[:])
                if "k" in dbg:
                    sy.wait_ge(cV.sem, tok_done)
                    dma(sy, dS, rr(dbg["k"][tap_rows, :]), TOKB[:])
                if "v" in dbg:
                    sy.wait_ge(cV.sem, tok_done)
                    dma(sy, dS, rr(dbg["v"][tap_rows, :]), v_tok[:])
                if "q" in dbg and own:
                    sy.wait_ge(cV.sem, tok_done)
                    dma(sy, dS, rr(dbg["q"][tap_rows, :]), TOKA[:])
                if "beta" in dbg:
                    sy.wait_ge(cS.sem, beta_done)
                    dma(sy, dS, dbg["beta"][tap_rows, :].rearrange("(T p) h -> p T h", p=128), beta_sb[:])
                if "kn" in dbg:
                    sy.wait_ge(cV.sem, norm_done)
                    dma(sy, dS, rr(dbg["kn"][tap_rows, :]), kn_tok[:])
                if "qn" in dbg and own:
                    sy.wait_ge(cV.sem, norm_done)
                    dma(sy, dS, rr(dbg["qn"][tap_rows, :]), TOKC[:])
                if "uw" in dbg:
                    sy.wait_ge(cS.sem, scan_done)
                    dma(sy, dS, dbg["uw"][2 * i:2 * i + 2].rearrange("T p h d -> p T h d"), Xu[:])
                if "delta_o" in dbg and own:
                    sy.wait_ge(cS.sem, scan_done)
                    dma(sy, dS, rr(dbg["delta_o"][tap_rows, :]), TOKA[:])

        if not own:
            w1_ld[i + 1] = w1_ld[i]
            continue

        with nc.psum_tensor("pTg%d" % i, [128, 1024], BF16) as pTg, \
             nc.psum_tensor("pCg%d" % i, [128, 512], F32) as pCg:
            for T in range(NT):
                @E("tensor")
                def _(t, T=T, scan_done=scan_done):
                    t.wait_ge(cS.sem, scan_done)
                    t.wait_ge(cV.sem, cV.n)
                    for hh in range(8):
                        h, half = hh // 2, hh % 2
                        ins = t.transpose(pTg[:, 128 * hh:128 * (hh + 1)],
                                          TOKA[:, T, DV * h + 128 * half:DV * h + 128 * (half + 1)],
                                          ident[:])
                    cT.tick(ins)

                @E("vector")
                def _(v, T=T):
                    v.wait_ge(cT.sem, cT.n)
                    cV.tick(v.tensor_copy(qTc[:, :, 128 * T:128 * (T + 1)],
                                          pTg[:].rearrange("p (k t) -> p k t", t=128)))
            delot_done = cV.n

            @E("vector")
            def _(v, i=i, buf=buf, qkv_done=qkv_done):
                v.wait_ge(cS.sem, qkv_done)
                v.tensor_copy(vwin[:, :, 0:30], vTc[1 - buf][:, :, SEG - 30:SEG])
                cV.tick(v.tensor_copy(vwin[:, :, 30:286], vTc[buf][:]))
            vwin_done = cV.n
            vwin_guard = vwin_done

            for ntap, ftap, dstT, dstTok in [(3, firs_t, knT, TOKB), (31, firl_t, kbnT, TOKC)]:
                @E("vector")
                def _(v, ntap=ntap, ftap=ftap):
                    v.wait_ge(cV.sem, vwin_done)
                    v.wait_ge(cS.sem, cS.n)
                    for dt_ in range(8):
                        base = 30 - (ntap - 1)
                        v.tensor_scalar(CONVB[:, dt_, :], vwin[:, dt_, base:base + SEG],
                                        ftap[:, dt_, 0:1], None, ALU.mult)
                        for j in range(1, ntap):
                            ins = v.scalar_tensor_tensor(CONVB[:, dt_, :],
                                                         vwin[:, dt_, base + j:base + j + SEG],
                                                         ftap[:, dt_, j:j + 1], CONVB[:, dt_, :],
                                                         ALU.mult, ALU.add)
                    cV.tick(ins)

                @E("scalar")
                def _(s, dstT=dstT):
                    s.wait_ge(cV.sem, cV.n)
                    cS.tick(s.copy(dstT[:].rearrange("p k z -> p (k z)"),
                                   CONVB[:].rearrange("p k z -> p (k z)")))
                fcast = cS.n

                for T in range(NT):
                    @E("tensor")
                    def _(t, dstT=dstT, T=T, fcast=fcast):
                        t.wait_ge(cS.sem, fcast)
                        t.wait_ge(cV.sem, cV.n)
                        for k in range(8):
                            ins = t.transpose(pTg[:, 128 * k:128 * (k + 1)],
                                              dstT[:, k, 128 * T:128 * (T + 1)], ident[:])
                        cT.tick(ins)

                    @E("vector")
                    def _(v, dstTok=dstTok, T=T):
                        v.wait_ge(cT.sem, cT.n)
                        cV.tick(v.tensor_copy(dstTok[:, T, :],
                                              pTg[:].rearrange("p (k t) -> p k t", t=128)))
            fir_done = cV.n

            for T in range(NT):
                for h in range(NH):
                    @E("vector")
                    def _(v, T=T, h=h, fir_done=fir_done):
                        v.wait_ge(cV.sem, fir_done if (T == 0 and h == 0) else cV.n)
                        for bi, bsrc in enumerate([TOKB, TOKC, TOKA, v_tok]):
                            sl = bsrc[:, T, DV * h:DV * (h + 1)]
                            v.reduce_sum(statv[:, T, h, 3 * bi:3 * bi + 1], sl, axis=AX.X)
                            ins = v.tensor_reduce(statv[:, T, h, 3 * bi + 2:3 * bi + 3], sl,
                                                  axis=AX.X, op=ALU.max)
                        cV.tick(ins)

                    @E("scalar")
                    def _(s, T=T, h=h):
                        s.wait_ge(cV.sem, cV.n)
                        for bi, bsrc in enumerate([TOKB, TOKC, TOKA, v_tok]):
                            ins = s.activation(sqd[:, 0:DV], bsrc[:, T, DV * h:DV * (h + 1)],
                                               AF.Square, accum_out=stat_s[:, 8 + bi:9 + bi])
                        cS.tick(ins)

                    @E("vector")
                    def _(v):
                        v.wait_ge(cS.sem, cS.n)
                        cV.tick(v.tensor_scalar(stat_s[:, 12:16], stat_s[:, 8:12], 1.0 / DV, 1e-8,
                                                ALU.mult, ALU.max))

                    @E("scalar")
                    def _(s):
                        s.wait_ge(cV.sem, cV.n)
                        cS.tick(s.activation(stat_s[:, 16:20], stat_s[:, 12:16], AF.Sqrt))

                    @E("vector")
                    def _(v, T=T, h=h):
                        v.wait_ge(cS.sem, cS.n)
                        cV.tick(v.tensor_copy(statv[:, T, h, 1:11:3], stat_s[:, 16:20]))
            stats_done = cV.n

            for T in range(NT):
                for h in range(NH):
                    @E("tensor")
                    def _(t, T=T, h=h):
                        t.wait_ge(cV.sem, cV.n)
                        cT.tick(t.transpose(pCg[0:12, 0:128], statv[:, T, h, :], identf[:]))

                    @E("vector")
                    def _(v, T=T, h=h):
                        v.wait_ge(cT.sem, cT.n)
                        cV.tick(v.tensor_copy(svT[0:12, h, 128 * T:128 * (T + 1)], pCg[0:12, 0:128]))
            svt_done = cV.n

            for h in range(NH):
                for Mc in range(4):
                    @E("tensor")
                    def _(t, i=i, h=h, Mc=Mc, svt_done=svt_done):
                        t.wait_ge(cV.sem, svt_done if (h == 0 and Mc == 0) else cV.n)
                        if h == 0 and Mc == 0:
                            t.wait_ge(dGm.sem, w4_ld[i])
                            t.wait_ge(dS.sem, w1s_dma)
                        t.wait_ge(cS.sem, cS.n)
                        pg = pA if Mc % 2 == 0 else pB
                        t.matmul(pg[:, 0:256], W_ring[:, 8 + 0, 128 * Mc:128 * (Mc + 1)], hT[:, 0, :],
                                 start=True, stop=False)
                        for k in range(1, 8):
                            t.matmul(pg[:, 0:256], W_ring[:, 8 + k, 128 * Mc:128 * (Mc + 1)],
                                     hT[:, k, :], start=False, stop=False)
                        for bi, bT in enumerate([knT, kbnT, qTc, vTc[i % 2]]):
                            for half in range(2):
                                t.matmul(pg[:, 0:256], W_ring[:, 16 + 2 * bi + half, 128 * Mc:128 * (Mc + 1)],
                                         bT[:, 2 * h + half, :], start=False, stop=False)
                        t.matmul(pg[:, 0:256], W1s[0:12, 128 * Mc:128 * (Mc + 1)], svT[0:12, h, :],
                                 start=False, stop=False)
                        cT.tick(t.matmul(pg[:, 0:256], gb1_b[0:1, 128 * Mc:128 * (Mc + 1)],
                                         ones1[0:1, 0:SEG], start=False, stop=True))

                    @E("scalar")
                    def _(s, Mc=Mc):
                        s.wait_ge(cT.sem, cT.n)
                        pg = pA if Mc % 2 == 0 else pB
                        cS.tick(s.activation(hmG[:, Mc, :], pg[:, 0:256], AF.Gelu))
                gelu_done = cS.n

                for T in range(NT):
                    @E("tensor")
                    def _(t, h=h, T=T, gelu_done=gelu_done):
                        t.wait_ge(cS.sem, gelu_done)
                        t.wait_ge(cV.sem, cV.n)
                        for Mc in range(4):
                            t.matmul(pCg[:, 4 * T:4 * (T + 1)], hmG[:, Mc, 128 * T:128 * (T + 1)],
                                     gW2h[:, Mc, 4 * h:4 * (h + 1)], start=(Mc == 0), stop=False,
                                     skip_group_check=True)
                        cT.tick(t.matmul(pCg[:, 4 * T:4 * (T + 1)], ones1[0:1, 128 * T:128 * (T + 1)],
                                         gb2h[0:1, 4 * h:4 * (h + 1)], start=False, stop=True,
                                         skip_group_check=True))

                    @E("vector")
                    def _(v, T=T):
                        v.wait_ge(cT.sem, cT.n)
                        cV.tick(v.tensor_reduce(stat_s[:, 0:1], pCg[:, 4 * T:4 * (T + 1)],
                                                axis=AX.X, op=ALU.max, negate=True))

                    @E("scalar")
                    def _(s, T=T):
                        s.wait_ge(cV.sem, cV.n)
                        cS.tick(s.activation(ez_sb[:], pCg[:, 4 * T:4 * (T + 1)], AF.Exp,
                                             bias=stat_s[:, 0:1], accum_out=stat_s[:, 1:2]))

                    @E("vector")
                    def _(v, T=T, h=h):
                        v.wait_ge(cS.sem, cS.n)
                        v.reciprocal(stat_s[:, 2:3], stat_s[:, 1:2])
                        v.drain()
                        v.tensor_scalar(ez_sb[:], ez_sb[:], stat_s[:, 2:3], None, ALU.mult)
                        v.drain()
                        v.tensor_tensor(ez_sb[:], ez_sb[:], floorb[:, 4 * h:4 * (h + 1)], ALU.max)
                        v.drain()
                        v.reduce_sum(stat_s[:, 3:4], ez_sb[:], axis=AX.X)
                        v.drain()
                        v.reciprocal(stat_s[:, 4:5], stat_s[:, 3:4])
                        v.drain()
                        cV.tick(v.tensor_scalar(wgt[:, T, h, :], ez_sb[:], stat_s[:, 4:5],
                                                None, ALU.mult))
            gate_done = cV.n

            for T in range(NT):
                for h in range(NH):
                    @E("vector")
                    def _(v, T=T, h=h, gate_done=gate_done):
                        v.wait_ge(cV.sem, gate_done if (T == 0 and h == 0) else cV.n)
                        dst = SCR[:, T, DV * h:DV * (h + 1)]
                        v.tensor_scalar(dst, TOKB[:, T, DV * h:DV * (h + 1)], wgt[:, T, h, 0:1],
                                        None, ALU.mult)
                        v.scalar_tensor_tensor(dst, TOKC[:, T, DV * h:DV * (h + 1)], wgt[:, T, h, 1:2],
                                               dst, ALU.mult, ALU.add)
                        v.scalar_tensor_tensor(dst, TOKA[:, T, DV * h:DV * (h + 1)], wgt[:, T, h, 2:3],
                                               dst, ALU.mult, ALU.add)
                        cV.tick(v.scalar_tensor_tensor(dst, v_tok[:, T, DV * h:DV * (h + 1)],
                                                       wgt[:, T, h, 3:4], dst, ALU.mult, ALU.add))

                    @E("scalar")
                    def _(s, T=T, h=h):
                        s.wait_ge(cV.sem, cV.n)
                        cS.tick(s.activation(sqd[:, 0:DV], SCR[:, T, DV * h:DV * (h + 1)], AF.Square,
                                             accum_out=stat_s[:, 8:9]))

                    @E("vector")
                    def _(v):
                        v.wait_ge(cS.sem, cS.n)
                        cV.tick(v.tensor_scalar(stat_s[:, 9:10], stat_s[:, 8:9], 1.0 / DV, 1e-5,
                                                ALU.mult, ALU.add))

                    @E("scalar")
                    def _(s):
                        s.wait_ge(cV.sem, cV.n)
                        cS.tick(s.activation(stat_s[:, 10:11], stat_s[:, 9:10], AF.Sqrt))

                    @E("vector")
                    def _(v, T=T, h=h):
                        v.wait_ge(cS.sem, cS.n)
                        v.reciprocal(stat_s[:, 11:12], stat_s[:, 10:11])
                        v.drain()
                        cV.tick(v.scalar_tensor_tensor(h_bf[:, T, DV * h:DV * (h + 1)],
                                                       SCR[:, T, DV * h:DV * (h + 1)],
                                                       stat_s[:, 11:12], onormb[:],
                                                       ALU.mult, ALU.mult))
            mix_done = cV.n

            for T in range(NT):
                @E("tensor")
                def _(t, T=T, mix_done=mix_done):
                    t.wait_ge(cV.sem, mix_done if T == 0 else cV.n)
                    for k in range(8):
                        ins = t.transpose(pTg[:, 128 * k:128 * (k + 1)],
                                          h_bf[:, T, 128 * k:128 * (k + 1)], ident[:])
                    cT.tick(ins)

                @E("vector")
                def _(v, T=T):
                    v.wait_ge(cT.sem, cT.n)
                    cV.tick(v.tensor_copy(kTc[:, :, 128 * T:128 * (T + 1)],
                                          pTg[:].rearrange("p (k t) -> p k t", t=128)))
            oT_done = cV.n

            for T in range(NT):
                for n in range(2):
                    @E("tensor")
                    def _(t, T=T, n=n, oT_done=oT_done):
                        t.wait_ge(cV.sem, oT_done if (T == 0 and n == 0) else cV.n)
                        pg = pA if n == 0 else pB
                        for k in range(8):
                            ins = t.matmul(pg[:], kTc[:, k, 128 * T:128 * (T + 1)],
                                           W_ring[:, 0 + k, 512 * n:512 * (n + 1)],
                                           start=(k == 0), stop=(k == 7))
                        cT.tick(ins)

                    @E("vector")
                    def _(v, T=T, n=n, i=i):
                        v.wait_ge(cT.sem, cT.n)
                        pg = pA if n == 0 else pB
                        dst = SCR[:, T, 512 * n:512 * (n + 1)]
                        v.tensor_tensor(dst, pg[:], ga_b[:, 512 * n:512 * (n + 1)], ALU.mult)
                        cV.tick(v.tensor_tensor(dst, dst, x_sb[i % 2][:, T, 512 * n:512 * (n + 1)],
                                                ALU.add))
            xmid_done = cV.n
            x_free[buf] = (cV, xmid_done)

            for T in range(NT):
                @E("vector")
                def _(v, T=T, xmid_done=xmid_done):
                    v.wait_ge(cV.sem, xmid_done)
                    cV.tick(v.reduce_sum(stat_s[:, 0:1], SCR[:, T, :], axis=AX.X))

                @E("scalar")
                def _(s, T=T):
                    s.wait_ge(cV.sem, cV.n)
                    cS.tick(s.activation(sqd[:], SCR[:, T, :], AF.Square, accum_out=stat_s[:, 1:2]))

                @E("vector")
                def _(v):
                    v.wait_ge(cS.sem, cS.n)
                    v.tensor_scalar(stat_s[:, 2:3], stat_s[:, 0:1], 1.0 / HID, None, ALU.mult)
                    v.tensor_tensor(stat_s[:, 3:4], stat_s[:, 2:3], stat_s[:, 2:3], ALU.mult)
                    v.tensor_scalar(stat_s[:, 4:5], stat_s[:, 1:2], 1.0 / HID, 1e-5, ALU.mult, ALU.add)
                    cV.tick(v.tensor_tensor(stat_s[:, 4:5], stat_s[:, 4:5], stat_s[:, 3:4], ALU.subtract))

                @E("scalar")
                def _(s):
                    s.wait_ge(cV.sem, cV.n)
                    cS.tick(s.activation(stat_s[:, 5:6], stat_s[:, 4:5], AF.Sqrt))

                @E("vector")
                def _(v, T=T):
                    v.wait_ge(cS.sem, cS.n)
                    v.reciprocal(stat_s[:, 6:7], stat_s[:, 5:6])
                    v.drain()
                    v.tensor_scalar(sqd[:], SCR[:, T, :], stat_s[:, 2:3], stat_s[:, 6:7],
                                    ALU.subtract, ALU.mult)
                    v.tensor_tensor(sqd[:], sqd[:], cb1_b[:], ALU.mult)
                    cV.tick(v.tensor_tensor(h2_bf[:, T, :], sqd[:], sbv_b[:], ALU.add))
            h2_done = cV.n

            @E("sync")
            def _(sy, orow=orow, h2_done=h2_done):
                sy.wait_ge(cV.sem, h2_done)
                dma(sy, dS, d_xmid[orow:orow + SEG, :].rearrange("(T p) d -> p T d", p=128), SCR[:])
                dma(sy, dS, d_h2[orow:orow + SEG, :].rearrange("(T p) d -> p T d", p=128), h2_bf[:])

            if taps:
                @E("sync")
                def _(sy, orow=orow, i=i, fir_done=fir_done, stats_done=stats_done,
                      gate_done=gate_done, mix_done=mix_done, xmid_done=xmid_done, h2_done=h2_done):
                    rr = lambda d: d.rearrange("(T p) d -> p T d", p=128)
                    if "fir_s" in dbg:
                        sy.wait_ge(cV.sem, fir_done)
                        dma(sy, dS, rr(dbg["fir_s"][orow:orow + SEG, :]), TOKB[:])
                    if "fir_l" in dbg:
                        sy.wait_ge(cV.sem, fir_done)
                        dma(sy, dS, rr(dbg["fir_l"][orow:orow + SEG, :]), TOKC[:])
                    if "stat" in dbg:
                        sy.wait_ge(cV.sem, stats_done)
                        dma(sy, dS, dbg["stat"][2 * (i - own0):2 * (i - own0) + 2]
                            .rearrange("T p h d -> p T h d"), statv[:])
                    if "wgt" in dbg:
                        sy.wait_ge(cV.sem, gate_done)
                        dma(sy, dS, dbg["wgt"][2 * (i - own0):2 * (i - own0) + 2]
                            .rearrange("T p h e -> p T h e"), wgt[:])
                    if "o_mix" in dbg:
                        sy.wait_ge(cV.sem, mix_done)
                        dma(sy, dS, rr(dbg["o_mix"][orow:orow + SEG, :]), h_bf[:])
                    if "x_mid" in dbg:
                        sy.wait_ge(cV.sem, xmid_done)
                        dma(sy, dS, rr(dbg["x_mid"][orow:orow + SEG, :]), SCR[:])
                    if "h2" in dbg:
                        sy.wait_ge(cV.sem, h2_done)
                        dma(sy, dS, rr(dbg["h2"][orow:orow + SEG, :]), h2_bf[:])

            if i + 1 < nseg:
                @E("gpsimd")
                def _(g):
                    g.wait_ge(cT.sem, cT.n)
                    dma(g, dGm, W_ring[:, 0:8, :], KT(dWq))
                    dma(g, dGm, W_ring[:, 8:16, :], KT(dWk))
                    dma(g, dGm, W_ring[:, 16:24, :], KT(dWv))
                w1_ld[i + 1] = dGm.n

    paT, paV, paS, paG, paD, paGm = cT.n, cV.n, cS.n, cG.n, dS.n, dGm.n
    pa.close()

    pb = contextlib.ExitStack()

    def sbb(name, shape, dt=F32):
        return pb.enter_context(nc.sbuf_tensor(name, shape, dt))

    h2T = sbb("h2T", [128, 8, PB_TOK], BF16)
    acc = sbb("acc", [128, PB_TOK // 128, HID])
    wb1 = [sbb("wb1_0", [128, 8, EI], BF16), sbb("wb1_1", [128, 8, EI], BF16)]
    wb3 = [sbb("wb3_0", [128, 8, EI], BF16), sbb("wb3_1", [128, 8, EI], BF16)]
    wb2 = [sbb("wb2_0", [128, 6, HID], BF16), sbb("wb2_1", [128, 6, HID], BF16)]
    SCB1 = sbb("SCB1", [128, PB_TOK // 128, NE])
    SCB2 = sbb("SCB2", [128, PB_TOK // 128, NE])
    mrscr = sbb("mrscr", [128, NE])
    mx8 = sbb("mx8", [128, 8])
    aT = sbb("aT", [128, 6, PB_CH], BF16)
    gtmp = sbb("gtmp", [128, PB_CH], BF16)
    yscr = sbb("yscr", [128, 512])
    xmb = sbb("xmb", [128, 2, HID])
    obuf = sbb("obuf", [128, 2, HID])

    pG = pb.enter_context(nc.psum_tensor("pG", [128, 512], F32))
    pU = pb.enter_context(nc.psum_tensor("pU", [128, 512], F32))
    pO = pb.enter_context(nc.psum_tensor("pO", [128, 512], F32))
    pO2 = pb.enter_context(nc.psum_tensor("pO2", [128, 512], F32))

    NPT = PB_TOK // 128
    barrier = dict(done=False)
    exp_seq = []

    for p in range(npass):
        pbase = p * PB_TOK

        @E("scalar")
        def _(s, p=p):
            if p == 0:
                s.wait_ge(cT.sem, paT)
                s.wait_ge(cV.sem, paV)
                s.wait_ge(cG.sem, paG)
                s.wait_ge(dS.sem, paD)
                s.wait_ge(dGm.sem, paGm)
            else:
                s.wait_ge(cV.sem, cV.n)
            for k in range(8):
                dma(s, dA, h2T[:, k, :], d_h2[pbase:pbase + PB_TOK, 128 * k:128 * (k + 1)],
                    transpose=True)
        h2t_ld = dA.n

        @E("tensor")
        def _(t, p=p):
            t.wait_ge(dA.sem, h2t_ld)
            if p == 0:
                t.wait_ge(cV.sem, paV)
                t.wait_ge(cS.sem, paS)
            for tt in range(NPT):
                for k in range(8):
                    ins = t.matmul(pO[:, NE * tt:NE * (tt + 1)], h2T[:, k, 128 * tt:128 * (tt + 1)],
                                   gatewT[:, k, :], start=(tt == 0 and k == 0),
                                   stop=(tt == NPT - 1 and k == 7))
            cT.tick(ins)

        @E("scalar")
        def _(s):
            s.wait_ge(cT.sem, cT.n)
            cS.tick(s.activation(SCB1[:].rearrange("p T e -> p (T e)"), pO[:, 0:NE * NPT], AF.Sigmoid))

        @E("vector")
        def _(v, p=p):
            v.wait_ge(cS.sem, cS.n)
            if p == 0:
                v.wait_ge(cG.sem, paG)
                v.wait_ge(dGm.sem, paGm)
            v.memset(mx8[:, 1:8], 3.0e38)
            for tt in range(NPT):
                v.tensor_tensor(SCB2[:, tt, :], SCB1[:, tt, :], ebias_b[:], ALU.add)
                v.drain()
                for r in range(4):
                    src = SCB2[:, tt, :] if r % 2 == 0 else mrscr[:]
                    dstm = mrscr[:] if r % 2 == 0 else SCB2[:, tt, :]
                    v.reduce_max(mx8[:, 0:1], src, axis=AX.X)
                    v.drain()
                    v.match_replace(dstm, mx8[:], src, -1e30)
                    v.drain()
                v.tensor_scalar(mrscr[:], SCB2[:, tt, :], -1e29, None, ALU.is_le)
                v.drain()
                v.tensor_tensor(mrscr[:], mrscr[:], SCB1[:, tt, :], ALU.mult)
                v.drain()
                v.reduce_sum(mx8[:, 0:1], mrscr[:], axis=AX.X)
                v.drain()
                v.reciprocal(mx8[:, 1:2], mx8[:, 0:1])
                v.drain()
                v.tensor_scalar(SCB1[:, tt, :], mrscr[:], mx8[:, 1:2], None, ALU.mult)
                ins = v.memset(mx8[:, 1:8], 3.0e38)
            cV.tick(ins)
        comb_done = cV.n

        if "combine" in dbg:
            @E("sync")
            def _(sy, pbase=pbase, comb_done=comb_done):
                sy.wait_ge(cV.sem, comb_done)
                dma(sy, dS, dbg["combine"][pbase:pbase + PB_TOK, :]
                    .rearrange("(T p) e -> p T e", p=128), SCB1[:])

        @E("vector")
        def _(v, comb_done=comb_done):
            v.wait_ge(cV.sem, comb_done)
            cV.tick(v.memset(acc[:], 0.0))
        acc_zero = cV.n

        for e in range(nexp):
            ebuf = e % 2

            @E("gpsimd")
            def _(g, p=p, e=e, ebuf=ebuf):
                if p == 0 and e == 0:
                    g.wait_ge(cV.sem, paV)
                    g.wait_ge(cS.sem, paS)
                    g.wait_ge(cT.sem, paT)
                if len(exp_seq) >= 2:
                    g.wait_ge(cT.sem, exp_seq[-2])
                dma(g, dGm, wb1[ebuf][:], KT(dw1[e]))
                dma(g, dGm, wb3[ebuf][:], KT(dw3[e]))
                dma(g, dGm, wb2[ebuf][:], KT(dw2[e]))
            wexp = dGm.n

            for ch in range(PB_TOK // PB_CH):
                cbase = PB_CH * ch
                for ic in range(6):
                    @E("tensor")
                    def _(t, e=e, ebuf=ebuf, ic=ic, cbase=cbase, wexp=wexp, acc_zero=acc_zero):
                        t.wait_ge(dGm.sem, wexp)
                        t.wait_ge(cV.sem, max(acc_zero, cV.n if (e or ch or ic) else 0))
                        for k in range(8):
                            t.matmul(pG[:], wb1[ebuf][:, k, 128 * ic:128 * (ic + 1)],
                                     h2T[:, k, cbase:cbase + PB_CH], start=(k == 0), stop=(k == 7))
                        for k in range(8):
                            ins = t.matmul(pU[:], wb3[ebuf][:, k, 128 * ic:128 * (ic + 1)],
                                           h2T[:, k, cbase:cbase + PB_CH], start=(k == 0), stop=(k == 7))
                        cT.tick(ins)

                    @E("scalar")
                    def _(s):
                        s.wait_ge(cT.sem, cT.n)
                        cS.tick(s.activation(gtmp[:], pG[:], AF.Silu))

                    @E("vector")
                    def _(v, ic=ic):
                        v.wait_ge(cS.sem, cS.n)
                        cV.tick(v.tensor_tensor(aT[:, ic, :], gtmp[:], pU[:], ALU.mult))
                a_done = cV.n

                for Mt in range(PB_CH // 128):
                    tt = (cbase // 128) + Mt
                    for n in range(2):
                        po = pO if (Mt % 2 == 0) else pO2

                        @E("tensor")
                        def _(t, ebuf=ebuf, Mt=Mt, n=n, po=po, a_done=a_done):
                            t.wait_ge(cV.sem, a_done)
                            t.wait_ge(cV.sem, cV.n)
                            for kc in range(6):
                                ins = t.matmul(po[:], aT[:, kc, 128 * Mt:128 * (Mt + 1)],
                                               wb2[ebuf][:, kc, 512 * n:512 * (n + 1)],
                                               start=(kc == 0), stop=(kc == 5))
                            cT.tick(ins)

                        @E("scalar")
                        def _(s, e=e, tt=tt, po=po):
                            s.wait_ge(cT.sem, cT.n)
                            cS.tick(s.activation(yscr[:], po[:], AF.Copy, scale=SCB1[:, tt, e:e + 1]))

                        @E("vector")
                        def _(v, tt=tt, n=n):
                            v.wait_ge(cS.sem, cS.n)
                            dst = acc[:, tt, 512 * n:512 * (n + 1)]
                            cV.tick(v.tensor_tensor(dst, dst, yscr[:], ALU.add))
            exp_seq.append(cT.n)
        moe_done = cV.n

        for g2 in range(NPT // 2):
            @E("sync")
            def _(sy, pbase=pbase, g2=g2, moe_done=moe_done):
                sy.wait_ge(cV.sem, moe_done if g2 == 0 else cV.n)
                r0 = pbase + 256 * g2
                dma(sy, dS, xmb[:], d_xmid[r0:r0 + 256, :].rearrange("(T p) d -> p T d", p=128))
            xm_ld = dS.n

            @E("vector")
            def _(v, g2=g2, xm_ld=xm_ld):
                v.wait_ge(dS.sem, xm_ld)
                for T in range(2):
                    tt = 2 * g2 + T
                    v.tensor_tensor(obuf[:, T, :], acc[:, tt, :], gb_b[:], ALU.mult)
                    ins = v.tensor_tensor(obuf[:, T, :], obuf[:, T, :], xmb[:, T, :], ALU.add)
                cV.tick(ins)

            @E("sync")
            def _(sy, pbase=pbase, g2=g2):
                sy.wait_ge(cV.sem, cV.n)
                r0 = pbase + 256 * g2
                dma(sy, dS, dout[r0:r0 + 256, :].rearrange("(T p) d -> p T d", p=128), obuf[:])

    @E("sync")
    def _(sy):
        sy.wait_ge(dS.sem, dS.n)

    pb.close()
    es.close()
    return nc


def _host_constants():
    import ml_dtypes
    ident = np.eye(128, dtype=ml_dtypes.bfloat16)
    identf = np.eye(128, dtype=np.float32)
    mA = np.zeros((128, 128), np.float32)
    mAU = np.zeros((128, 128), np.float32)
    mT = np.zeros((128, 128), np.float32)
    for c in range(4):
        s = slice(32 * c, 32 * (c + 1))
        tril = np.tril(np.ones((32, 32), np.float32), -1)
        mA[s, s] = -tril
        mAU[s, s] = -tril.T
        mT[s, s] = np.triu(np.ones((32, 32), np.float32))
    return ident, identf, mA, mAU, mT


def make_in_maps(inp, taps=()):
    ident, identf, mA, mAU, mT = _host_constants()
    f32 = lambda a: np.ascontiguousarray(np.asarray(a, np.float32))
    conv_taps = np.stack([f32(inp["L0_attn_qc_w"]), f32(inp["L0_attn_kc_w"]),
                          f32(inp["L0_attn_vc_w"])])
    convt = np.ascontiguousarray(conv_taps.reshape(3, 8, 128, 4).transpose(2, 0, 1, 3))
    firs = np.ascontiguousarray(f32(inp["L0_attn_fir_s"]).reshape(1024, 3)
                                .reshape(8, 128, 3).transpose(1, 0, 2))
    firl = np.ascontiguousarray(f32(inp["L0_attn_fir_l"]).reshape(1024, 31)
                                .reshape(8, 128, 31).transpose(1, 0, 2))
    shared = {
        "Wq": f32(inp["L0_attn_Wq"]), "Wk": f32(inp["L0_attn_Wk"]),
        "Wv": f32(inp["L0_attn_Wv"]), "Wb": f32(inp["L0_attn_Wb"]),
        "Wo": f32(inp["L0_attn_Wo"]),
        "convtaps": convt, "firs": firs, "firl": firl,
        "gW1": f32(inp["L0_attn_gW1"]), "gb1": f32(inp["L0_attn_gb1"]).reshape(1, 512),
        "gW2": f32(inp["L0_attn_gW2"]), "gb2": f32(inp["L0_attn_gb2"]).reshape(1, 4),
        "temp": f32(inp["L0_attn_temp"]).reshape(1, 4),
        "floorv": f32(inp["L0_attn_eps_floor"]).reshape(1, 16),
        "onorm": f32(inp["L0_attn_onorm_w"]).reshape(1, DV),
        "wmod": f32(inp["L0_wmod"]),
        "gate_wT": np.ascontiguousarray(f32(inp["L0_ffn_gate_w"]).T),
        "e_bias": f32(inp["L0_ffn_e_bias"]).reshape(1, NE),
        "w1": f32(inp["L0_ffn_w1"]), "w3": f32(inp["L0_ffn_w3"]), "w2": f32(inp["L0_ffn_w2"]),
        "identity": ident, "identityf": identf, "maskA": mA, "maskAU": mAU, "maskT": mT,
    }
    x = f32(inp["L0_attn_x"])
    cfull = f32(inp["L0_c"])
    in_maps = []
    for core in range(8):
        b, s = core // 2, core % 2
        xb = x[b] if s == 1 else np.ascontiguousarray(np.roll(x[b], HALF, axis=0))
        mask = np.ones((L,), np.float32)
        if s == 0:
            mask[:HALF] = 0.0
        m = dict(shared)
        m["x"] = xb
        m["tokmask"] = np.ascontiguousarray(mask.reshape(L // 128, 128).T)
        m["c_bT"] = np.ascontiguousarray(cfull[b].reshape(8, 128).T)
        in_maps.append(m)
    return in_maps


def kernel(**inp):
    nc = build_program()
    in_maps = make_in_maps(inp)
    res = run_bass_kernel_spmd(nc, in_maps, core_ids=list(range(8)))
    out = np.empty((4, L, HID), np.float32)
    for core in range(8):
        b, s = core // 2, core % 2
        out[b, HALF * s:HALF * (s + 1)] = res.results[core]["out"]
    return out


# version 02
# speedup vs baseline: 1.3854x; 1.3854x over previous
import numpy as np
import contextlib
import concourse.bass as bass
import concourse.mybir as mybir
from concourse.bass_utils import run_bass_kernel_spmd

F32 = mybir.dt.float32
BF16 = mybir.dt.bfloat16
AF = mybir.ActivationFunctionType
ALU = mybir.AluOpType
AX = mybir.AxisListType

HID, NH, DK, DV = 1024, 4, 256, 256
L, HALF = 4096, 2048
SEG = 256
NSEG = L // SEG
NT = SEG // 128
NE, EI = 32, 768
TOK = HALF
PB_TOK = 1024
PB_CH = 512


class Ctr:
    def __init__(self, sem):
        self.sem = sem
        self.n = 0

    def tick(self, instr, k=1):
        instr.then_inc(self.sem, k)
        self.n += k
        return self.n


def build_program(nseg=NSEG, nexp=NE, taps=(), own0=NSEG // 2, npass=2):
    nc = bass.Bass("TRN2", target_bir_lowering=False, detect_race_conditions=False)

    dx = nc.dram_tensor("x", [L, HID], F32, kind="ExternalInput").ap()
    dmask = nc.dram_tensor("tokmask", [128, L // 128], F32, kind="ExternalInput").ap()
    dWq = nc.dram_tensor("Wq", [HID, HID], F32, kind="ExternalInput").ap()
    dWk = nc.dram_tensor("Wk", [HID, HID], F32, kind="ExternalInput").ap()
    dWv = nc.dram_tensor("Wv", [HID, HID], F32, kind="ExternalInput").ap()
    dWb = nc.dram_tensor("Wb", [HID, NH], F32, kind="ExternalInput").ap()
    dWo = nc.dram_tensor("Wo", [HID, HID], F32, kind="ExternalInput").ap()
    dconv = nc.dram_tensor("convtaps", [128, 3, 8, 4], F32, kind="ExternalInput").ap()
    dfirs = nc.dram_tensor("firs", [128, 8, 3], F32, kind="ExternalInput").ap()
    dfirl = nc.dram_tensor("firl", [128, 8, 31], F32, kind="ExternalInput").ap()
    dgW1 = nc.dram_tensor("gW1", [5120, 512], F32, kind="ExternalInput").ap()
    dgb1 = nc.dram_tensor("gb1", [1, 512], F32, kind="ExternalInput").ap()
    dgW2 = nc.dram_tensor("gW2", [512, 4], F32, kind="ExternalInput").ap()
    dgb2 = nc.dram_tensor("gb2", [1, 4], F32, kind="ExternalInput").ap()
    dtemp = nc.dram_tensor("temp", [1, 4], F32, kind="ExternalInput").ap()
    dfloor = nc.dram_tensor("floorv", [1, 16], F32, kind="ExternalInput").ap()
    donorm = nc.dram_tensor("onorm", [1, DV], F32, kind="ExternalInput").ap()
    dcT = nc.dram_tensor("c_bT", [128, 8], F32, kind="ExternalInput").ap()
    dwmod = nc.dram_tensor("wmod", [HID, 6 * HID], F32, kind="ExternalInput").ap()
    dgatew = nc.dram_tensor("gate_wT", [HID, NE], F32, kind="ExternalInput").ap()
    debias = nc.dram_tensor("e_bias", [1, NE], F32, kind="ExternalInput").ap()
    dw1 = nc.dram_tensor("w1", [NE, 128, 8 * EI], F32, kind="ExternalInput").ap()
    dw3 = nc.dram_tensor("w3", [NE, 128, 8 * EI], F32, kind="ExternalInput").ap()
    dw2 = nc.dram_tensor("w2", [NE, 128, 6 * HID], F32, kind="ExternalInput").ap()
    dident = nc.dram_tensor("identity", [128, 128], BF16, kind="ExternalInput").ap()
    didentf = nc.dram_tensor("identityf", [128, 128], F32, kind="ExternalInput").ap()
    dmaskA = nc.dram_tensor("maskA", [128, 128], F32, kind="ExternalInput").ap()
    dmaskAU = nc.dram_tensor("maskAU", [128, 128], F32, kind="ExternalInput").ap()
    dmaskT = nc.dram_tensor("maskT", [128, 128], F32, kind="ExternalInput").ap()
    dout = nc.dram_tensor("out", [TOK, HID], F32, kind="ExternalOutput").ap()
    d_xmid = nc.dram_tensor("xmid_scr", [TOK, HID], F32, kind="Internal").ap()
    d_h2 = nc.dram_tensor("h2_scr", [TOK, HID], BF16, kind="Internal").ap()

    TAPSPEC = {
        "h": ([L, HID], BF16), "q": ([L, HID], BF16), "k": ([L, HID], BF16),
        "v": ([L, HID], BF16), "beta": ([L, NH], F32),
        "qn": ([L, HID], BF16), "kn": ([L, HID], BF16),
        "uw": ([L // 128, 128, NH, 512], BF16), "delta_o": ([L, HID], BF16),
        "fir_s": ([TOK, HID], BF16), "fir_l": ([TOK, HID], BF16),
        "stat": ([TOK // 128, 128, NH, 12], F32), "wgt": ([TOK // 128, 128, NH, 4], F32),
        "o_mix": ([TOK, HID], BF16), "x_mid": ([TOK, HID], F32), "h2": ([TOK, HID], BF16),
        "combine": ([TOK, NE], F32),
        "setup": ([8, 128, HID], F32),
    }
    dbg = {}
    for t in taps:
        shp, dt = TAPSPEC[t]
        dbg[t] = nc.dram_tensor("dbg_" + t, shp, dt, kind="ExternalOutput").ap()

    es = contextlib.ExitStack()

    def sbt(name, shape, dt=F32):
        return es.enter_context(nc.sbuf_tensor(name, shape, dt))

    def mksem(name):
        return Ctr(es.enter_context(nc.semaphore(name)))

    cT = mksem("cT")
    cV = mksem("cV")
    cS = mksem("cS")
    cG = mksem("cG")
    dS = mksem("dS")
    dGm = mksem("dGm")
    dA = mksem("dA")

    ident = sbt("ident", [128, 128], BF16)
    identf = sbt("identf", [128, 128], F32)
    maskA_s = sbt("maskA_s", [128, 128])
    maskAU_s = sbt("maskAU_s", [128, 128])
    maskT_s = sbt("maskT_s", [128, 128])
    convt = sbt("convt", [128, 3, 8, 4])
    firs_t = sbt("firs_t", [128, 8, 3])
    firl_t = sbt("firl_t", [128, 8, 31])
    mask_sb = sbt("mask_sb", [128, L // 128])
    Wb_s = sbt("Wb_s", [128, 8, NH], BF16)
    W1s = sbt("W1s", [12, 512], BF16)
    gW2h = sbt("gW2h", [128, 4, 4 * NH], BF16)
    gb2h = sbt("gb2h", [1, 4 * NH], BF16)
    gb1_b = sbt("gb1_b", [1, 512], BF16)
    floorb = sbt("floorb", [128, NH * 4])
    onormb = sbt("onormb", [128, DV])
    ebias_b = sbt("ebias_b", [128, NE])
    gatewT = sbt("gatewT", [128, 8, NE], BF16)
    invt_c = sbt("invt_c", [128, 4])
    ones1 = sbt("ones1", [1, 512], BF16)
    onec = sbt("onec", [128, 1], BF16)
    zrow = sbt("zrow", [1, 512], BF16)
    sa_b = sbt("sa_b", [128, HID], BF16)
    ca1_b = sbt("ca1_b", [128, HID], BF16)
    ga_b = sbt("ga_b", [128, HID], BF16)
    sbv_b = sbt("sbv_b", [128, HID], BF16)
    cb1_b = sbt("cb1_b", [128, HID], BF16)
    gb_b = sbt("gb_b", [128, HID], BF16)

    block = es.enter_context(nc.Block())

    def E(name):
        def deco(f):
            getattr(block, name)(f)
        return deco

    def dma(eng, ctr, out, in_, **kw):
        ctr.tick(eng.dma_start(out=out, in_=in_, **kw), 16)

    KT = lambda d: d.rearrange("(k p) e -> p k e", p=128)

    sc = contextlib.ExitStack()

    def sbl(name, shape, dt=F32):
        return sc.enter_context(nc.sbuf_tensor(name, shape, dt))

    modrow = sbl("modrow", [1, 6 * HID])
    modrow_b = sbl("modrow_b", [1, 6 * HID], BF16)
    c_sb = sbl("c_sb", [128, 8])
    silc = sbl("silc", [128, 8], BF16)
    wmbuf = sbl("wmbuf", [128, 2, 8, 512], BF16)
    g1stg = sbl("g1stg", [128, 2, 512], BF16)
    gW2f = sbl("gW2f", [128, 4, 4])
    rows_s = sbl("rows_s", [1, 64])
    rows_b = sbl("rows_b", [1, 64], BF16)
    onr = sbl("onr", [1, DV])
    onr_b = sbl("onr_b", [1, DV], BF16)
    ebr = sbl("ebr", [1, NE])
    ebr_b = sbl("ebr_b", [1, NE], BF16)
    clampt = sbl("clampt", [128, HID])
    clampt2 = sbl("clampt2", [128, HID])
    w1row = sbl("w1row", [1, 512], BF16)
    pSu = sc.enter_context(nc.psum_tensor("pSu", [128, 512], F32))
    pRu = sc.enter_context(nc.psum_tensor("pRu", [1, 512], F32))

    @E("sync")
    def _(sy):
        for o, i in [(ident[:], dident), (identf[:], didentf), (maskA_s[:], dmaskA),
                     (maskAU_s[:], dmaskAU), (maskT_s[:], dmaskT), (convt[:], dconv),
                     (firs_t[:], dfirs), (firl_t[:], dfirl), (mask_sb[:], dmask),
                     (c_sb[:], dcT), (gW2f[:], KT(dgW2)),
                     (rows_s[0:1, 0:4], dgb2), (rows_s[0:1, 4:8], dtemp),
                     (rows_s[0:1, 16:32], dfloor), (onr[:], donorm), (ebr[:], debias)]:
            dma(sy, dS, o, i)
    ld_smalls = dS.n

    @E("gpsimd")
    def _(g):
        dma(g, dGm, Wb_s[:], KT(dWb))
        dma(g, dGm, gatewT[:], KT(dgatew))
        dma(g, dGm, gb1_b[:], dgb1)
    n_pre_g = 3

    @E("vector")
    def _(v):
        v.memset(ones1[:], 1.0)
        v.memset(onec[:], 1.0)
        cV.tick(v.memset(zrow[:], 0.0))
    ones_done = cV.n

    @E("scalar")
    def _(s):
        s.wait_ge(dS.sem, ld_smalls)
        cS.tick(s.activation(silc[:], c_sb[:], AF.Silu))
    silc_done = cS.n

    mod_evac = {}
    for n in range(12):
        @E("gpsimd")
        def _(g, n=n):
            if n >= 2:
                g.wait_ge(cS.sem, mod_evac[n - 2])
            dma(g, dGm, wmbuf[:, n % 2], KT(dwmod[:, 512 * n:512 * (n + 1)]))

        @E("tensor")
        def _(t, n=n):
            t.wait_ge(dGm.sem, dGm.n)
            if n == 0:
                t.wait_ge(cS.sem, silc_done)
            else:
                t.wait_ge(cS.sem, mod_evac[n - 1])
            for k in range(8):
                ins = t.matmul(pRu[:], silc[:, k:k + 1], wmbuf[:, n % 2, k, :],
                               start=(k == 0), stop=(k == 7))
            cT.tick(ins)

        @E("scalar")
        def _(s, n=n):
            s.wait_ge(cT.sem, cT.n)
            cS.tick(s.copy(modrow[0:1, 512 * n:512 * (n + 1)], pRu[:]))
        mod_evac[n] = cS.n

    @E("vector")
    def _(v):
        v.wait_ge(cS.sem, mod_evac[11])
        cV.tick(v.tensor_copy(modrow_b[:], modrow[:]))
    modb_done = cV.n

    vec_tiles = [sa_b, ca1_b, ga_b, sbv_b, cb1_b, gb_b]
    for vi in range(6):
        for half in range(2):
            @E("tensor")
            def _(t, vi=vi, half=half):
                t.wait_ge(cV.sem, modb_done if (vi == 0 and half == 0) else cV.n)
                cT.tick(t.matmul(pSu[:], ones1[0:1, 0:128],
                                 modrow_b[0:1, HID * vi + 512 * half: HID * vi + 512 * (half + 1)],
                                 start=True, stop=True))

            @E("vector")
            def _(v, vi=vi, half=half):
                v.wait_ge(cT.sem, cT.n)
                dst = vec_tiles[vi][:, 512 * half:512 * (half + 1)]
                if vi in (1, 4):
                    cV.tick(v.tensor_scalar(dst, pSu[:], 1.0, None, ALU.add))
                else:
                    cV.tick(v.tensor_copy(dst, pSu[:]))

    @E("vector")
    def _(v):
        v.wait_ge(dS.sem, ld_smalls)
        cV.tick(v.tensor_copy(rows_b[0:1, 32:33], rows_s[0:1, 16:17]))

    for gt in (ga_b, gb_b):
        @E("scalar")
        def _(s, gt=gt):
            s.wait_ge(cV.sem, cV.n)
            cS.tick(s.activation(clampt2[:], gt[:], AF.Abs))

        @E("vector")
        def _(v, gt=gt):
            v.wait_ge(cS.sem, cS.n)
            v.tensor_scalar(clampt[:], gt[:], 0.0, None, ALU.is_ge)
            v.tensor_scalar(clampt[:], clampt[:], 2.0, -1.0, ALU.mult, ALU.add)
            v.tensor_scalar(clampt2[:], clampt2[:], 0.25, None, ALU.max)
            cV.tick(v.tensor_tensor(gt[:], clampt[:], clampt2[:], ALU.mult))

    @E("vector")
    def _(v):
        v.wait_ge(cV.sem, cV.n)
        v.tensor_scalar(rows_s[0:1, 8:12], rows_s[0:1, 4:8], 0.2, 10.0, ALU.max, ALU.min)
        v.drain()
        v.reciprocal(rows_s[0:1, 12:16], rows_s[0:1, 8:12])
        v.drain()
        v.tensor_scalar(rows_s[0:1, 16:32], rows_s[0:1, 16:32], 1e-7, 0.1, ALU.max, ALU.min)
        v.drain()
        v.tensor_copy(rows_b[0:1, 0:32], rows_s[0:1, 0:32])
        v.tensor_copy(onr_b[:], onr[:])
        cV.tick(v.tensor_copy(ebr_b[:], ebr[:]))

    for src, dst, wid in [(rows_b[0:1, 12:16], invt_c[:], 4),
                          (rows_b[0:1, 16:32], floorb[:], 16),
                          (ebr_b[0:1, :], ebias_b[:], NE),
                          (onr_b[0:1, :], onormb[:], DV)]:
        @E("tensor")
        def _(t, src=src, wid=wid):
            t.wait_ge(cV.sem, cV.n)
            cT.tick(t.matmul(pSu[:, 0:wid], ones1[0:1, 0:128], src, start=True, stop=True))

        @E("vector")
        def _(v, dst=dst, wid=wid):
            v.wait_ge(cT.sem, cT.n)
            cV.tick(v.tensor_copy(dst, pSu[:, 0:wid]))

    @E("vector")
    def _(v):
        v.wait_ge(cV.sem, cV.n)
        for h in range(NH):
            v.tensor_scalar(gW2h[:, :, 4 * h:4 * (h + 1)], gW2f[:], invt_c[:, h:h + 1], None, ALU.mult)
            ins = v.tensor_scalar(gb2h[0:1, 4 * h:4 * (h + 1)], rows_s[0:1, 0:4],
                                  rows_s[0:1, 12 + h:13 + h], None, ALU.mult)
        cV.tick(ins)

    w1s_evac = {}
    for j in range(24):
        @E("gpsimd")
        def _(g, j=j):
            if j >= 2:
                g.wait_ge(cS.sem, w1s_evac[j - 2])
            dma(g, dGm, g1stg[:, j % 2, :], dgW1[HID + 128 * j: HID + 128 * (j + 1), :])

        @E("tensor")
        def _(t, j=j):
            t.wait_ge(dGm.sem, dGm.n)
            if j >= 2:
                t.wait_ge(cS.sem, w1s_evac[j - 1])
            cT.tick(t.matmul(pRu[:], onec[:], g1stg[:, j % 2, :], start=(j % 2 == 0), stop=(j % 2 == 1)))

        if j % 2 == 1:
            @E("scalar")
            def _(s, j=j):
                s.wait_ge(cT.sem, cT.n)
                if j >= 3:
                    s.wait_ge(dS.sem, dS.n)
                grp = j // 2
                scale = (1.0 / 256.0) if (grp % 3 == 0) else 1.0
                cS.tick(s.activation(w1row[:], pRu[:], AF.Copy, scale=scale))

            @E("sync")
            def _(sy, j=j):
                sy.wait_ge(cS.sem, cS.n)
                dma(sy, dS, W1s[j // 2:j // 2 + 1, :], w1row[:])
        w1s_evac[j] = cS.n

    if "setup" in dbg:
        @E("gpsimd")
        def _(sy):
            sy.wait_ge(cS.sem, cS.n)
            sy.wait_ge(cV.sem, cV.n)
            for j, tsr in enumerate([sa_b, ca1_b, ga_b, sbv_b, cb1_b, gb_b]):
                dma(sy, dGm, dbg["setup"][j], tsr[:])
            dma(sy, dGm, dbg["setup"][6, 0:12, 0:512], W1s[:])
            dma(sy, dGm, dbg["setup"][7, :, 0:4], invt_c[:])
            dma(sy, dGm, dbg["setup"][7, :, 4:20], floorb[:])
            dma(sy, dGm, dbg["setup"][7, :, 20:276], onormb[:])
            dma(sy, dGm, dbg["setup"][7, 0:1, 276:340], rows_s[:])
            dma(sy, dGm, dbg["setup"][7, 1:2, 276:340], rows_b[:])
    w1s_dma = dS.n
    sc.close()

    pa = contextlib.ExitStack()

    def sba(name, shape, dt=F32):
        return pa.enter_context(nc.sbuf_tensor(name, shape, dt))

    x_sb = [sba("x_sb0", [128, NT, HID]), sba("x_sb1", [128, NT, HID])]
    h_bf = sba("h_bf", [128, NT, HID], BF16)
    hT = sba("hT", [128, 8, SEG], BF16)
    SCR = sba("SCR", [128, NT, HID])
    CONVB = sba("CONVB", [128, 8, SEG])
    tail3 = sba("tail3", [128, 3, 8, 3])
    qTc = sba("qTc", [128, 8, SEG], BF16)
    kTc = sba("kTc", [128, 8, SEG], BF16)
    vTc = [sba("vTc0", [128, 8, SEG], BF16), sba("vTc1", [128, 8, SEG], BF16)]
    TOKA = sba("TOKA", [128, NT, HID], BF16)
    TOKB = sba("TOKB", [128, NT, HID], BF16)
    v_tok = sba("v_tok", [128, NT, HID], BF16)
    kn_tok = sba("kn_tok", [128, NT, HID], BF16)
    TOKC = sba("TOKC", [128, NT, HID], BF16)
    Xu = sba("Xu", [128, NT, NH, 2 * DV], BF16)
    qnT = sba("qnT", [128, 8, SEG], BF16)
    knT = sba("knT", [128, 8, SEG], BF16)
    kbnT = sba("kbnT", [128, 8, SEG], BF16)
    wT_sb = sba("wT_sb", [128, 8, SEG], BF16)
    attnT = sba("attnT", [128, NT * NH, 128], BF16)
    S_bf = sba("S_bf", [128, NH, 2 * DV], BF16)
    TpP = sba("TpP", [128, 128], BF16)
    TpB5 = sba("TpB5", [128, 5, 128], BF16)
    beta_sb = sba("beta_sb", [128, NT, NH])
    statv = sba("statv", [128, NT, NH, 12])
    svT = sba("svT", [12, NH, SEG], BF16)
    hmG = sba("hmG", [128, 4, SEG], BF16)
    wgt = sba("wgt", [128, NT, NH, 4])
    h2_bf = sba("h2_bf", [128, NT, HID], BF16)
    stat_s = sba("stat_s", [128, 32])
    sqd = sba("sqd", [128, HID])
    vwin = sba("vwin", [128, 8, 286], BF16)
    W_ring = sba("W_ring", [128, 24, HID], BF16)
    ez_sb = sba("ez_sb", [128, 4])

    S_ps = [pa.enter_context(nc.psum_tensor(f"S_ps{h}", [128, 512], F32)) for h in range(NH)]
    pA = pa.enter_context(nc.psum_tensor("pA", [128, 512], F32))
    pB = pa.enter_context(nc.psum_tensor("pB", [128, 512], F32))

    @E("sync")
    def _(sy):
        dma(sy, dS, x_sb[0][:], dx[0:SEG, :].rearrange("(T p) d -> p T d", p=128))
    x_ld = {0: dS.n}

    @E("gpsimd")
    def _(g):
        dma(g, dGm, W_ring[:, 0:8, :], KT(dWq))
        dma(g, dGm, W_ring[:, 8:16, :], KT(dWk))
        dma(g, dGm, W_ring[:, 16:24, :], KT(dWv))
    w1_ld = {0: dGm.n}

    @E("vector")
    def _(v):
        v.memset(S_bf[:], 0.0)
        cV.tick(v.memset(tail3[:], 0.0))
    sbf_zero = cV.n

    setup_cS, setup_cV, setup_dS = cS.n, cV.n, dS.n

    @E("tensor")
    def _(t):
        t.wait_ge(cV.sem, max(ones_done, setup_cV))
        t.wait_ge(cS.sem, setup_cS)
        t.wait_ge(dS.sem, setup_dS)
        for h in range(NH):
            ins = t.matmul(S_ps[h][:], zrow[0:1, 0:128], zrow[0:1, 0:512],
                           start=True, stop=False, skip_group_check=True)
        cT.tick(ins)

    x_free = {0: (cV, 0), 1: (cV, 0)}
    scast = {h: (cS, 0) for h in range(NH)}
    vwin_guard = 0
    w4_ld = {}

    for i in range(nseg):
        buf = i % 2
        own = i >= own0
        orow = (i - own0) * SEG

        if i + 1 < nseg:
            @E("sync")
            def _(sy, i=i):
                ctr, val = x_free[(i + 1) % 2]
                if val:
                    sy.wait_ge(ctr.sem, val)
                dma(sy, dS, x_sb[(i + 1) % 2][:],
                    dx[SEG * (i + 1):SEG * (i + 2), :].rearrange("(T p) d -> p T d", p=128))
            x_ld[i + 1] = dS.n

        for T in range(NT):
            @E("vector")
            def _(v, i=i, T=T):
                v.wait_ge(dS.sem, x_ld[i])
                if i == 0 and T == 0:
                    v.wait_ge(cV.sem, sbf_zero)
                cV.tick(v.reduce_sum(stat_s[:, 0:1], x_sb[i % 2][:, T, :], axis=AX.X))

            @E("scalar")
            def _(s, i=i, T=T):
                s.wait_ge(dS.sem, x_ld[i])
                s.wait_ge(cV.sem, cV.n)
                cS.tick(s.activation(sqd[:], x_sb[i % 2][:, T, :], AF.Square, accum_out=stat_s[:, 1:2]))

            @E("vector")
            def _(v):
                v.wait_ge(cS.sem, cS.n)
                v.tensor_scalar(stat_s[:, 2:3], stat_s[:, 0:1], 1.0 / HID, None, ALU.mult)
                v.drain()
                v.tensor_tensor(stat_s[:, 3:4], stat_s[:, 2:3], stat_s[:, 2:3], ALU.mult)
                v.tensor_scalar(stat_s[:, 4:5], stat_s[:, 1:2], 1.0 / HID, 1e-5, ALU.mult, ALU.add)
                v.drain()
                cV.tick(v.tensor_tensor(stat_s[:, 4:5], stat_s[:, 4:5], stat_s[:, 3:4], ALU.subtract))

            @E("scalar")
            def _(s):
                s.wait_ge(cV.sem, cV.n)
                cS.tick(s.activation(stat_s[:, 5:6], stat_s[:, 4:5], AF.Sqrt))

            @E("vector")
            def _(v, i=i, T=T):
                v.wait_ge(cS.sem, cS.n)
                v.reciprocal(stat_s[:, 6:7], stat_s[:, 5:6])
                v.drain()
                v.tensor_scalar(sqd[:], x_sb[i % 2][:, T, :], stat_s[:, 2:3], stat_s[:, 6:7],
                                ALU.subtract, ALU.mult)
                v.tensor_tensor(sqd[:], sqd[:], ca1_b[:], ALU.mult)
                v.tensor_tensor(sqd[:], sqd[:], sa_b[:], ALU.add)
                cV.tick(v.tensor_scalar(h_bf[:, T, :], sqd[:],
                                        mask_sb[:, i * NT + T:i * NT + T + 1], None, ALU.mult))
        h_done = cV.n

        pTt_cm = nc.psum_tensor("pTt%d" % i, [128, 1024], BF16)
        with pTt_cm as pTt:
            for T in range(NT):
                @E("tensor")
                def _(t, T=T):
                    t.wait_ge(cV.sem, h_done if T == 0 else cV.n)
                    for k in range(8):
                        ins = t.transpose(pTt[:, 128 * k:128 * (k + 1)],
                                          h_bf[:, T, 128 * k:128 * (k + 1)], ident[:])
                    cT.tick(ins)

                @E("vector")
                def _(v, T=T):
                    v.wait_ge(cT.sem, cT.n)
                    cV.tick(v.tensor_copy(hT[:, :, 128 * T:128 * (T + 1)],
                                          pTt[:].rearrange("p (k t) -> p k t", t=128)))
            hT_done = cV.n

            @E("tensor")
            def _(t, i=i):
                t.wait_ge(cV.sem, hT_done)
                t.wait_ge(dGm.sem, w1_ld[i])
                t.wait_ge(cS.sem, scast[NH - 1][1])
                for T in range(NT):
                    for k in range(8):
                        ins = t.matmul(pB[:, 4 * T:4 * (T + 1)], hT[:, k, 128 * T:128 * (T + 1)],
                                       Wb_s[:, k, :], start=(T == 0 and k == 0),
                                       stop=(T == NT - 1 and k == 7))
                cT.tick(ins)

            @E("scalar")
            def _(s):
                s.wait_ge(cT.sem, cT.n)
                cS.tick(s.activation(beta_sb[:].rearrange("p T h -> p (T h)"),
                                     pB[:, 0:4 * NT], AF.Sigmoid))
            beta_done = cS.n

            q_on = own or (i == own0 - 1)
            tensors3 = [(qTc, 0, 0), (kTc, 8, 1), (vTc[buf], 16, 2)] if q_on else \
                       [(kTc, 8, 1), (vTc[buf], 16, 2)]
            for dst, base, wi in tensors3:
                for f in range(8):
                    @E("tensor")
                    def _(t, base=base, f=f):
                        t.wait_ge(cS.sem, cS.n)
                        for k in range(8):
                            ins = t.matmul(pA[:, 256 * (f % 2):256 * (f % 2 + 1)],
                                           W_ring[:, base + k, 128 * f:128 * (f + 1)],
                                           hT[:, k, :], start=(k == 0), stop=(k == 7))
                        cT.tick(ins)

                    @E("scalar")
                    def _(s, f=f):
                        s.wait_ge(cT.sem, cT.n)
                        scr8 = SCR[:].rearrange("p T (k2 z) -> p (T k2) z", z=SEG)
                        cS.tick(s.copy(scr8[:, f, :], pA[:, 256 * (f % 2):256 * (f % 2 + 1)]))
                raw_done = cS.n

                @E("vector")
                def _(v, wi=wi, i=i, raw_done=raw_done):
                    v.wait_ge(cS.sem, raw_done)
                    scr8 = SCR[:].rearrange("p T (k2 z) -> p (T k2) z", z=SEG)
                    for dt_ in range(8):
                        tp = lambda j: convt[:, wi, dt_, j:j + 1]
                        v.tensor_scalar(CONVB[:, dt_, :], scr8[:, dt_, :], tp(3), None, ALU.mult)
                        for sh in (1, 2, 3):
                            v.scalar_tensor_tensor(CONVB[:, dt_, sh:SEG], scr8[:, dt_, 0:SEG - sh],
                                                   tp(3 - sh), CONVB[:, dt_, sh:SEG], ALU.mult, ALU.add)
                            v.scalar_tensor_tensor(CONVB[:, dt_, 0:sh], tail3[:, wi, dt_, 3 - sh:3],
                                                   tp(3 - sh), CONVB[:, dt_, 0:sh], ALU.mult, ALU.add)
                    cV.tick(v.tensor_copy(tail3[:, wi, :, :], scr8[:, :, SEG - 3:SEG]))

                @E("scalar")
                def _(s, dst=dst, wi=wi):
                    s.wait_ge(cV.sem, cV.n)
                    if wi == 2 and vwin_guard:
                        s.wait_ge(cV.sem, vwin_guard)
                    cS.tick(s.activation(dst[:].rearrange("p k z -> p (k z)"),
                                         CONVB[:].rearrange("p k z -> p (k z)"), AF.Silu))
            qkv_done = cS.n

            if own:
                @E("gpsimd")
                def _(g):
                    g.wait_ge(cS.sem, qkv_done)
                    dma(g, dGm, W_ring[:, 0:8, :], KT(dWo))
                    dma(g, dGm, W_ring[:, 8:16, 0:512], KT(dgW1[0:HID, :]))
                    dma(g, dGm, W_ring[:, 16:24, 0:512], KT(dgW1[HID + 3 * DV * NH:5120, :]))
                w4_ld[i] = dGm.n

            srcdst = ([(qTc, TOKA)] if own else []) + [(kTc, TOKB), (vTc[buf], v_tok)]
            for src, dst2 in srcdst:
                for T in range(NT):
                    @E("tensor")
                    def _(t, src=src, T=T):
                        t.wait_ge(cS.sem, qkv_done)
                        t.wait_ge(cV.sem, cV.n)
                        for k in range(8):
                            ins = t.transpose(pTt[:, 128 * k:128 * (k + 1)],
                                              src[:, k, 128 * T:128 * (T + 1)], ident[:])
                        cT.tick(ins)

                    @E("vector")
                    def _(v, dst2=dst2, T=T):
                        v.wait_ge(cT.sem, cT.n)
                        cV.tick(v.tensor_copy(dst2[:, T, :],
                                              pTt[:].rearrange("p (k t) -> p k t", t=128)))
            tok_done = cV.n

            for T in range(NT):
                @E("scalar")
                def _(s, T=T, own=own):
                    s.wait_ge(cV.sem, tok_done)
                    for h in range(NH):
                        if own:
                            s.activation(sqd[:, 0:DV], TOKA[:, T, DV * h:DV * (h + 1)], AF.Square,
                                         accum_out=stat_s[:, 8 + h:9 + h])
                        ins = s.activation(sqd[:, 0:DV], TOKB[:, T, DV * h:DV * (h + 1)], AF.Square,
                                           accum_out=stat_s[:, 12 + h:13 + h])
                    cS.tick(ins)

                lo = 8 if own else 12

                @E("vector")
                def _(v, lo=lo):
                    v.wait_ge(cS.sem, cS.n)
                    cV.tick(v.tensor_scalar(stat_s[:, lo:16], stat_s[:, lo:16], 1e-6, None, ALU.add))

                @E("scalar")
                def _(s, lo=lo):
                    s.wait_ge(cV.sem, cV.n)
                    cS.tick(s.activation(stat_s[:, lo + 8:24], stat_s[:, lo:16], AF.Sqrt))

                @E("vector")
                def _(v, T=T, own=own, lo=lo):
                    v.wait_ge(cS.sem, cS.n)
                    v.reciprocal(stat_s[:, lo + 16:32], stat_s[:, lo + 8:24])
                    v.drain()
                    for h in range(NH):
                        if own:
                            v.tensor_scalar(TOKC[:, T, DV * h:DV * (h + 1)],
                                            TOKA[:, T, DV * h:DV * (h + 1)],
                                            stat_s[:, 24 + h:25 + h], None, ALU.mult)
                        v.tensor_scalar(kn_tok[:, T, DV * h:DV * (h + 1)],
                                        TOKB[:, T, DV * h:DV * (h + 1)],
                                        stat_s[:, 28 + h:29 + h], None, ALU.mult)
                        v.tensor_scalar(Xu[:, T, h, 0:DV], v_tok[:, T, DV * h:DV * (h + 1)],
                                        beta_sb[:, T, h:h + 1], None, ALU.mult)
                        ins = v.tensor_scalar(Xu[:, T, h, DV:2 * DV],
                                              kn_tok[:, T, DV * h:DV * (h + 1)],
                                              beta_sb[:, T, h:h + 1], None, ALU.mult)
                    cV.tick(ins)
            norm_done = cV.n

            tlist = ([("qn", qnT)] if own else []) + [("kn", knT), ("kb", kbnT)]
            for srcf, dstt in tlist:
                for T in range(NT):
                    @E("tensor")
                    def _(t, srcf=srcf, T=T):
                        t.wait_ge(cV.sem, cV.n)
                        for hh in range(8):
                            h, half = hh // 2, hh % 2
                            if srcf == "qn":
                                src_ap = TOKC[:, T, DV * h + 128 * half:DV * h + 128 * (half + 1)]
                            elif srcf == "kn":
                                src_ap = kn_tok[:, T, DV * h + 128 * half:DV * h + 128 * (half + 1)]
                            else:
                                src_ap = Xu[:, T, h, DV + 128 * half:DV + 128 * (half + 1)]
                            ins = t.transpose(pTt[:, 128 * hh:128 * (hh + 1)], src_ap, ident[:])
                        cT.tick(ins)

                    @E("vector")
                    def _(v, dstt=dstt, T=T):
                        v.wait_ge(cT.sem, cT.n)
                        cV.tick(v.tensor_copy(dstt[:, :, 128 * T:128 * (T + 1)],
                                              pTt[:].rearrange("p (k t) -> p k t", t=128)))
            tposed_done = cV.n

        with nc.psum_tensor("pTb%d" % i, [128, 512], BF16) as pTb:
            for T in range(NT):
                for h in range(NH):
                    @E("tensor")
                    def _(t, T=T, h=h, own=own):
                        t.wait_ge(cV.sem, cV.n)
                        for half in range(2):
                            t.matmul(pA[:, 0:128], kbnT[:, 2 * h + half, 128 * T:128 * (T + 1)],
                                     knT[:, 2 * h + half, 128 * T:128 * (T + 1)],
                                     start=(half == 0), stop=(half == 1))
                        for half in range(2):
                            ins = t.matmul(pA[:, 128:256], knT[:, 2 * h + half, 128 * T:128 * (T + 1)],
                                           kbnT[:, 2 * h + half, 128 * T:128 * (T + 1)],
                                           start=(half == 0), stop=(half == 1))
                        if own:
                            for half in range(2):
                                ins = t.matmul(pA[:, 256:384], knT[:, 2 * h + half, 128 * T:128 * (T + 1)],
                                               qnT[:, 2 * h + half, 128 * T:128 * (T + 1)],
                                               start=(half == 0), stop=(half == 1))
                        cT.tick(ins)

                    @E("vector")
                    def _(v, T=T, h=h, own=own):
                        v.wait_ge(cT.sem, cT.n)
                        v.tensor_tensor(TpP[:], pA[:, 0:128], maskA_s[:], ALU.mult)
                        ins = v.tensor_tensor(TpB5[:, 0, :], pA[:, 128:256], maskAU_s[:], ALU.mult)
                        if own:
                            ins = v.tensor_tensor(attnT[:, NH * T + h, :], pA[:, 256:384],
                                                  maskT_s[:], ALU.mult)
                        cV.tick(ins)

                    for j in range(1, 5):
                        @E("tensor")
                        def _(t, j=j):
                            t.wait_ge(cV.sem, cV.n)
                            t.wait_ge(cS.sem, cS.n)
                            cT.tick(t.matmul(pB[:, 0:128], TpB5[:, j - 1, :], TpP[:],
                                             start=True, stop=True))

                        @E("scalar")
                        def _(s):
                            s.wait_ge(cT.sem, cT.n)
                            cS.tick(s.copy(TpP[:], pB[:, 0:128]))

                        @E("vector")
                        def _(v, j=j):
                            v.wait_ge(cS.sem, cS.n)
                            cV.tick(v.transpose(TpB5[:, j, :], TpP[:]))

                    for j in range(4, -1, -1):
                        @E("tensor")
                        def _(t, T=T, h=h, j=j):
                            t.wait_ge(cV.sem, cV.n)
                            cT.tick(t.matmul(pB[:], TpB5[:, j, :], Xu[:, T, h, :], start=True, stop=True))

                        @E("vector")
                        def _(v, T=T, h=h):
                            v.wait_ge(cT.sem, cT.n)
                            cV.tick(v.tensor_tensor(Xu[:, T, h, :], Xu[:, T, h, :], pB[:], ALU.add))
            tchain_done = cV.n

            for T in range(NT):
                for grp in range(2):
                    @E("tensor")
                    def _(t, T=T, grp=grp):
                        t.wait_ge(cV.sem, cV.n)
                        for j in range(4):
                            hh = 4 * grp + j
                            h, half = hh // 2, hh % 2
                            ins = t.transpose(pTb[:, 128 * j:128 * (j + 1)],
                                              Xu[:, T, h, DV + 128 * half:DV + 128 * (half + 1)],
                                              ident[:])
                        cT.tick(ins)

                    @E("vector")
                    def _(v, T=T, grp=grp):
                        v.wait_ge(cT.sem, cT.n)
                        cV.tick(v.tensor_copy(wT_sb[:, 4 * grp:4 * (grp + 1), 128 * T:128 * (T + 1)],
                                              pTb[:, 0:512].rearrange("p (k t) -> p k t", t=128)))
            wT_done = cV.n

        with nc.psum_tensor("wSa%d" % i, [128, 512], F32) as wSa, \
             nc.psum_tensor("wSb%d" % i, [128, 512], F32) as wSb:
            for T in range(NT):
                for c in range(4):
                    for h in range(NH):
                        wbank = wSa if h % 2 == 0 else wSb
                        obank = pA if h < 2 else pB
                        ocol = 256 * (h % 2)
                        r0, r1 = 32 * c, 32 * (c + 1)

                        @E("tensor")
                        def _(t, T=T, c=c, h=h, wbank=wbank, obank=obank, ocol=ocol,
                              r0=r0, r1=r1, own=own):
                            t.wait_ge(cS.sem, max(scast[h][1], cS.n if (c == 0 and h == 0) else 0))
                            if T == 0 and c == 0 and h == 0:
                                t.wait_ge(cV.sem, wT_done)
                            if h >= 2:
                                t.wait_ge(cV.sem, cV.n)
                            for half in range(2):
                                ins = t.matmul(wbank[r0:r1, 0:256],
                                               wT_sb[:, 2 * h + half, 128 * T + r0:128 * T + r1],
                                               S_bf[:, h, 256 * half:256 * (half + 1)],
                                               start=(half == 0), stop=(half == 1),
                                               tile_position=(0, r0), skip_group_check=True)
                            if own:
                                for half in range(2):
                                    ins = t.matmul(obank[r0:r1, ocol:ocol + 256],
                                                   qnT[:, 2 * h + half, 128 * T + r0:128 * T + r1],
                                                   S_bf[:, h, 256 * half:256 * (half + 1)],
                                                   start=(half == 0 and h % 2 == 0), stop=False,
                                                   tile_position=(0, r0), skip_group_check=True)
                            cT.tick(ins)

                        @E("vector")
                        def _(v, T=T, h=h, wbank=wbank, r0=r0, r1=r1):
                            v.wait_ge(cT.sem, cT.n)
                            cV.tick(v.tensor_tensor(Xu[r0:r1, T, h, 0:DV], Xu[r0:r1, T, h, 0:DV],
                                                    wbank[r0:r1, 0:256], ALU.subtract))

                        @E("tensor")
                        def _(t, T=T, c=c, h=h, r0=r0, r1=r1):
                            t.wait_ge(cV.sem, cV.n)
                            for mc in range(2):
                                ins = t.matmul(S_ps[h][:, 256 * mc:256 * (mc + 1)],
                                               kn_tok[r0:r1, T, DV * h + 128 * mc:DV * h + 128 * (mc + 1)],
                                               Xu[r0:r1, T, h, 0:DV],
                                               start=False, stop=False,
                                               tile_position=(r0, 0), skip_group_check=True)
                            cT.tick(ins)

                        @E("scalar")
                        def _(s, h=h):
                            s.wait_ge(cT.sem, cT.n)
                            cS.tick(s.copy(S_bf[:, h, :], S_ps[h][:]))
                        scast[h] = (cS, cS.n)

                if own:
                    for h in range(NH):
                        obank = pA if h < 2 else pB
                        ocol = 256 * (h % 2)

                        @E("tensor")
                        def _(t, T=T, h=h, obank=obank, ocol=ocol):
                            t.wait_ge(cV.sem, cV.n)
                            cT.tick(t.matmul(obank[:, ocol:ocol + 256], attnT[:, NH * T + h, :],
                                             Xu[:, T, h, 0:DV], start=False, stop=(h % 2 == 1),
                                             skip_group_check=True))
                        if h % 2 == 1:
                            @E("scalar")
                            def _(s, T=T, h=h, obank=obank):
                                s.wait_ge(cT.sem, cT.n)
                                cS.tick(s.copy(TOKA[:, T, 512 * (h // 2):512 * (h // 2 + 1)], obank[:]))
            scan_done = cS.n
        if not own:
            x_free[buf] = (cS, scan_done)

        if taps:
            @E("sync")
            def _(sy, i=i, h_done=h_done, tok_done=tok_done, beta_done=beta_done,
                  norm_done=norm_done, scan_done=scan_done, own=own):
                rr = lambda d: d.rearrange("(T p) d -> p T d", p=128)
                tap_rows = slice(SEG * i, SEG * (i + 1))
                if "h" in dbg:
                    sy.wait_ge(cV.sem, h_done)
                    dma(sy, dS, rr(dbg["h"][tap_rows, :]), h_bf[:])
                if "k" in dbg:
                    sy.wait_ge(cV.sem, tok_done)
                    dma(sy, dS, rr(dbg["k"][tap_rows, :]), TOKB[:])
                if "v" in dbg:
                    sy.wait_ge(cV.sem, tok_done)
                    dma(sy, dS, rr(dbg["v"][tap_rows, :]), v_tok[:])
                if "q" in dbg and own:
                    sy.wait_ge(cV.sem, tok_done)
                    dma(sy, dS, rr(dbg["q"][tap_rows, :]), TOKA[:])
                if "beta" in dbg:
                    sy.wait_ge(cS.sem, beta_done)
                    dma(sy, dS, dbg["beta"][tap_rows, :].rearrange("(T p) h -> p T h", p=128), beta_sb[:])
                if "kn" in dbg:
                    sy.wait_ge(cV.sem, norm_done)
                    dma(sy, dS, rr(dbg["kn"][tap_rows, :]), kn_tok[:])
                if "qn" in dbg and own:
                    sy.wait_ge(cV.sem, norm_done)
                    dma(sy, dS, rr(dbg["qn"][tap_rows, :]), TOKC[:])
                if "uw" in dbg:
                    sy.wait_ge(cS.sem, scan_done)
                    dma(sy, dS, dbg["uw"][2 * i:2 * i + 2].rearrange("T p h d -> p T h d"), Xu[:])
                if "delta_o" in dbg and own:
                    sy.wait_ge(cS.sem, scan_done)
                    dma(sy, dS, rr(dbg["delta_o"][tap_rows, :]), TOKA[:])

        if not own:
            w1_ld[i + 1] = w1_ld[i]
            continue

        with nc.psum_tensor("pTg%d" % i, [128, 1024], BF16) as pTg, \
             nc.psum_tensor("pCg%d" % i, [128, 512], F32) as pCg:
            for T in range(NT):
                @E("tensor")
                def _(t, T=T, scan_done=scan_done):
                    t.wait_ge(cS.sem, scan_done)
                    t.wait_ge(cV.sem, cV.n)
                    for hh in range(8):
                        h, half = hh // 2, hh % 2
                        ins = t.transpose(pTg[:, 128 * hh:128 * (hh + 1)],
                                          TOKA[:, T, DV * h + 128 * half:DV * h + 128 * (half + 1)],
                                          ident[:])
                    cT.tick(ins)

                @E("vector")
                def _(v, T=T):
                    v.wait_ge(cT.sem, cT.n)
                    cV.tick(v.tensor_copy(qTc[:, :, 128 * T:128 * (T + 1)],
                                          pTg[:].rearrange("p (k t) -> p k t", t=128)))
            delot_done = cV.n

            @E("vector")
            def _(v, i=i, buf=buf, qkv_done=qkv_done):
                v.wait_ge(cS.sem, qkv_done)
                v.tensor_copy(vwin[:, :, 0:30], vTc[1 - buf][:, :, SEG - 30:SEG])
                cV.tick(v.tensor_copy(vwin[:, :, 30:286], vTc[buf][:]))
            vwin_done = cV.n
            vwin_guard = vwin_done

            for ntap, ftap, dstT, dstTok in [(3, firs_t, knT, TOKB), (31, firl_t, kbnT, TOKC)]:
                @E("vector")
                def _(v, ntap=ntap, ftap=ftap):
                    v.wait_ge(cV.sem, vwin_done)
                    v.wait_ge(cS.sem, cS.n)
                    for dt_ in range(8):
                        base = 30 - (ntap - 1)
                        v.tensor_scalar(CONVB[:, dt_, :], vwin[:, dt_, base:base + SEG],
                                        ftap[:, dt_, 0:1], None, ALU.mult)
                        for j in range(1, ntap):
                            ins = v.scalar_tensor_tensor(CONVB[:, dt_, :],
                                                         vwin[:, dt_, base + j:base + j + SEG],
                                                         ftap[:, dt_, j:j + 1], CONVB[:, dt_, :],
                                                         ALU.mult, ALU.add)
                    cV.tick(ins)

                @E("scalar")
                def _(s, dstT=dstT):
                    s.wait_ge(cV.sem, cV.n)
                    cS.tick(s.copy(dstT[:].rearrange("p k z -> p (k z)"),
                                   CONVB[:].rearrange("p k z -> p (k z)")))
                fcast = cS.n

                for T in range(NT):
                    @E("tensor")
                    def _(t, dstT=dstT, T=T, fcast=fcast):
                        t.wait_ge(cS.sem, fcast)
                        t.wait_ge(cV.sem, cV.n)
                        for k in range(8):
                            ins = t.transpose(pTg[:, 128 * k:128 * (k + 1)],
                                              dstT[:, k, 128 * T:128 * (T + 1)], ident[:])
                        cT.tick(ins)

                    @E("vector")
                    def _(v, dstTok=dstTok, T=T):
                        v.wait_ge(cT.sem, cT.n)
                        cV.tick(v.tensor_copy(dstTok[:, T, :],
                                              pTg[:].rearrange("p (k t) -> p k t", t=128)))
            fir_done = cV.n

            for T in range(NT):
                for h in range(NH):
                    @E("vector")
                    def _(v, T=T, h=h, fir_done=fir_done):
                        v.wait_ge(cV.sem, fir_done if (T == 0 and h == 0) else cV.n)
                        for bi, bsrc in enumerate([TOKB, TOKC, TOKA, v_tok]):
                            sl = bsrc[:, T, DV * h:DV * (h + 1)]
                            v.reduce_sum(statv[:, T, h, 3 * bi:3 * bi + 1], sl, axis=AX.X)
                            ins = v.tensor_reduce(statv[:, T, h, 3 * bi + 2:3 * bi + 3], sl,
                                                  axis=AX.X, op=ALU.max)
                        cV.tick(ins)

                    @E("scalar")
                    def _(s, T=T, h=h):
                        s.wait_ge(cV.sem, cV.n)
                        for bi, bsrc in enumerate([TOKB, TOKC, TOKA, v_tok]):
                            ins = s.activation(sqd[:, 0:DV], bsrc[:, T, DV * h:DV * (h + 1)],
                                               AF.Square, accum_out=stat_s[:, 8 + bi:9 + bi])
                        cS.tick(ins)

                    @E("vector")
                    def _(v):
                        v.wait_ge(cS.sem, cS.n)
                        cV.tick(v.tensor_scalar(stat_s[:, 12:16], stat_s[:, 8:12], 1.0 / DV, 1e-8,
                                                ALU.mult, ALU.max))

                    @E("scalar")
                    def _(s):
                        s.wait_ge(cV.sem, cV.n)
                        cS.tick(s.activation(stat_s[:, 16:20], stat_s[:, 12:16], AF.Sqrt))

                    @E("vector")
                    def _(v, T=T, h=h):
                        v.wait_ge(cS.sem, cS.n)
                        cV.tick(v.tensor_copy(statv[:, T, h, 1:11:3], stat_s[:, 16:20]))
            stats_done = cV.n

            for T in range(NT):
                for h in range(NH):
                    @E("tensor")
                    def _(t, T=T, h=h):
                        t.wait_ge(cV.sem, cV.n)
                        cT.tick(t.transpose(pCg[0:12, 0:128], statv[:, T, h, :], identf[:]))

                    @E("vector")
                    def _(v, T=T, h=h):
                        v.wait_ge(cT.sem, cT.n)
                        cV.tick(v.tensor_copy(svT[0:12, h, 128 * T:128 * (T + 1)], pCg[0:12, 0:128]))
            svt_done = cV.n

            for h in range(NH):
                for Mc in range(4):
                    @E("tensor")
                    def _(t, i=i, h=h, Mc=Mc, svt_done=svt_done):
                        t.wait_ge(cV.sem, svt_done if (h == 0 and Mc == 0) else cV.n)
                        if h == 0 and Mc == 0:
                            t.wait_ge(dGm.sem, w4_ld[i])
                            t.wait_ge(dS.sem, w1s_dma)
                        t.wait_ge(cS.sem, cS.n)
                        pg = pA if Mc % 2 == 0 else pB
                        t.matmul(pg[:, 0:256], W_ring[:, 8 + 0, 128 * Mc:128 * (Mc + 1)], hT[:, 0, :],
                                 start=True, stop=False)
                        for k in range(1, 8):
                            t.matmul(pg[:, 0:256], W_ring[:, 8 + k, 128 * Mc:128 * (Mc + 1)],
                                     hT[:, k, :], start=False, stop=False)
                        for bi, bT in enumerate([knT, kbnT, qTc, vTc[i % 2]]):
                            for half in range(2):
                                t.matmul(pg[:, 0:256], W_ring[:, 16 + 2 * bi + half, 128 * Mc:128 * (Mc + 1)],
                                         bT[:, 2 * h + half, :], start=False, stop=False)
                        t.matmul(pg[:, 0:256], W1s[0:12, 128 * Mc:128 * (Mc + 1)], svT[0:12, h, :],
                                 start=False, stop=False)
                        cT.tick(t.matmul(pg[:, 0:256], gb1_b[0:1, 128 * Mc:128 * (Mc + 1)],
                                         ones1[0:1, 0:SEG], start=False, stop=True))

                    @E("scalar")
                    def _(s, Mc=Mc):
                        s.wait_ge(cT.sem, cT.n)
                        pg = pA if Mc % 2 == 0 else pB
                        cS.tick(s.activation(hmG[:, Mc, :], pg[:, 0:256], AF.Gelu))
                gelu_done = cS.n

                for T in range(NT):
                    @E("tensor")
                    def _(t, h=h, T=T, gelu_done=gelu_done):
                        t.wait_ge(cS.sem, gelu_done)
                        t.wait_ge(cV.sem, cV.n)
                        for Mc in range(4):
                            t.matmul(pCg[:, 4 * T:4 * (T + 1)], hmG[:, Mc, 128 * T:128 * (T + 1)],
                                     gW2h[:, Mc, 4 * h:4 * (h + 1)], start=(Mc == 0), stop=False,
                                     skip_group_check=True)
                        cT.tick(t.matmul(pCg[:, 4 * T:4 * (T + 1)], ones1[0:1, 128 * T:128 * (T + 1)],
                                         gb2h[0:1, 4 * h:4 * (h + 1)], start=False, stop=True,
                                         skip_group_check=True))

                    @E("vector")
                    def _(v, T=T):
                        v.wait_ge(cT.sem, cT.n)
                        cV.tick(v.tensor_reduce(stat_s[:, 0:1], pCg[:, 4 * T:4 * (T + 1)],
                                                axis=AX.X, op=ALU.max, negate=True))

                    @E("scalar")
                    def _(s, T=T):
                        s.wait_ge(cV.sem, cV.n)
                        cS.tick(s.activation(ez_sb[:], pCg[:, 4 * T:4 * (T + 1)], AF.Exp,
                                             bias=stat_s[:, 0:1], accum_out=stat_s[:, 1:2]))

                    @E("vector")
                    def _(v, T=T, h=h):
                        v.wait_ge(cS.sem, cS.n)
                        v.reciprocal(stat_s[:, 2:3], stat_s[:, 1:2])
                        v.drain()
                        v.tensor_scalar(ez_sb[:], ez_sb[:], stat_s[:, 2:3], None, ALU.mult)
                        v.drain()
                        v.tensor_tensor(ez_sb[:], ez_sb[:], floorb[:, 4 * h:4 * (h + 1)], ALU.max)
                        v.drain()
                        v.reduce_sum(stat_s[:, 3:4], ez_sb[:], axis=AX.X)
                        v.drain()
                        v.reciprocal(stat_s[:, 4:5], stat_s[:, 3:4])
                        v.drain()
                        cV.tick(v.tensor_scalar(wgt[:, T, h, :], ez_sb[:], stat_s[:, 4:5],
                                                None, ALU.mult))
            gate_done = cV.n

            for T in range(NT):
                for h in range(NH):
                    @E("vector")
                    def _(v, T=T, h=h, gate_done=gate_done):
                        v.wait_ge(cV.sem, gate_done if (T == 0 and h == 0) else cV.n)
                        dst = SCR[:, T, DV * h:DV * (h + 1)]
                        v.tensor_scalar(dst, TOKB[:, T, DV * h:DV * (h + 1)], wgt[:, T, h, 0:1],
                                        None, ALU.mult)
                        v.scalar_tensor_tensor(dst, TOKC[:, T, DV * h:DV * (h + 1)], wgt[:, T, h, 1:2],
                                               dst, ALU.mult, ALU.add)
                        v.scalar_tensor_tensor(dst, TOKA[:, T, DV * h:DV * (h + 1)], wgt[:, T, h, 2:3],
                                               dst, ALU.mult, ALU.add)
                        cV.tick(v.scalar_tensor_tensor(dst, v_tok[:, T, DV * h:DV * (h + 1)],
                                                       wgt[:, T, h, 3:4], dst, ALU.mult, ALU.add))

                    @E("scalar")
                    def _(s, T=T, h=h):
                        s.wait_ge(cV.sem, cV.n)
                        cS.tick(s.activation(sqd[:, 0:DV], SCR[:, T, DV * h:DV * (h + 1)], AF.Square,
                                             accum_out=stat_s[:, 8:9]))

                    @E("vector")
                    def _(v):
                        v.wait_ge(cS.sem, cS.n)
                        cV.tick(v.tensor_scalar(stat_s[:, 9:10], stat_s[:, 8:9], 1.0 / DV, 1e-5,
                                                ALU.mult, ALU.add))

                    @E("scalar")
                    def _(s):
                        s.wait_ge(cV.sem, cV.n)
                        cS.tick(s.activation(stat_s[:, 10:11], stat_s[:, 9:10], AF.Sqrt))

                    @E("vector")
                    def _(v, T=T, h=h):
                        v.wait_ge(cS.sem, cS.n)
                        v.reciprocal(stat_s[:, 11:12], stat_s[:, 10:11])
                        v.drain()
                        cV.tick(v.scalar_tensor_tensor(h_bf[:, T, DV * h:DV * (h + 1)],
                                                       SCR[:, T, DV * h:DV * (h + 1)],
                                                       stat_s[:, 11:12], onormb[:],
                                                       ALU.mult, ALU.mult))
            mix_done = cV.n

            for T in range(NT):
                @E("tensor")
                def _(t, T=T, mix_done=mix_done):
                    t.wait_ge(cV.sem, mix_done if T == 0 else cV.n)
                    for k in range(8):
                        ins = t.transpose(pTg[:, 128 * k:128 * (k + 1)],
                                          h_bf[:, T, 128 * k:128 * (k + 1)], ident[:])
                    cT.tick(ins)

                @E("vector")
                def _(v, T=T):
                    v.wait_ge(cT.sem, cT.n)
                    cV.tick(v.tensor_copy(kTc[:, :, 128 * T:128 * (T + 1)],
                                          pTg[:].rearrange("p (k t) -> p k t", t=128)))
            oT_done = cV.n

            for T in range(NT):
                for n in range(2):
                    @E("tensor")
                    def _(t, T=T, n=n, oT_done=oT_done):
                        t.wait_ge(cV.sem, oT_done if (T == 0 and n == 0) else cV.n)
                        pg = pA if n == 0 else pB
                        for k in range(8):
                            ins = t.matmul(pg[:], kTc[:, k, 128 * T:128 * (T + 1)],
                                           W_ring[:, 0 + k, 512 * n:512 * (n + 1)],
                                           start=(k == 0), stop=(k == 7))
                        cT.tick(ins)

                    @E("vector")
                    def _(v, T=T, n=n, i=i):
                        v.wait_ge(cT.sem, cT.n)
                        pg = pA if n == 0 else pB
                        dst = SCR[:, T, 512 * n:512 * (n + 1)]
                        v.tensor_tensor(dst, pg[:], ga_b[:, 512 * n:512 * (n + 1)], ALU.mult)
                        cV.tick(v.tensor_tensor(dst, dst, x_sb[i % 2][:, T, 512 * n:512 * (n + 1)],
                                                ALU.add))
            xmid_done = cV.n
            x_free[buf] = (cV, xmid_done)

            for T in range(NT):
                @E("vector")
                def _(v, T=T, xmid_done=xmid_done):
                    v.wait_ge(cV.sem, xmid_done)
                    cV.tick(v.reduce_sum(stat_s[:, 0:1], SCR[:, T, :], axis=AX.X))

                @E("scalar")
                def _(s, T=T):
                    s.wait_ge(cV.sem, cV.n)
                    cS.tick(s.activation(sqd[:], SCR[:, T, :], AF.Square, accum_out=stat_s[:, 1:2]))

                @E("vector")
                def _(v):
                    v.wait_ge(cS.sem, cS.n)
                    v.tensor_scalar(stat_s[:, 2:3], stat_s[:, 0:1], 1.0 / HID, None, ALU.mult)
                    v.tensor_tensor(stat_s[:, 3:4], stat_s[:, 2:3], stat_s[:, 2:3], ALU.mult)
                    v.tensor_scalar(stat_s[:, 4:5], stat_s[:, 1:2], 1.0 / HID, 1e-5, ALU.mult, ALU.add)
                    cV.tick(v.tensor_tensor(stat_s[:, 4:5], stat_s[:, 4:5], stat_s[:, 3:4], ALU.subtract))

                @E("scalar")
                def _(s):
                    s.wait_ge(cV.sem, cV.n)
                    cS.tick(s.activation(stat_s[:, 5:6], stat_s[:, 4:5], AF.Sqrt))

                @E("vector")
                def _(v, T=T):
                    v.wait_ge(cS.sem, cS.n)
                    v.reciprocal(stat_s[:, 6:7], stat_s[:, 5:6])
                    v.drain()
                    v.tensor_scalar(sqd[:], SCR[:, T, :], stat_s[:, 2:3], stat_s[:, 6:7],
                                    ALU.subtract, ALU.mult)
                    v.tensor_tensor(sqd[:], sqd[:], cb1_b[:], ALU.mult)
                    cV.tick(v.tensor_tensor(h2_bf[:, T, :], sqd[:], sbv_b[:], ALU.add))
            h2_done = cV.n

            @E("sync")
            def _(sy, orow=orow, h2_done=h2_done):
                sy.wait_ge(cV.sem, h2_done)
                dma(sy, dS, d_xmid[orow:orow + SEG, :].rearrange("(T p) d -> p T d", p=128), SCR[:])
                dma(sy, dS, d_h2[orow:orow + SEG, :].rearrange("(T p) d -> p T d", p=128), h2_bf[:])

            if taps:
                @E("sync")
                def _(sy, orow=orow, i=i, fir_done=fir_done, stats_done=stats_done,
                      gate_done=gate_done, mix_done=mix_done, xmid_done=xmid_done, h2_done=h2_done):
                    rr = lambda d: d.rearrange("(T p) d -> p T d", p=128)
                    if "fir_s" in dbg:
                        sy.wait_ge(cV.sem, fir_done)
                        dma(sy, dS, rr(dbg["fir_s"][orow:orow + SEG, :]), TOKB[:])
                    if "fir_l" in dbg:
                        sy.wait_ge(cV.sem, fir_done)
                        dma(sy, dS, rr(dbg["fir_l"][orow:orow + SEG, :]), TOKC[:])
                    if "stat" in dbg:
                        sy.wait_ge(cV.sem, stats_done)
                        dma(sy, dS, dbg["stat"][2 * (i - own0):2 * (i - own0) + 2]
                            .rearrange("T p h d -> p T h d"), statv[:])
                    if "wgt" in dbg:
                        sy.wait_ge(cV.sem, gate_done)
                        dma(sy, dS, dbg["wgt"][2 * (i - own0):2 * (i - own0) + 2]
                            .rearrange("T p h e -> p T h e"), wgt[:])
                    if "o_mix" in dbg:
                        sy.wait_ge(cV.sem, mix_done)
                        dma(sy, dS, rr(dbg["o_mix"][orow:orow + SEG, :]), h_bf[:])
                    if "x_mid" in dbg:
                        sy.wait_ge(cV.sem, xmid_done)
                        dma(sy, dS, rr(dbg["x_mid"][orow:orow + SEG, :]), SCR[:])
                    if "h2" in dbg:
                        sy.wait_ge(cV.sem, h2_done)
                        dma(sy, dS, rr(dbg["h2"][orow:orow + SEG, :]), h2_bf[:])

            if i + 1 < nseg:
                @E("gpsimd")
                def _(g):
                    g.wait_ge(cT.sem, cT.n)
                    dma(g, dGm, W_ring[:, 0:8, :], KT(dWq))
                    dma(g, dGm, W_ring[:, 8:16, :], KT(dWk))
                    dma(g, dGm, W_ring[:, 16:24, :], KT(dWv))
                w1_ld[i + 1] = dGm.n

    paT, paV, paS, paG, paD, paGm = cT.n, cV.n, cS.n, cG.n, dS.n, dGm.n
    pa.close()

    pb = contextlib.ExitStack()

    def sbb(name, shape, dt=F32):
        return pb.enter_context(nc.sbuf_tensor(name, shape, dt))

    h2T = sbb("h2T", [128, 8, PB_TOK], BF16)
    acc = sbb("acc", [128, PB_TOK // 128, HID])
    wb1 = [sbb("wb1_0", [128, 8, EI], BF16), sbb("wb1_1", [128, 8, EI], BF16)]
    wb3 = [sbb("wb3_0", [128, 8, EI], BF16), sbb("wb3_1", [128, 8, EI], BF16)]
    wb2 = [sbb("wb2_0", [128, 6, HID], BF16), sbb("wb2_1", [128, 6, HID], BF16)]
    SCB1 = sbb("SCB1", [128, PB_TOK // 128, NE])
    SCB2 = sbb("SCB2", [128, PB_TOK // 128, NE])
    mrscr = sbb("mrscr", [128, NE])
    mx8 = sbb("mx8", [128, 8])
    aT = sbb("aT", [128, 6, PB_CH], BF16)
    gtmp = sbb("gtmp", [128, 2, PB_CH], BF16)
    yscr = sbb("yscr", [128, 2, 512])
    xmb = sbb("xmb", [128, 2, HID])
    obuf = sbb("obuf", [128, 2, HID])

    pG = [pb.enter_context(nc.psum_tensor("pG0", [128, 512], F32)),
          pb.enter_context(nc.psum_tensor("pG1", [128, 512], F32))]
    pU = [pb.enter_context(nc.psum_tensor("pU0", [128, 512], F32)),
          pb.enter_context(nc.psum_tensor("pU1", [128, 512], F32))]
    pO = [pb.enter_context(nc.psum_tensor("pO0", [128, 512], F32)),
          pb.enter_context(nc.psum_tensor("pO1", [128, 512], F32))]

    NPT = PB_TOK // 128
    exp_seq = []
    gu_free = {}
    po_free = {}
    at_free = {0: 0}

    for p in range(npass):
        pbase = p * PB_TOK

        @E("scalar")
        def _(s, p=p):
            if p == 0:
                s.wait_ge(cT.sem, paT)
                s.wait_ge(cV.sem, paV)
                s.wait_ge(cG.sem, paG)
                s.wait_ge(dS.sem, paD)
                s.wait_ge(dGm.sem, paGm)
            else:
                s.wait_ge(cV.sem, cV.n)
            for k in range(8):
                dma(s, dA, h2T[:, k, :], d_h2[pbase:pbase + PB_TOK, 128 * k:128 * (k + 1)],
                    transpose=True)
        h2t_ld = dA.n

        @E("tensor")
        def _(t, p=p):
            t.wait_ge(dA.sem, h2t_ld)
            if p == 0:
                t.wait_ge(cV.sem, paV)
                t.wait_ge(cS.sem, paS)
            for tt in range(NPT):
                for k in range(8):
                    ins = t.matmul(pO[0][:, NE * tt:NE * (tt + 1)], h2T[:, k, 128 * tt:128 * (tt + 1)],
                                   gatewT[:, k, :], start=(tt == 0 and k == 0),
                                   stop=(tt == NPT - 1 and k == 7))
            cT.tick(ins)

        @E("scalar")
        def _(s):
            s.wait_ge(cT.sem, cT.n)
            cS.tick(s.activation(SCB1[:].rearrange("p T e -> p (T e)"), pO[0][:, 0:NE * NPT], AF.Sigmoid))

        @E("vector")
        def _(v, p=p):
            v.wait_ge(cS.sem, cS.n)
            if p == 0:
                v.wait_ge(cG.sem, paG)
                v.wait_ge(dGm.sem, paGm)
            v.memset(mx8[:, 1:8], 3.0e38)
            for tt in range(NPT):
                v.tensor_tensor(SCB2[:, tt, :], SCB1[:, tt, :], ebias_b[:], ALU.add)
                v.drain()
                for r in range(4):
                    src = SCB2[:, tt, :] if r % 2 == 0 else mrscr[:]
                    dstm = mrscr[:] if r % 2 == 0 else SCB2[:, tt, :]
                    v.reduce_max(mx8[:, 0:1], src, axis=AX.X)
                    v.drain()
                    v.match_replace(dstm, mx8[:], src, -1e30)
                    v.drain()
                v.tensor_scalar(mrscr[:], SCB2[:, tt, :], -1e29, None, ALU.is_le)
                v.drain()
                v.tensor_tensor(mrscr[:], mrscr[:], SCB1[:, tt, :], ALU.mult)
                v.drain()
                v.reduce_sum(mx8[:, 0:1], mrscr[:], axis=AX.X)
                v.drain()
                v.reciprocal(mx8[:, 1:2], mx8[:, 0:1])
                v.drain()
                v.tensor_scalar(SCB1[:, tt, :], mrscr[:], mx8[:, 1:2], None, ALU.mult)
                ins = v.memset(mx8[:, 1:8], 3.0e38)
            cV.tick(ins)
        comb_done = cV.n

        if "combine" in dbg:
            @E("sync")
            def _(sy, pbase=pbase, comb_done=comb_done):
                sy.wait_ge(cV.sem, comb_done)
                dma(sy, dS, dbg["combine"][pbase:pbase + PB_TOK, :]
                    .rearrange("(T p) e -> p T e", p=128), SCB1[:])

        @E("vector")
        def _(v, comb_done=comb_done):
            v.wait_ge(cV.sem, comb_done)
            cV.tick(v.memset(acc[:], 0.0))
        acc_zero = cV.n

        for e in range(nexp):
            ebuf = e % 2

            @E("gpsimd")
            def _(g, p=p, e=e, ebuf=ebuf):
                if p == 0 and e == 0:
                    g.wait_ge(cV.sem, paV)
                    g.wait_ge(cS.sem, paS)
                    g.wait_ge(cT.sem, paT)
                if len(exp_seq) >= 2:
                    g.wait_ge(cT.sem, exp_seq[-2])
                dma(g, dGm, wb1[ebuf][:].rearrange("p k i -> p (k i)"), dw1[e])
                dma(g, dGm, wb3[ebuf][:].rearrange("p k i -> p (k i)"), dw3[e])
                dma(g, dGm, wb2[ebuf][:].rearrange("p k i -> p (k i)"), dw2[e])
            wexp = dGm.n

            for ch in range(PB_TOK // PB_CH):
                cbase = PB_CH * ch
                for ic in range(6):
                    gp = ic % 2

                    @E("tensor")
                    def _(t, e=e, ebuf=ebuf, ic=ic, gp=gp, cbase=cbase, wexp=wexp, acc_zero=acc_zero):
                        t.wait_ge(dGm.sem, wexp)
                        t.wait_ge(cV.sem, max(acc_zero, gu_free.get(gp, 0)))
                        for k in range(8):
                            t.matmul(pG[gp][:], wb1[ebuf][:, k, 128 * ic:128 * (ic + 1)],
                                     h2T[:, k, cbase:cbase + PB_CH], start=(k == 0), stop=(k == 7))
                        for k in range(8):
                            ins = t.matmul(pU[gp][:], wb3[ebuf][:, k, 128 * ic:128 * (ic + 1)],
                                           h2T[:, k, cbase:cbase + PB_CH], start=(k == 0), stop=(k == 7))
                        cT.tick(ins)
                    mm_done = cT.n

                    @E("scalar")
                    def _(s, gp=gp, mm_done=mm_done):
                        s.wait_ge(cT.sem, mm_done)
                        cS.tick(s.activation(gtmp[:, gp, :], pG[gp][:], AF.Silu))

                    @E("vector")
                    def _(v, ic=ic, gp=gp):
                        v.wait_ge(cS.sem, cS.n)
                        if ic == 0:
                            v.wait_ge(cT.sem, at_free[0])
                        cV.tick(v.tensor_tensor(aT[:, ic, :], gtmp[:, gp, :], pU[gp][:], ALU.mult))
                    gu_free[gp] = cV.n
                a_done = cV.n

                for Mt in range(PB_CH // 128):
                    tt = (cbase // 128) + Mt
                    for n in range(2):
                        ob = (Mt * 2 + n) % 2

                        @E("tensor")
                        def _(t, ebuf=ebuf, Mt=Mt, n=n, ob=ob, a_done=a_done):
                            t.wait_ge(cV.sem, max(a_done, po_free.get(ob, 0)))
                            for kc in range(6):
                                ins = t.matmul(pO[ob][:], aT[:, kc, 128 * Mt:128 * (Mt + 1)],
                                               wb2[ebuf][:, kc, 512 * n:512 * (n + 1)],
                                               start=(kc == 0), stop=(kc == 5))
                            cT.tick(ins)
                        w2_done = cT.n
                        at_free[0] = cT.n

                        @E("scalar")
                        def _(s, e=e, tt=tt, ob=ob, w2_done=w2_done):
                            s.wait_ge(cT.sem, w2_done)
                            cS.tick(s.activation(yscr[:, ob, :], pO[ob][:], AF.Copy,
                                                 scale=SCB1[:, tt, e:e + 1]))

                        @E("vector")
                        def _(v, tt=tt, n=n, ob=ob):
                            v.wait_ge(cS.sem, cS.n)
                            dst = acc[:, tt, 512 * n:512 * (n + 1)]
                            cV.tick(v.tensor_tensor(dst, dst, yscr[:, ob, :], ALU.add))
                        po_free[ob] = cV.n
            exp_seq.append(cT.n)
        moe_done = cV.n

        for g2 in range(NPT // 2):
            @E("sync")
            def _(sy, pbase=pbase, g2=g2, moe_done=moe_done):
                sy.wait_ge(cV.sem, moe_done if g2 == 0 else cV.n)
                r0 = pbase + 256 * g2
                dma(sy, dS, xmb[:], d_xmid[r0:r0 + 256, :].rearrange("(T p) d -> p T d", p=128))
            xm_ld = dS.n

            @E("vector")
            def _(v, g2=g2, xm_ld=xm_ld):
                v.wait_ge(dS.sem, xm_ld)
                for T in range(2):
                    tt = 2 * g2 + T
                    v.tensor_tensor(obuf[:, T, :], acc[:, tt, :], gb_b[:], ALU.mult)
                    ins = v.tensor_tensor(obuf[:, T, :], obuf[:, T, :], xmb[:, T, :], ALU.add)
                cV.tick(ins)

            @E("sync")
            def _(sy, pbase=pbase, g2=g2):
                sy.wait_ge(cV.sem, cV.n)
                r0 = pbase + 256 * g2
                dma(sy, dS, dout[r0:r0 + 256, :].rearrange("(T p) d -> p T d", p=128), obuf[:])

    @E("sync")
    def _(sy):
        sy.wait_ge(dS.sem, dS.n)

    pb.close()
    es.close()
    return nc


def _host_constants():
    import ml_dtypes
    ident = np.eye(128, dtype=ml_dtypes.bfloat16)
    identf = np.eye(128, dtype=np.float32)
    mA = np.zeros((128, 128), np.float32)
    mAU = np.zeros((128, 128), np.float32)
    mT = np.zeros((128, 128), np.float32)
    for c in range(4):
        s = slice(32 * c, 32 * (c + 1))
        tril = np.tril(np.ones((32, 32), np.float32), -1)
        mA[s, s] = -tril
        mAU[s, s] = -tril.T
        mT[s, s] = np.triu(np.ones((32, 32), np.float32))
    return ident, identf, mA, mAU, mT


def make_in_maps(inp, taps=()):
    ident, identf, mA, mAU, mT = _host_constants()
    f32 = lambda a: np.ascontiguousarray(np.asarray(a, np.float32))
    conv_taps = np.stack([f32(inp["L0_attn_qc_w"]), f32(inp["L0_attn_kc_w"]),
                          f32(inp["L0_attn_vc_w"])])
    convt = np.ascontiguousarray(conv_taps.reshape(3, 8, 128, 4).transpose(2, 0, 1, 3))
    firs = np.ascontiguousarray(f32(inp["L0_attn_fir_s"]).reshape(1024, 3)
                                .reshape(8, 128, 3).transpose(1, 0, 2))
    firl = np.ascontiguousarray(f32(inp["L0_attn_fir_l"]).reshape(1024, 31)
                                .reshape(8, 128, 31).transpose(1, 0, 2))
    shared = {
        "Wq": f32(inp["L0_attn_Wq"]), "Wk": f32(inp["L0_attn_Wk"]),
        "Wv": f32(inp["L0_attn_Wv"]), "Wb": f32(inp["L0_attn_Wb"]),
        "Wo": f32(inp["L0_attn_Wo"]),
        "convtaps": convt, "firs": firs, "firl": firl,
        "gW1": f32(inp["L0_attn_gW1"]), "gb1": f32(inp["L0_attn_gb1"]).reshape(1, 512),
        "gW2": f32(inp["L0_attn_gW2"]), "gb2": f32(inp["L0_attn_gb2"]).reshape(1, 4),
        "temp": f32(inp["L0_attn_temp"]).reshape(1, 4),
        "floorv": f32(inp["L0_attn_eps_floor"]).reshape(1, 16),
        "onorm": f32(inp["L0_attn_onorm_w"]).reshape(1, DV),
        "wmod": f32(inp["L0_wmod"]),
        "gate_wT": np.ascontiguousarray(f32(inp["L0_ffn_gate_w"]).T),
        "e_bias": f32(inp["L0_ffn_e_bias"]).reshape(1, NE),
        "w1": np.ascontiguousarray(f32(inp["L0_ffn_w1"]).reshape(NE, 8, 128, EI)
                                   .transpose(0, 2, 1, 3).reshape(NE, 128, 8 * EI)),
        "w3": np.ascontiguousarray(f32(inp["L0_ffn_w3"]).reshape(NE, 8, 128, EI)
                                   .transpose(0, 2, 1, 3).reshape(NE, 128, 8 * EI)),
        "w2": np.ascontiguousarray(f32(inp["L0_ffn_w2"]).reshape(NE, 6, 128, HID)
                                   .transpose(0, 2, 1, 3).reshape(NE, 128, 6 * HID)),
        "identity": ident, "identityf": identf, "maskA": mA, "maskAU": mAU, "maskT": mT,
    }
    x = f32(inp["L0_attn_x"])
    cfull = f32(inp["L0_c"])
    in_maps = []
    for core in range(8):
        b, s = core // 2, core % 2
        xb = x[b] if s == 1 else np.ascontiguousarray(np.roll(x[b], HALF, axis=0))
        mask = np.ones((L,), np.float32)
        if s == 0:
            mask[:HALF] = 0.0
        m = dict(shared)
        m["x"] = xb
        m["tokmask"] = np.ascontiguousarray(mask.reshape(L // 128, 128).T)
        m["c_bT"] = np.ascontiguousarray(cfull[b].reshape(8, 128).T)
        in_maps.append(m)
    return in_maps


def kernel(**inp):
    nc = build_program()
    in_maps = make_in_maps(inp)
    res = run_bass_kernel_spmd(nc, in_maps, core_ids=list(range(8)))
    out = np.empty((4, L, HID), np.float32)
    for core in range(8):
        b, s = core // 2, core % 2
        out[b, HALF * s:HALF * (s + 1)] = res.results[core]["out"]
    return out


# version 03
# speedup vs baseline: 1.5451x; 1.1153x over previous
import numpy as np
import contextlib
import concourse.bass as bass
import concourse.mybir as mybir
from concourse.bass_utils import run_bass_kernel_spmd

F32 = mybir.dt.float32
BF16 = mybir.dt.bfloat16
AF = mybir.ActivationFunctionType
ALU = mybir.AluOpType
AX = mybir.AxisListType

HID, NH, DK, DV = 1024, 4, 256, 256
L, HALF = 4096, 2048
SEG = 256
NSEG = L // SEG
NT = SEG // 128
NE, EI = 32, 768
TOK = HALF
PB_TOK = 1024
PB_CH = 512


class Ctr:
    def __init__(self, sem):
        self.sem = sem
        self.n = 0

    def tick(self, instr, k=1):
        instr.then_inc(self.sem, k)
        self.n += k
        return self.n


def build_program(nseg=NSEG, nexp=NE, taps=(), own0=NSEG // 2, npass=2):
    nc = bass.Bass("TRN2", target_bir_lowering=False, detect_race_conditions=False)

    dx = nc.dram_tensor("x", [L, HID], F32, kind="ExternalInput").ap()
    dmask = nc.dram_tensor("tokmask", [128, L // 128], F32, kind="ExternalInput").ap()
    dWq = nc.dram_tensor("Wq", [HID, HID], F32, kind="ExternalInput").ap()
    dWk = nc.dram_tensor("Wk", [HID, HID], F32, kind="ExternalInput").ap()
    dWv = nc.dram_tensor("Wv", [HID, HID], F32, kind="ExternalInput").ap()
    dWb = nc.dram_tensor("Wb", [HID, NH], F32, kind="ExternalInput").ap()
    dWo = nc.dram_tensor("Wo", [HID, HID], F32, kind="ExternalInput").ap()
    dconv = nc.dram_tensor("convtaps", [128, 3, 8, 4], F32, kind="ExternalInput").ap()
    dfirs = nc.dram_tensor("firs", [128, 8, 3], F32, kind="ExternalInput").ap()
    dfirl = nc.dram_tensor("firl", [128, 8, 31], F32, kind="ExternalInput").ap()
    dgW1 = nc.dram_tensor("gW1", [5120, 512], F32, kind="ExternalInput").ap()
    dgb1 = nc.dram_tensor("gb1", [1, 512], F32, kind="ExternalInput").ap()
    dgW2 = nc.dram_tensor("gW2", [512, 4], F32, kind="ExternalInput").ap()
    dgb2 = nc.dram_tensor("gb2", [1, 4], F32, kind="ExternalInput").ap()
    dtemp = nc.dram_tensor("temp", [1, 4], F32, kind="ExternalInput").ap()
    dfloor = nc.dram_tensor("floorv", [1, 16], F32, kind="ExternalInput").ap()
    donorm = nc.dram_tensor("onorm", [1, DV], F32, kind="ExternalInput").ap()
    dcT = nc.dram_tensor("c_bT", [128, 8], F32, kind="ExternalInput").ap()
    dwmod = nc.dram_tensor("wmod", [HID, 6 * HID], F32, kind="ExternalInput").ap()
    dgatew = nc.dram_tensor("gate_wT", [HID, NE], F32, kind="ExternalInput").ap()
    debias = nc.dram_tensor("e_bias", [1, NE], F32, kind="ExternalInput").ap()
    dw1 = nc.dram_tensor("w1", [NE, 128, 8 * EI], F32, kind="ExternalInput").ap()
    dw3 = nc.dram_tensor("w3", [NE, 128, 8 * EI], F32, kind="ExternalInput").ap()
    dw2 = nc.dram_tensor("w2", [NE, 128, 6 * HID], F32, kind="ExternalInput").ap()
    dident = nc.dram_tensor("identity", [128, 128], BF16, kind="ExternalInput").ap()
    didentf = nc.dram_tensor("identityf", [128, 128], F32, kind="ExternalInput").ap()
    dmaskA = nc.dram_tensor("maskA", [128, 128], F32, kind="ExternalInput").ap()
    dmaskAU = nc.dram_tensor("maskAU", [128, 128], F32, kind="ExternalInput").ap()
    dmaskT = nc.dram_tensor("maskT", [128, 128], F32, kind="ExternalInput").ap()
    dout = nc.dram_tensor("out", [TOK, HID], F32, kind="ExternalOutput").ap()
    d_xmid = nc.dram_tensor("xmid_scr", [TOK, HID], F32, kind="Internal").ap()
    d_h2 = nc.dram_tensor("h2_scr", [TOK, HID], BF16, kind="Internal").ap()

    TAPSPEC = {
        "h": ([L, HID], BF16), "q": ([L, HID], BF16), "k": ([L, HID], BF16),
        "v": ([L, HID], BF16), "beta": ([L, NH], F32),
        "qn": ([L, HID], BF16), "kn": ([L, HID], BF16),
        "uw": ([L // 128, 128, NH, 512], BF16), "delta_o": ([L, HID], BF16),
        "fir_s": ([TOK, HID], BF16), "fir_l": ([TOK, HID], BF16),
        "stat": ([TOK // 128, 128, NH, 12], F32), "wgt": ([TOK // 128, 128, NH, 4], F32),
        "o_mix": ([TOK, HID], BF16), "x_mid": ([TOK, HID], F32), "h2": ([TOK, HID], BF16),
        "combine": ([TOK, NE], F32),
        "setup": ([8, 128, HID], F32),
    }
    dbg = {}
    for t in taps:
        shp, dt = TAPSPEC[t]
        dbg[t] = nc.dram_tensor("dbg_" + t, shp, dt, kind="ExternalOutput").ap()

    es = contextlib.ExitStack()

    def sbt(name, shape, dt=F32):
        return es.enter_context(nc.sbuf_tensor(name, shape, dt))

    def mksem(name):
        return Ctr(es.enter_context(nc.semaphore(name)))

    cT = mksem("cT")
    cV = mksem("cV")
    cS = mksem("cS")
    cG = mksem("cG")
    dS = mksem("dS")
    dGm = mksem("dGm")
    dA = mksem("dA")

    ident = sbt("ident", [128, 128], BF16)
    identf = sbt("identf", [128, 128], F32)
    maskA_s = sbt("maskA_s", [128, 128])
    maskAU_s = sbt("maskAU_s", [128, 128])
    maskT_s = sbt("maskT_s", [128, 128])
    convt = sbt("convt", [128, 3, 8, 4])
    firs_t = sbt("firs_t", [128, 8, 3])
    firl_t = sbt("firl_t", [128, 8, 31])
    mask_sb = sbt("mask_sb", [128, L // 128])
    Wb_s = sbt("Wb_s", [128, 8, NH], BF16)
    W1s = sbt("W1s", [12, 512], BF16)
    gW2h = sbt("gW2h", [128, 4, 4 * NH], BF16)
    gb2h = sbt("gb2h", [1, 4 * NH], BF16)
    gb1_b = sbt("gb1_b", [1, 512], BF16)
    floorb = sbt("floorb", [128, NH * 4])
    onormb = sbt("onormb", [128, DV])
    ebias_b = sbt("ebias_b", [128, NE])
    gatewT = sbt("gatewT", [128, 8, NE], BF16)
    invt_c = sbt("invt_c", [128, 4])
    ones1 = sbt("ones1", [1, 512], BF16)
    onec = sbt("onec", [128, 1], BF16)
    zrow = sbt("zrow", [1, 512], BF16)
    sa_b = sbt("sa_b", [128, HID], BF16)
    ca1_b = sbt("ca1_b", [128, HID], BF16)
    ga_b = sbt("ga_b", [128, HID], BF16)
    sbv_b = sbt("sbv_b", [128, HID], BF16)
    cb1_b = sbt("cb1_b", [128, HID], BF16)
    gb_b = sbt("gb_b", [128, HID], BF16)

    block = es.enter_context(nc.Block())

    def E(name):
        def deco(f):
            getattr(block, name)(f)
        return deco

    def dma(eng, ctr, out, in_, **kw):
        ctr.tick(eng.dma_start(out=out, in_=in_, **kw), 16)

    KT = lambda d: d.rearrange("(k p) e -> p k e", p=128)

    sc = contextlib.ExitStack()

    def sbl(name, shape, dt=F32):
        return sc.enter_context(nc.sbuf_tensor(name, shape, dt))

    modrow = sbl("modrow", [1, 6 * HID])
    modrow_b = sbl("modrow_b", [1, 6 * HID], BF16)
    c_sb = sbl("c_sb", [128, 8])
    silc = sbl("silc", [128, 8], BF16)
    wmbuf = sbl("wmbuf", [128, 2, 8, 512], BF16)
    g1stg = sbl("g1stg", [128, 2, 512], BF16)
    gW2f = sbl("gW2f", [128, 4, 4])
    rows_s = sbl("rows_s", [1, 64])
    rows_b = sbl("rows_b", [1, 64], BF16)
    onr = sbl("onr", [1, DV])
    onr_b = sbl("onr_b", [1, DV], BF16)
    ebr = sbl("ebr", [1, NE])
    ebr_b = sbl("ebr_b", [1, NE], BF16)
    clampt = sbl("clampt", [128, HID])
    clampt2 = sbl("clampt2", [128, HID])
    w1row = sbl("w1row", [1, 512], BF16)
    pSu = sc.enter_context(nc.psum_tensor("pSu", [128, 512], F32))
    pRu = sc.enter_context(nc.psum_tensor("pRu", [1, 512], F32))

    @E("sync")
    def _(sy):
        for o, i in [(ident[:], dident), (identf[:], didentf), (maskA_s[:], dmaskA),
                     (maskAU_s[:], dmaskAU), (maskT_s[:], dmaskT), (convt[:], dconv),
                     (firs_t[:], dfirs), (firl_t[:], dfirl), (mask_sb[:], dmask),
                     (c_sb[:], dcT), (gW2f[:], KT(dgW2)),
                     (rows_s[0:1, 0:4], dgb2), (rows_s[0:1, 4:8], dtemp),
                     (rows_s[0:1, 16:32], dfloor), (onr[:], donorm), (ebr[:], debias)]:
            dma(sy, dS, o, i)
    ld_smalls = dS.n

    @E("gpsimd")
    def _(g):
        dma(g, dGm, Wb_s[:], KT(dWb))
        dma(g, dGm, gatewT[:], KT(dgatew))
        dma(g, dGm, gb1_b[:], dgb1)
    n_pre_g = 3

    @E("vector")
    def _(v):
        v.memset(ones1[:], 1.0)
        v.memset(onec[:], 1.0)
        cV.tick(v.memset(zrow[:], 0.0))
    ones_done = cV.n

    @E("scalar")
    def _(s):
        s.wait_ge(dS.sem, ld_smalls)
        cS.tick(s.activation(silc[:], c_sb[:], AF.Silu))
    silc_done = cS.n

    mod_evac = {}
    for n in range(12):
        @E("gpsimd")
        def _(g, n=n):
            if n >= 2:
                g.wait_ge(cS.sem, mod_evac[n - 2])
            dma(g, dGm, wmbuf[:, n % 2], KT(dwmod[:, 512 * n:512 * (n + 1)]))

        @E("tensor")
        def _(t, n=n):
            t.wait_ge(dGm.sem, dGm.n)
            if n == 0:
                t.wait_ge(cS.sem, silc_done)
            else:
                t.wait_ge(cS.sem, mod_evac[n - 1])
            for k in range(8):
                ins = t.matmul(pRu[:], silc[:, k:k + 1], wmbuf[:, n % 2, k, :],
                               start=(k == 0), stop=(k == 7))
            cT.tick(ins)

        @E("scalar")
        def _(s, n=n):
            s.wait_ge(cT.sem, cT.n)
            cS.tick(s.copy(modrow[0:1, 512 * n:512 * (n + 1)], pRu[:]))
        mod_evac[n] = cS.n

    @E("vector")
    def _(v):
        v.wait_ge(cS.sem, mod_evac[11])
        cV.tick(v.tensor_copy(modrow_b[:], modrow[:]))
    modb_done = cV.n

    vec_tiles = [sa_b, ca1_b, ga_b, sbv_b, cb1_b, gb_b]
    for vi in range(6):
        for half in range(2):
            @E("tensor")
            def _(t, vi=vi, half=half):
                t.wait_ge(cV.sem, modb_done if (vi == 0 and half == 0) else cV.n)
                cT.tick(t.matmul(pSu[:], ones1[0:1, 0:128],
                                 modrow_b[0:1, HID * vi + 512 * half: HID * vi + 512 * (half + 1)],
                                 start=True, stop=True))

            @E("vector")
            def _(v, vi=vi, half=half):
                v.wait_ge(cT.sem, cT.n)
                dst = vec_tiles[vi][:, 512 * half:512 * (half + 1)]
                if vi in (1, 4):
                    cV.tick(v.tensor_scalar(dst, pSu[:], 1.0, None, ALU.add))
                else:
                    cV.tick(v.tensor_copy(dst, pSu[:]))

    @E("vector")
    def _(v):
        v.wait_ge(dS.sem, ld_smalls)
        cV.tick(v.tensor_copy(rows_b[0:1, 32:33], rows_s[0:1, 16:17]))

    for gt in (ga_b, gb_b):
        @E("scalar")
        def _(s, gt=gt):
            s.wait_ge(cV.sem, cV.n)
            cS.tick(s.activation(clampt2[:], gt[:], AF.Abs))

        @E("vector")
        def _(v, gt=gt):
            v.wait_ge(cS.sem, cS.n)
            v.tensor_scalar(clampt[:], gt[:], 0.0, None, ALU.is_ge)
            v.tensor_scalar(clampt[:], clampt[:], 2.0, -1.0, ALU.mult, ALU.add)
            v.tensor_scalar(clampt2[:], clampt2[:], 0.25, None, ALU.max)
            cV.tick(v.tensor_tensor(gt[:], clampt[:], clampt2[:], ALU.mult))

    @E("vector")
    def _(v):
        v.wait_ge(cV.sem, cV.n)
        v.tensor_scalar(rows_s[0:1, 8:12], rows_s[0:1, 4:8], 0.2, 10.0, ALU.max, ALU.min)
        v.drain()
        v.reciprocal(rows_s[0:1, 12:16], rows_s[0:1, 8:12])
        v.drain()
        v.tensor_scalar(rows_s[0:1, 16:32], rows_s[0:1, 16:32], 1e-7, 0.1, ALU.max, ALU.min)
        v.drain()
        v.tensor_copy(rows_b[0:1, 0:32], rows_s[0:1, 0:32])
        v.tensor_copy(onr_b[:], onr[:])
        cV.tick(v.tensor_copy(ebr_b[:], ebr[:]))

    for src, dst, wid in [(rows_b[0:1, 12:16], invt_c[:], 4),
                          (rows_b[0:1, 16:32], floorb[:], 16),
                          (ebr_b[0:1, :], ebias_b[:], NE),
                          (onr_b[0:1, :], onormb[:], DV)]:
        @E("tensor")
        def _(t, src=src, wid=wid):
            t.wait_ge(cV.sem, cV.n)
            cT.tick(t.matmul(pSu[:, 0:wid], ones1[0:1, 0:128], src, start=True, stop=True))

        @E("vector")
        def _(v, dst=dst, wid=wid):
            v.wait_ge(cT.sem, cT.n)
            cV.tick(v.tensor_copy(dst, pSu[:, 0:wid]))

    @E("vector")
    def _(v):
        v.wait_ge(cV.sem, cV.n)
        for h in range(NH):
            v.tensor_scalar(gW2h[:, :, 4 * h:4 * (h + 1)], gW2f[:], invt_c[:, h:h + 1], None, ALU.mult)
            ins = v.tensor_scalar(gb2h[0:1, 4 * h:4 * (h + 1)], rows_s[0:1, 0:4],
                                  rows_s[0:1, 12 + h:13 + h], None, ALU.mult)
        cV.tick(ins)

    w1s_evac = {}
    for j in range(24):
        @E("gpsimd")
        def _(g, j=j):
            if j >= 2:
                g.wait_ge(cS.sem, w1s_evac[j - 2])
            dma(g, dGm, g1stg[:, j % 2, :], dgW1[HID + 128 * j: HID + 128 * (j + 1), :])

        @E("tensor")
        def _(t, j=j):
            t.wait_ge(dGm.sem, dGm.n)
            if j >= 2:
                t.wait_ge(cS.sem, w1s_evac[j - 1])
            cT.tick(t.matmul(pRu[:], onec[:], g1stg[:, j % 2, :], start=(j % 2 == 0), stop=(j % 2 == 1)))

        if j % 2 == 1:
            @E("scalar")
            def _(s, j=j):
                s.wait_ge(cT.sem, cT.n)
                if j >= 3:
                    s.wait_ge(dS.sem, dS.n)
                grp = j // 2
                scale = (1.0 / 256.0) if (grp % 3 == 0) else 1.0
                cS.tick(s.activation(w1row[:], pRu[:], AF.Copy, scale=scale))

            @E("sync")
            def _(sy, j=j):
                sy.wait_ge(cS.sem, cS.n)
                dma(sy, dS, W1s[j // 2:j // 2 + 1, :], w1row[:])
        w1s_evac[j] = cS.n

    if "setup" in dbg:
        @E("gpsimd")
        def _(sy):
            sy.wait_ge(cS.sem, cS.n)
            sy.wait_ge(cV.sem, cV.n)
            for j, tsr in enumerate([sa_b, ca1_b, ga_b, sbv_b, cb1_b, gb_b]):
                dma(sy, dGm, dbg["setup"][j], tsr[:])
            dma(sy, dGm, dbg["setup"][6, 0:12, 0:512], W1s[:])
            dma(sy, dGm, dbg["setup"][7, :, 0:4], invt_c[:])
            dma(sy, dGm, dbg["setup"][7, :, 4:20], floorb[:])
            dma(sy, dGm, dbg["setup"][7, :, 20:276], onormb[:])
            dma(sy, dGm, dbg["setup"][7, 0:1, 276:340], rows_s[:])
            dma(sy, dGm, dbg["setup"][7, 1:2, 276:340], rows_b[:])
    w1s_dma = dS.n
    sc.close()

    pa = contextlib.ExitStack()

    def sba(name, shape, dt=F32):
        return pa.enter_context(nc.sbuf_tensor(name, shape, dt))

    x_sb = [sba("x_sb0", [128, NT, HID]), sba("x_sb1", [128, NT, HID])]
    h_bf = sba("h_bf", [128, NT, HID], BF16)
    hT = sba("hT", [128, 8, SEG], BF16)
    SCR = sba("SCR", [128, NT, HID])
    CONVB = sba("CONVB", [128, 8, SEG])
    tail3 = sba("tail3", [128, 3, 8, 3])
    qTc = sba("qTc", [128, 8, SEG], BF16)
    kTc = sba("kTc", [128, 8, SEG], BF16)
    vTc = [sba("vTc0", [128, 8, SEG], BF16), sba("vTc1", [128, 8, SEG], BF16)]
    TOKA = sba("TOKA", [128, NT, HID], BF16)
    TOKB = sba("TOKB", [128, NT, HID], BF16)
    v_tok = sba("v_tok", [128, NT, HID], BF16)
    kn_tok = sba("kn_tok", [128, NT, HID], BF16)
    TOKC = sba("TOKC", [128, NT, HID], BF16)
    Xu = sba("Xu", [128, NT, NH, 2 * DV], BF16)
    qnT = sba("qnT", [128, 8, SEG], BF16)
    knT = sba("knT", [128, 8, SEG], BF16)
    kbnT = sba("kbnT", [128, 8, SEG], BF16)
    wT_sb = sba("wT_sb", [128, 8, SEG], BF16)
    attnT = sba("attnT", [128, NT * NH, 128], BF16)
    S_bf = sba("S_bf", [128, NH, 2 * DV], BF16)
    TpP = sba("TpP", [128, NT * NH, 128], BF16)
    TpB5 = sba("TpB5", [128, NT * NH, 5, 128], BF16)
    beta_sb = sba("beta_sb", [128, NT, NH])
    statv = sba("statv", [128, NT, NH, 12])
    svT = sba("svT", [12, NH, SEG], BF16)
    hmG = sba("hmG", [128, 4, SEG], BF16)
    wgt = sba("wgt", [128, NT, NH, 4])
    h2_bf = sba("h2_bf", [128, NT, HID], BF16)
    stat_s = sba("stat_s", [128, 32])
    sqd = sba("sqd", [128, HID])
    vwin = sba("vwin", [128, 8, 286], BF16)
    W_ring = sba("W_ring", [128, 24, HID], BF16)
    ez_sb = sba("ez_sb", [128, 4])

    S_ps = [pa.enter_context(nc.psum_tensor(f"S_ps{h}", [128, 512], F32)) for h in range(NH)]
    pA = pa.enter_context(nc.psum_tensor("pA", [128, 512], F32))
    pB = pa.enter_context(nc.psum_tensor("pB", [128, 512], F32))

    @E("sync")
    def _(sy):
        dma(sy, dS, x_sb[0][:], dx[0:SEG, :].rearrange("(T p) d -> p T d", p=128))
    x_ld = {0: dS.n}

    @E("gpsimd")
    def _(g):
        dma(g, dGm, W_ring[:, 0:8, :], KT(dWq))
        dma(g, dGm, W_ring[:, 8:16, :], KT(dWk))
        dma(g, dGm, W_ring[:, 16:24, :], KT(dWv))
    w1_ld = {0: dGm.n}

    @E("vector")
    def _(v):
        v.memset(S_bf[:], 0.0)
        cV.tick(v.memset(tail3[:], 0.0))
    sbf_zero = cV.n

    setup_cS, setup_cV, setup_dS = cS.n, cV.n, dS.n

    @E("tensor")
    def _(t):
        t.wait_ge(cV.sem, max(ones_done, setup_cV))
        t.wait_ge(cS.sem, setup_cS)
        t.wait_ge(dS.sem, setup_dS)
        for h in range(NH):
            ins = t.matmul(S_ps[h][:], zrow[0:1, 0:128], zrow[0:1, 0:512],
                           start=True, stop=False, skip_group_check=True)
        cT.tick(ins)

    x_free = {0: (cV, 0), 1: (cV, 0)}
    scast = {h: (cS, 0) for h in range(NH)}
    vwin_guard = 0
    w4_ld = {}

    for i in range(nseg):
        buf = i % 2
        own = i >= own0
        orow = (i - own0) * SEG

        if i + 1 < nseg:
            @E("sync")
            def _(sy, i=i):
                ctr, val = x_free[(i + 1) % 2]
                if val:
                    sy.wait_ge(ctr.sem, val)
                dma(sy, dS, x_sb[(i + 1) % 2][:],
                    dx[SEG * (i + 1):SEG * (i + 2), :].rearrange("(T p) d -> p T d", p=128))
            x_ld[i + 1] = dS.n

        for T in range(NT):
            @E("vector")
            def _(v, i=i, T=T):
                v.wait_ge(dS.sem, x_ld[i])
                if i == 0 and T == 0:
                    v.wait_ge(cV.sem, sbf_zero)
                cV.tick(v.reduce_sum(stat_s[:, 0:1], x_sb[i % 2][:, T, :], axis=AX.X))

            @E("scalar")
            def _(s, i=i, T=T):
                s.wait_ge(dS.sem, x_ld[i])
                s.wait_ge(cV.sem, cV.n)
                cS.tick(s.activation(sqd[:], x_sb[i % 2][:, T, :], AF.Square, accum_out=stat_s[:, 1:2]))

            @E("vector")
            def _(v):
                v.wait_ge(cS.sem, cS.n)
                v.tensor_scalar(stat_s[:, 2:3], stat_s[:, 0:1], 1.0 / HID, None, ALU.mult)
                v.drain()
                v.tensor_tensor(stat_s[:, 3:4], stat_s[:, 2:3], stat_s[:, 2:3], ALU.mult)
                v.tensor_scalar(stat_s[:, 4:5], stat_s[:, 1:2], 1.0 / HID, 1e-5, ALU.mult, ALU.add)
                v.drain()
                cV.tick(v.tensor_tensor(stat_s[:, 4:5], stat_s[:, 4:5], stat_s[:, 3:4], ALU.subtract))

            @E("scalar")
            def _(s):
                s.wait_ge(cV.sem, cV.n)
                cS.tick(s.activation(stat_s[:, 5:6], stat_s[:, 4:5], AF.Sqrt))

            @E("vector")
            def _(v, i=i, T=T):
                v.wait_ge(cS.sem, cS.n)
                v.reciprocal(stat_s[:, 6:7], stat_s[:, 5:6])
                v.drain()
                v.tensor_scalar(sqd[:], x_sb[i % 2][:, T, :], stat_s[:, 2:3], stat_s[:, 6:7],
                                ALU.subtract, ALU.mult)
                v.tensor_tensor(sqd[:], sqd[:], ca1_b[:], ALU.mult)
                v.tensor_tensor(sqd[:], sqd[:], sa_b[:], ALU.add)
                cV.tick(v.tensor_scalar(h_bf[:, T, :], sqd[:],
                                        mask_sb[:, i * NT + T:i * NT + T + 1], None, ALU.mult))
        h_done = cV.n

        pTt_cm = nc.psum_tensor("pTt%d" % i, [128, 1024], BF16)
        with pTt_cm as pTt:
            for T in range(NT):
                @E("tensor")
                def _(t, T=T):
                    t.wait_ge(cV.sem, h_done if T == 0 else cV.n)
                    for k in range(8):
                        ins = t.transpose(pTt[:, 128 * k:128 * (k + 1)],
                                          h_bf[:, T, 128 * k:128 * (k + 1)], ident[:])
                    cT.tick(ins)

                @E("vector")
                def _(v, T=T):
                    v.wait_ge(cT.sem, cT.n)
                    cV.tick(v.tensor_copy(hT[:, :, 128 * T:128 * (T + 1)],
                                          pTt[:].rearrange("p (k t) -> p k t", t=128)))
            hT_done = cV.n

            @E("tensor")
            def _(t, i=i):
                t.wait_ge(cV.sem, hT_done)
                t.wait_ge(dGm.sem, w1_ld[i])
                t.wait_ge(cS.sem, scast[NH - 1][1])
                for T in range(NT):
                    for k in range(8):
                        ins = t.matmul(pB[:, 4 * T:4 * (T + 1)], hT[:, k, 128 * T:128 * (T + 1)],
                                       Wb_s[:, k, :], start=(T == 0 and k == 0),
                                       stop=(T == NT - 1 and k == 7))
                cT.tick(ins)

            @E("scalar")
            def _(s):
                s.wait_ge(cT.sem, cT.n)
                cS.tick(s.activation(beta_sb[:].rearrange("p T h -> p (T h)"),
                                     pB[:, 0:4 * NT], AF.Sigmoid))
            beta_done = cS.n

            q_on = own or (i == own0 - 1)
            tensors3 = [(qTc, 0, 0), (kTc, 8, 1), (vTc[buf], 16, 2)] if q_on else \
                       [(kTc, 8, 1), (vTc[buf], 16, 2)]
            evac_par = {0: cS.n, 1: cS.n}
            conv_done_m = 0
            for dst, base, wi in tensors3:
                for f in range(8):
                    @E("tensor")
                    def _(t, base=base, f=f, evp=evac_par[f % 2]):
                        if evp:
                            t.wait_ge(cS.sem, evp)
                        pbank = pA if f % 2 == 0 else pB
                        for k in range(8):
                            ins = t.matmul(pbank[:, 0:256],
                                           W_ring[:, base + k, 128 * f:128 * (f + 1)],
                                           hT[:, k, :], start=(k == 0), stop=(k == 7))
                        cT.tick(ins)

                    @E("scalar")
                    def _(s, f=f, cdm=conv_done_m):
                        s.wait_ge(cT.sem, cT.n)
                        if f == 0 and cdm:
                            s.wait_ge(cV.sem, cdm)
                        pbank = pA if f % 2 == 0 else pB
                        scr8 = SCR[:].rearrange("p T (k2 z) -> p (T k2) z", z=SEG)
                        cS.tick(s.copy(scr8[:, f, :], pbank[:, 0:256]))
                    evac_par[f % 2] = cS.n
                raw_done = cS.n

                @E("vector")
                def _(v, wi=wi, i=i, raw_done=raw_done):
                    v.wait_ge(cS.sem, raw_done)
                    scr8 = SCR[:].rearrange("p T (k2 z) -> p (T k2) z", z=SEG)
                    for dt_ in range(8):
                        tp = lambda j: convt[:, wi, dt_, j:j + 1]
                        v.tensor_scalar(CONVB[:, dt_, :], scr8[:, dt_, :], tp(3), None, ALU.mult)
                        for sh in (1, 2, 3):
                            v.scalar_tensor_tensor(CONVB[:, dt_, sh:SEG], scr8[:, dt_, 0:SEG - sh],
                                                   tp(3 - sh), CONVB[:, dt_, sh:SEG], ALU.mult, ALU.add)
                            v.scalar_tensor_tensor(CONVB[:, dt_, 0:sh], tail3[:, wi, dt_, 3 - sh:3],
                                                   tp(3 - sh), CONVB[:, dt_, 0:sh], ALU.mult, ALU.add)
                    cV.tick(v.tensor_copy(tail3[:, wi, :, :], scr8[:, :, SEG - 3:SEG]))
                conv_done_m = cV.n

                @E("scalar")
                def _(s, dst=dst, wi=wi):
                    s.wait_ge(cV.sem, cV.n)
                    if wi == 2 and vwin_guard:
                        s.wait_ge(cV.sem, vwin_guard)
                    cS.tick(s.activation(dst[:].rearrange("p k z -> p (k z)"),
                                         CONVB[:].rearrange("p k z -> p (k z)"), AF.Silu))
            qkv_done = cS.n

            if own:
                @E("gpsimd")
                def _(g):
                    g.wait_ge(cS.sem, qkv_done)
                    dma(g, dGm, W_ring[:, 0:8, :], KT(dWo))
                    dma(g, dGm, W_ring[:, 8:16, 0:512], KT(dgW1[0:HID, :]))
                    dma(g, dGm, W_ring[:, 16:24, 0:512], KT(dgW1[HID + 3 * DV * NH:5120, :]))
                w4_ld[i] = dGm.n

            srcdst = ([(qTc, TOKA)] if own else []) + [(kTc, TOKB), (vTc[buf], v_tok)]
            for src, dst2 in srcdst:
                for T in range(NT):
                    @E("tensor")
                    def _(t, src=src, T=T):
                        t.wait_ge(cS.sem, qkv_done)
                        t.wait_ge(cV.sem, cV.n)
                        for k in range(8):
                            ins = t.transpose(pTt[:, 128 * k:128 * (k + 1)],
                                              src[:, k, 128 * T:128 * (T + 1)], ident[:])
                        cT.tick(ins)

                    @E("vector")
                    def _(v, dst2=dst2, T=T):
                        v.wait_ge(cT.sem, cT.n)
                        cV.tick(v.tensor_copy(dst2[:, T, :],
                                              pTt[:].rearrange("p (k t) -> p k t", t=128)))
            tok_done = cV.n

            for T in range(NT):
                @E("scalar")
                def _(s, T=T, own=own):
                    s.wait_ge(cV.sem, tok_done)
                    for h in range(NH):
                        if own:
                            s.activation(sqd[:, 0:DV], TOKA[:, T, DV * h:DV * (h + 1)], AF.Square,
                                         accum_out=stat_s[:, 8 + h:9 + h])
                        ins = s.activation(sqd[:, 0:DV], TOKB[:, T, DV * h:DV * (h + 1)], AF.Square,
                                           accum_out=stat_s[:, 12 + h:13 + h])
                    cS.tick(ins)

                lo = 8 if own else 12

                @E("vector")
                def _(v, lo=lo):
                    v.wait_ge(cS.sem, cS.n)
                    cV.tick(v.tensor_scalar(stat_s[:, lo:16], stat_s[:, lo:16], 1e-6, None, ALU.add))

                @E("scalar")
                def _(s, lo=lo):
                    s.wait_ge(cV.sem, cV.n)
                    cS.tick(s.activation(stat_s[:, lo + 8:24], stat_s[:, lo:16], AF.Sqrt))

                @E("vector")
                def _(v, T=T, own=own, lo=lo):
                    v.wait_ge(cS.sem, cS.n)
                    v.reciprocal(stat_s[:, lo + 16:32], stat_s[:, lo + 8:24])
                    v.drain()
                    for h in range(NH):
                        if own:
                            v.tensor_scalar(TOKC[:, T, DV * h:DV * (h + 1)],
                                            TOKA[:, T, DV * h:DV * (h + 1)],
                                            stat_s[:, 24 + h:25 + h], None, ALU.mult)
                        v.tensor_scalar(kn_tok[:, T, DV * h:DV * (h + 1)],
                                        TOKB[:, T, DV * h:DV * (h + 1)],
                                        stat_s[:, 28 + h:29 + h], None, ALU.mult)
                        v.tensor_scalar(Xu[:, T, h, 0:DV], v_tok[:, T, DV * h:DV * (h + 1)],
                                        beta_sb[:, T, h:h + 1], None, ALU.mult)
                        ins = v.tensor_scalar(Xu[:, T, h, DV:2 * DV],
                                              kn_tok[:, T, DV * h:DV * (h + 1)],
                                              beta_sb[:, T, h:h + 1], None, ALU.mult)
                    cV.tick(ins)
            norm_done = cV.n

            tlist = ([("qn", qnT)] if own else []) + [("kn", knT), ("kb", kbnT)]
            for srcf, dstt in tlist:
                for T in range(NT):
                    @E("tensor")
                    def _(t, srcf=srcf, T=T):
                        t.wait_ge(cV.sem, cV.n)
                        for hh in range(8):
                            h, half = hh // 2, hh % 2
                            if srcf == "qn":
                                src_ap = TOKC[:, T, DV * h + 128 * half:DV * h + 128 * (half + 1)]
                            elif srcf == "kn":
                                src_ap = kn_tok[:, T, DV * h + 128 * half:DV * h + 128 * (half + 1)]
                            else:
                                src_ap = Xu[:, T, h, DV + 128 * half:DV + 128 * (half + 1)]
                            ins = t.transpose(pTt[:, 128 * hh:128 * (hh + 1)], src_ap, ident[:])
                        cT.tick(ins)

                    @E("vector")
                    def _(v, dstt=dstt, T=T):
                        v.wait_ge(cT.sem, cT.n)
                        cV.tick(v.tensor_copy(dstt[:, :, 128 * T:128 * (T + 1)],
                                              pTt[:].rearrange("p (k t) -> p k t", t=128)))
            tposed_done = cV.n

        with nc.psum_tensor("pTb%d" % i, [128, 512], BF16) as pTb:
            chains = [(T, h) for T in range(NT) for h in range(NH)]
            lvl_mm = {}
            lvl_evac = {}
            lvl_tp = {}
            app_add = {}
            prev_mask = 0
            for ci, (T, h) in enumerate(chains):
                @E("tensor")
                def _(t, T=T, h=h, own=own, pm=prev_mask):
                    t.wait_ge(cV.sem, pm if pm else cV.n)
                    for half in range(2):
                        t.matmul(pA[:, 0:128], kbnT[:, 2 * h + half, 128 * T:128 * (T + 1)],
                                 knT[:, 2 * h + half, 128 * T:128 * (T + 1)],
                                 start=(half == 0), stop=(half == 1))
                    for half in range(2):
                        ins = t.matmul(pA[:, 128:256], knT[:, 2 * h + half, 128 * T:128 * (T + 1)],
                                       kbnT[:, 2 * h + half, 128 * T:128 * (T + 1)],
                                       start=(half == 0), stop=(half == 1))
                    if own:
                        for half in range(2):
                            ins = t.matmul(pA[:, 256:384], knT[:, 2 * h + half, 128 * T:128 * (T + 1)],
                                           qnT[:, 2 * h + half, 128 * T:128 * (T + 1)],
                                           start=(half == 0), stop=(half == 1))
                    cT.tick(ins)

                @E("vector")
                def _(v, T=T, h=h, ci=ci, own=own):
                    v.wait_ge(cT.sem, cT.n)
                    v.tensor_tensor(TpP[:, ci, :], pA[:, 0:128], maskA_s[:], ALU.mult)
                    ins = v.tensor_tensor(TpB5[:, ci, 0, :], pA[:, 128:256], maskAU_s[:], ALU.mult)
                    if own:
                        ins = v.tensor_tensor(attnT[:, NH * T + h, :], pA[:, 256:384],
                                              maskT_s[:], ALU.mult)
                    cV.tick(ins)
                prev_mask = cV.n
                lvl_tp[ci] = cV.n

            for j in range(1, 5):
                for ci in range(len(chains)):
                    @E("tensor")
                    def _(t, ci=ci, j=j):
                        t.wait_ge(cV.sem, lvl_tp[ci])
                        ev = lvl_evac.get((j, ci - 4), 0) if ci >= 4 else lvl_evac.get((j - 1, ci + 4), 0)
                        if ev:
                            t.wait_ge(cS.sem, ev)
                        cT.tick(t.matmul(pB[:, 128 * (ci % 4):128 * (ci % 4 + 1)],
                                         TpB5[:, ci, j - 1, :], TpP[:, ci, :],
                                         start=True, stop=True, skip_group_check=True))
                    lvl_mm[(j, ci)] = cT.n

                    @E("scalar")
                    def _(s, ci=ci, j=j):
                        s.wait_ge(cT.sem, lvl_mm[(j, ci)])
                        cS.tick(s.copy(TpP[:, ci, :], pB[:, 128 * (ci % 4):128 * (ci % 4 + 1)]))
                    lvl_evac[(j, ci)] = cS.n

                    @E("vector")
                    def _(v, ci=ci, j=j):
                        v.wait_ge(cS.sem, lvl_evac[(j, ci)])
                        cV.tick(v.transpose(TpB5[:, ci, j, :], TpP[:, ci, :]))
                    lvl_tp[ci] = cV.n

            powers_done = cV.n
            bank_free = {0: 0, 1: 0}
            for j in range(4, -1, -1):
                for ci, (T, h) in enumerate(chains):
                    bank = pA if ci % 2 == 0 else pB
                    @E("tensor")
                    def _(t, ci=ci, j=j, T=T, h=h, bank=bank, bf=bank_free[ci % 2]):
                        t.wait_ge(cV.sem, max(powers_done, lvl_tp[ci], bf,
                                              app_add.get((j + 1, ci), 0)))
                        cT.tick(t.matmul(bank[:], TpB5[:, ci, j, :], Xu[:, T, h, :],
                                         start=True, stop=True))
                    mmv = cT.n

                    @E("vector")
                    def _(v, ci=ci, T=T, h=h, bank=bank, mmv=mmv):
                        v.wait_ge(cT.sem, mmv)
                        cV.tick(v.tensor_tensor(Xu[:, T, h, :], Xu[:, T, h, :], bank[:], ALU.add))
                    app_add[(j, ci)] = cV.n
                    bank_free[ci % 2] = cV.n
            tchain_done = cV.n

            for T in range(NT):
                for grp in range(2):
                    @E("tensor")
                    def _(t, T=T, grp=grp):
                        t.wait_ge(cV.sem, cV.n)
                        for j in range(4):
                            hh = 4 * grp + j
                            h, half = hh // 2, hh % 2
                            ins = t.transpose(pTb[:, 128 * j:128 * (j + 1)],
                                              Xu[:, T, h, DV + 128 * half:DV + 128 * (half + 1)],
                                              ident[:])
                        cT.tick(ins)

                    @E("vector")
                    def _(v, T=T, grp=grp):
                        v.wait_ge(cT.sem, cT.n)
                        cV.tick(v.tensor_copy(wT_sb[:, 4 * grp:4 * (grp + 1), 128 * T:128 * (T + 1)],
                                              pTb[:, 0:512].rearrange("p (k t) -> p k t", t=128)))
            wT_done = cV.n

        with nc.psum_tensor("wSa%d" % i, [128, 512], F32) as wSa, \
             nc.psum_tensor("wSb%d" % i, [128, 512], F32) as wSb:
            for T in range(NT):
                for c in range(4):
                    for h in range(NH):
                        wbank = wSa if h % 2 == 0 else wSb
                        obank = pA if h < 2 else pB
                        ocol = 256 * (h % 2)
                        r0, r1 = 32 * c, 32 * (c + 1)

                        @E("tensor")
                        def _(t, T=T, c=c, h=h, wbank=wbank, obank=obank, ocol=ocol,
                              r0=r0, r1=r1, own=own):
                            t.wait_ge(cS.sem, max(scast[h][1], cS.n if (c == 0 and h == 0) else 0))
                            if T == 0 and c == 0 and h == 0:
                                t.wait_ge(cV.sem, wT_done)
                            if h >= 2:
                                t.wait_ge(cV.sem, cV.n)
                            for half in range(2):
                                ins = t.matmul(wbank[r0:r1, 0:256],
                                               wT_sb[:, 2 * h + half, 128 * T + r0:128 * T + r1],
                                               S_bf[:, h, 256 * half:256 * (half + 1)],
                                               start=(half == 0), stop=(half == 1),
                                               tile_position=(0, r0), skip_group_check=True)
                            if own:
                                for half in range(2):
                                    ins = t.matmul(obank[r0:r1, ocol:ocol + 256],
                                                   qnT[:, 2 * h + half, 128 * T + r0:128 * T + r1],
                                                   S_bf[:, h, 256 * half:256 * (half + 1)],
                                                   start=(half == 0 and h % 2 == 0), stop=False,
                                                   tile_position=(0, r0), skip_group_check=True)
                            cT.tick(ins)

                        @E("vector")
                        def _(v, T=T, h=h, wbank=wbank, r0=r0, r1=r1):
                            v.wait_ge(cT.sem, cT.n)
                            cV.tick(v.tensor_tensor(Xu[r0:r1, T, h, 0:DV], Xu[r0:r1, T, h, 0:DV],
                                                    wbank[r0:r1, 0:256], ALU.subtract))

                        @E("tensor")
                        def _(t, T=T, c=c, h=h, r0=r0, r1=r1):
                            t.wait_ge(cV.sem, cV.n)
                            for mc in range(2):
                                ins = t.matmul(S_ps[h][:, 256 * mc:256 * (mc + 1)],
                                               kn_tok[r0:r1, T, DV * h + 128 * mc:DV * h + 128 * (mc + 1)],
                                               Xu[r0:r1, T, h, 0:DV],
                                               start=False, stop=False,
                                               tile_position=(r0, 0), skip_group_check=True)
                            cT.tick(ins)

                        @E("scalar")
                        def _(s, h=h):
                            s.wait_ge(cT.sem, cT.n)
                            cS.tick(s.copy(S_bf[:, h, :], S_ps[h][:]))
                        scast[h] = (cS, cS.n)

                if own:
                    for h in range(NH):
                        obank = pA if h < 2 else pB
                        ocol = 256 * (h % 2)

                        @E("tensor")
                        def _(t, T=T, h=h, obank=obank, ocol=ocol):
                            t.wait_ge(cV.sem, cV.n)
                            cT.tick(t.matmul(obank[:, ocol:ocol + 256], attnT[:, NH * T + h, :],
                                             Xu[:, T, h, 0:DV], start=False, stop=(h % 2 == 1),
                                             skip_group_check=True))
                        if h % 2 == 1:
                            @E("scalar")
                            def _(s, T=T, h=h, obank=obank):
                                s.wait_ge(cT.sem, cT.n)
                                cS.tick(s.copy(TOKA[:, T, 512 * (h // 2):512 * (h // 2 + 1)], obank[:]))
            scan_done = cS.n
        if not own:
            x_free[buf] = (cS, scan_done)

        if taps:
            @E("sync")
            def _(sy, i=i, h_done=h_done, tok_done=tok_done, beta_done=beta_done,
                  norm_done=norm_done, scan_done=scan_done, own=own):
                rr = lambda d: d.rearrange("(T p) d -> p T d", p=128)
                tap_rows = slice(SEG * i, SEG * (i + 1))
                if "h" in dbg:
                    sy.wait_ge(cV.sem, h_done)
                    dma(sy, dS, rr(dbg["h"][tap_rows, :]), h_bf[:])
                if "k" in dbg:
                    sy.wait_ge(cV.sem, tok_done)
                    dma(sy, dS, rr(dbg["k"][tap_rows, :]), TOKB[:])
                if "v" in dbg:
                    sy.wait_ge(cV.sem, tok_done)
                    dma(sy, dS, rr(dbg["v"][tap_rows, :]), v_tok[:])
                if "q" in dbg and own:
                    sy.wait_ge(cV.sem, tok_done)
                    dma(sy, dS, rr(dbg["q"][tap_rows, :]), TOKA[:])
                if "beta" in dbg:
                    sy.wait_ge(cS.sem, beta_done)
                    dma(sy, dS, dbg["beta"][tap_rows, :].rearrange("(T p) h -> p T h", p=128), beta_sb[:])
                if "kn" in dbg:
                    sy.wait_ge(cV.sem, norm_done)
                    dma(sy, dS, rr(dbg["kn"][tap_rows, :]), kn_tok[:])
                if "qn" in dbg and own:
                    sy.wait_ge(cV.sem, norm_done)
                    dma(sy, dS, rr(dbg["qn"][tap_rows, :]), TOKC[:])
                if "uw" in dbg:
                    sy.wait_ge(cS.sem, scan_done)
                    dma(sy, dS, dbg["uw"][2 * i:2 * i + 2].rearrange("T p h d -> p T h d"), Xu[:])
                if "delta_o" in dbg and own:
                    sy.wait_ge(cS.sem, scan_done)
                    dma(sy, dS, rr(dbg["delta_o"][tap_rows, :]), TOKA[:])

        if not own:
            w1_ld[i + 1] = w1_ld[i]
            continue

        with nc.psum_tensor("pTg%d" % i, [128, 1024], BF16) as pTg, \
             nc.psum_tensor("pCg%d" % i, [128, 512], F32) as pCg:
            for T in range(NT):
                @E("tensor")
                def _(t, T=T, scan_done=scan_done):
                    t.wait_ge(cS.sem, scan_done)
                    t.wait_ge(cV.sem, cV.n)
                    for hh in range(8):
                        h, half = hh // 2, hh % 2
                        ins = t.transpose(pTg[:, 128 * hh:128 * (hh + 1)],
                                          TOKA[:, T, DV * h + 128 * half:DV * h + 128 * (half + 1)],
                                          ident[:])
                    cT.tick(ins)

                @E("vector")
                def _(v, T=T):
                    v.wait_ge(cT.sem, cT.n)
                    cV.tick(v.tensor_copy(qTc[:, :, 128 * T:128 * (T + 1)],
                                          pTg[:].rearrange("p (k t) -> p k t", t=128)))
            delot_done = cV.n

            @E("vector")
            def _(v, i=i, buf=buf, qkv_done=qkv_done):
                v.wait_ge(cS.sem, qkv_done)
                v.tensor_copy(vwin[:, :, 0:30], vTc[1 - buf][:, :, SEG - 30:SEG])
                cV.tick(v.tensor_copy(vwin[:, :, 30:286], vTc[buf][:]))
            vwin_done = cV.n
            vwin_guard = vwin_done

            for ntap, ftap, dstT, dstTok in [(3, firs_t, knT, TOKB), (31, firl_t, kbnT, TOKC)]:
                @E("vector")
                def _(v, ntap=ntap, ftap=ftap):
                    v.wait_ge(cV.sem, vwin_done)
                    v.wait_ge(cS.sem, cS.n)
                    for dt_ in range(8):
                        base = 30 - (ntap - 1)
                        v.tensor_scalar(CONVB[:, dt_, :], vwin[:, dt_, base:base + SEG],
                                        ftap[:, dt_, 0:1], None, ALU.mult)
                        for j in range(1, ntap):
                            ins = v.scalar_tensor_tensor(CONVB[:, dt_, :],
                                                         vwin[:, dt_, base + j:base + j + SEG],
                                                         ftap[:, dt_, j:j + 1], CONVB[:, dt_, :],
                                                         ALU.mult, ALU.add)
                    cV.tick(ins)

                @E("scalar")
                def _(s, dstT=dstT):
                    s.wait_ge(cV.sem, cV.n)
                    cS.tick(s.copy(dstT[:].rearrange("p k z -> p (k z)"),
                                   CONVB[:].rearrange("p k z -> p (k z)")))
                fcast = cS.n

                for T in range(NT):
                    @E("tensor")
                    def _(t, dstT=dstT, T=T, fcast=fcast):
                        t.wait_ge(cS.sem, fcast)
                        t.wait_ge(cV.sem, cV.n)
                        for k in range(8):
                            ins = t.transpose(pTg[:, 128 * k:128 * (k + 1)],
                                              dstT[:, k, 128 * T:128 * (T + 1)], ident[:])
                        cT.tick(ins)

                    @E("vector")
                    def _(v, dstTok=dstTok, T=T):
                        v.wait_ge(cT.sem, cT.n)
                        cV.tick(v.tensor_copy(dstTok[:, T, :],
                                              pTg[:].rearrange("p (k t) -> p k t", t=128)))
            fir_done = cV.n

            for T in range(NT):
                for h in range(NH):
                    @E("vector")
                    def _(v, T=T, h=h, fir_done=fir_done):
                        v.wait_ge(cV.sem, fir_done if (T == 0 and h == 0) else cV.n)
                        for bi, bsrc in enumerate([TOKB, TOKC, TOKA, v_tok]):
                            sl = bsrc[:, T, DV * h:DV * (h + 1)]
                            v.reduce_sum(statv[:, T, h, 3 * bi:3 * bi + 1], sl, axis=AX.X)
                            ins = v.tensor_reduce(statv[:, T, h, 3 * bi + 2:3 * bi + 3], sl,
                                                  axis=AX.X, op=ALU.max)
                        cV.tick(ins)

                    @E("scalar")
                    def _(s, T=T, h=h):
                        s.wait_ge(cV.sem, cV.n)
                        for bi, bsrc in enumerate([TOKB, TOKC, TOKA, v_tok]):
                            ins = s.activation(sqd[:, 0:DV], bsrc[:, T, DV * h:DV * (h + 1)],
                                               AF.Square, accum_out=stat_s[:, 8 + bi:9 + bi])
                        cS.tick(ins)

                    @E("vector")
                    def _(v):
                        v.wait_ge(cS.sem, cS.n)
                        cV.tick(v.tensor_scalar(stat_s[:, 12:16], stat_s[:, 8:12], 1.0 / DV, 1e-8,
                                                ALU.mult, ALU.max))

                    @E("scalar")
                    def _(s):
                        s.wait_ge(cV.sem, cV.n)
                        cS.tick(s.activation(stat_s[:, 16:20], stat_s[:, 12:16], AF.Sqrt))

                    @E("vector")
                    def _(v, T=T, h=h):
                        v.wait_ge(cS.sem, cS.n)
                        cV.tick(v.tensor_copy(statv[:, T, h, 1:11:3], stat_s[:, 16:20]))
            stats_done = cV.n

            for T in range(NT):
                for h in range(NH):
                    @E("tensor")
                    def _(t, T=T, h=h):
                        t.wait_ge(cV.sem, cV.n)
                        cT.tick(t.transpose(pCg[0:12, 0:128], statv[:, T, h, :], identf[:]))

                    @E("vector")
                    def _(v, T=T, h=h):
                        v.wait_ge(cT.sem, cT.n)
                        cV.tick(v.tensor_copy(svT[0:12, h, 128 * T:128 * (T + 1)], pCg[0:12, 0:128]))
            svt_done = cV.n

            for h in range(NH):
                for Mc in range(4):
                    @E("tensor")
                    def _(t, i=i, h=h, Mc=Mc, svt_done=svt_done):
                        t.wait_ge(cV.sem, svt_done if (h == 0 and Mc == 0) else cV.n)
                        if h == 0 and Mc == 0:
                            t.wait_ge(dGm.sem, w4_ld[i])
                            t.wait_ge(dS.sem, w1s_dma)
                        t.wait_ge(cS.sem, cS.n)
                        pg = pA if Mc % 2 == 0 else pB
                        t.matmul(pg[:, 0:256], W_ring[:, 8 + 0, 128 * Mc:128 * (Mc + 1)], hT[:, 0, :],
                                 start=True, stop=False)
                        for k in range(1, 8):
                            t.matmul(pg[:, 0:256], W_ring[:, 8 + k, 128 * Mc:128 * (Mc + 1)],
                                     hT[:, k, :], start=False, stop=False)
                        for bi, bT in enumerate([knT, kbnT, qTc, vTc[i % 2]]):
                            for half in range(2):
                                t.matmul(pg[:, 0:256], W_ring[:, 16 + 2 * bi + half, 128 * Mc:128 * (Mc + 1)],
                                         bT[:, 2 * h + half, :], start=False, stop=False)
                        t.matmul(pg[:, 0:256], W1s[0:12, 128 * Mc:128 * (Mc + 1)], svT[0:12, h, :],
                                 start=False, stop=False)
                        cT.tick(t.matmul(pg[:, 0:256], gb1_b[0:1, 128 * Mc:128 * (Mc + 1)],
                                         ones1[0:1, 0:SEG], start=False, stop=True))

                    @E("scalar")
                    def _(s, Mc=Mc):
                        s.wait_ge(cT.sem, cT.n)
                        pg = pA if Mc % 2 == 0 else pB
                        cS.tick(s.activation(hmG[:, Mc, :], pg[:, 0:256], AF.Gelu))
                gelu_done = cS.n

                for T in range(NT):
                    @E("tensor")
                    def _(t, h=h, T=T, gelu_done=gelu_done):
                        t.wait_ge(cS.sem, gelu_done)
                        t.wait_ge(cV.sem, cV.n)
                        for Mc in range(4):
                            t.matmul(pCg[:, 4 * T:4 * (T + 1)], hmG[:, Mc, 128 * T:128 * (T + 1)],
                                     gW2h[:, Mc, 4 * h:4 * (h + 1)], start=(Mc == 0), stop=False,
                                     skip_group_check=True)
                        cT.tick(t.matmul(pCg[:, 4 * T:4 * (T + 1)], ones1[0:1, 128 * T:128 * (T + 1)],
                                         gb2h[0:1, 4 * h:4 * (h + 1)], start=False, stop=True,
                                         skip_group_check=True))

                    @E("vector")
                    def _(v, T=T):
                        v.wait_ge(cT.sem, cT.n)
                        cV.tick(v.tensor_reduce(stat_s[:, 0:1], pCg[:, 4 * T:4 * (T + 1)],
                                                axis=AX.X, op=ALU.max, negate=True))

                    @E("scalar")
                    def _(s, T=T):
                        s.wait_ge(cV.sem, cV.n)
                        cS.tick(s.activation(ez_sb[:], pCg[:, 4 * T:4 * (T + 1)], AF.Exp,
                                             bias=stat_s[:, 0:1], accum_out=stat_s[:, 1:2]))

                    @E("vector")
                    def _(v, T=T, h=h):
                        v.wait_ge(cS.sem, cS.n)
                        v.reciprocal(stat_s[:, 2:3], stat_s[:, 1:2])
                        v.drain()
                        v.tensor_scalar(ez_sb[:], ez_sb[:], stat_s[:, 2:3], None, ALU.mult)
                        v.drain()
                        v.tensor_tensor(ez_sb[:], ez_sb[:], floorb[:, 4 * h:4 * (h + 1)], ALU.max)
                        v.drain()
                        v.reduce_sum(stat_s[:, 3:4], ez_sb[:], axis=AX.X)
                        v.drain()
                        v.reciprocal(stat_s[:, 4:5], stat_s[:, 3:4])
                        v.drain()
                        cV.tick(v.tensor_scalar(wgt[:, T, h, :], ez_sb[:], stat_s[:, 4:5],
                                                None, ALU.mult))
            gate_done = cV.n

            for T in range(NT):
                for h in range(NH):
                    @E("vector")
                    def _(v, T=T, h=h, gate_done=gate_done):
                        v.wait_ge(cV.sem, gate_done if (T == 0 and h == 0) else cV.n)
                        dst = SCR[:, T, DV * h:DV * (h + 1)]
                        v.tensor_scalar(dst, TOKB[:, T, DV * h:DV * (h + 1)], wgt[:, T, h, 0:1],
                                        None, ALU.mult)
                        v.scalar_tensor_tensor(dst, TOKC[:, T, DV * h:DV * (h + 1)], wgt[:, T, h, 1:2],
                                               dst, ALU.mult, ALU.add)
                        v.scalar_tensor_tensor(dst, TOKA[:, T, DV * h:DV * (h + 1)], wgt[:, T, h, 2:3],
                                               dst, ALU.mult, ALU.add)
                        cV.tick(v.scalar_tensor_tensor(dst, v_tok[:, T, DV * h:DV * (h + 1)],
                                                       wgt[:, T, h, 3:4], dst, ALU.mult, ALU.add))

                    @E("scalar")
                    def _(s, T=T, h=h):
                        s.wait_ge(cV.sem, cV.n)
                        cS.tick(s.activation(sqd[:, 0:DV], SCR[:, T, DV * h:DV * (h + 1)], AF.Square,
                                             accum_out=stat_s[:, 8:9]))

                    @E("vector")
                    def _(v):
                        v.wait_ge(cS.sem, cS.n)
                        cV.tick(v.tensor_scalar(stat_s[:, 9:10], stat_s[:, 8:9], 1.0 / DV, 1e-5,
                                                ALU.mult, ALU.add))

                    @E("scalar")
                    def _(s):
                        s.wait_ge(cV.sem, cV.n)
                        cS.tick(s.activation(stat_s[:, 10:11], stat_s[:, 9:10], AF.Sqrt))

                    @E("vector")
                    def _(v, T=T, h=h):
                        v.wait_ge(cS.sem, cS.n)
                        v.reciprocal(stat_s[:, 11:12], stat_s[:, 10:11])
                        v.drain()
                        cV.tick(v.scalar_tensor_tensor(h_bf[:, T, DV * h:DV * (h + 1)],
                                                       SCR[:, T, DV * h:DV * (h + 1)],
                                                       stat_s[:, 11:12], onormb[:],
                                                       ALU.mult, ALU.mult))
            mix_done = cV.n

            for T in range(NT):
                @E("tensor")
                def _(t, T=T, mix_done=mix_done):
                    t.wait_ge(cV.sem, mix_done if T == 0 else cV.n)
                    for k in range(8):
                        ins = t.transpose(pTg[:, 128 * k:128 * (k + 1)],
                                          h_bf[:, T, 128 * k:128 * (k + 1)], ident[:])
                    cT.tick(ins)

                @E("vector")
                def _(v, T=T):
                    v.wait_ge(cT.sem, cT.n)
                    cV.tick(v.tensor_copy(kTc[:, :, 128 * T:128 * (T + 1)],
                                          pTg[:].rearrange("p (k t) -> p k t", t=128)))
            oT_done = cV.n

            for T in range(NT):
                for n in range(2):
                    @E("tensor")
                    def _(t, T=T, n=n, oT_done=oT_done):
                        t.wait_ge(cV.sem, oT_done if (T == 0 and n == 0) else cV.n)
                        pg = pA if n == 0 else pB
                        for k in range(8):
                            ins = t.matmul(pg[:], kTc[:, k, 128 * T:128 * (T + 1)],
                                           W_ring[:, 0 + k, 512 * n:512 * (n + 1)],
                                           start=(k == 0), stop=(k == 7))
                        cT.tick(ins)

                    @E("vector")
                    def _(v, T=T, n=n, i=i):
                        v.wait_ge(cT.sem, cT.n)
                        pg = pA if n == 0 else pB
                        dst = SCR[:, T, 512 * n:512 * (n + 1)]
                        v.tensor_tensor(dst, pg[:], ga_b[:, 512 * n:512 * (n + 1)], ALU.mult)
                        cV.tick(v.tensor_tensor(dst, dst, x_sb[i % 2][:, T, 512 * n:512 * (n + 1)],
                                                ALU.add))
            xmid_done = cV.n
            x_free[buf] = (cV, xmid_done)

            for T in range(NT):
                @E("vector")
                def _(v, T=T, xmid_done=xmid_done):
                    v.wait_ge(cV.sem, xmid_done)
                    cV.tick(v.reduce_sum(stat_s[:, 0:1], SCR[:, T, :], axis=AX.X))

                @E("scalar")
                def _(s, T=T):
                    s.wait_ge(cV.sem, cV.n)
                    cS.tick(s.activation(sqd[:], SCR[:, T, :], AF.Square, accum_out=stat_s[:, 1:2]))

                @E("vector")
                def _(v):
                    v.wait_ge(cS.sem, cS.n)
                    v.tensor_scalar(stat_s[:, 2:3], stat_s[:, 0:1], 1.0 / HID, None, ALU.mult)
                    v.tensor_tensor(stat_s[:, 3:4], stat_s[:, 2:3], stat_s[:, 2:3], ALU.mult)
                    v.tensor_scalar(stat_s[:, 4:5], stat_s[:, 1:2], 1.0 / HID, 1e-5, ALU.mult, ALU.add)
                    cV.tick(v.tensor_tensor(stat_s[:, 4:5], stat_s[:, 4:5], stat_s[:, 3:4], ALU.subtract))

                @E("scalar")
                def _(s):
                    s.wait_ge(cV.sem, cV.n)
                    cS.tick(s.activation(stat_s[:, 5:6], stat_s[:, 4:5], AF.Sqrt))

                @E("vector")
                def _(v, T=T):
                    v.wait_ge(cS.sem, cS.n)
                    v.reciprocal(stat_s[:, 6:7], stat_s[:, 5:6])
                    v.drain()
                    v.tensor_scalar(sqd[:], SCR[:, T, :], stat_s[:, 2:3], stat_s[:, 6:7],
                                    ALU.subtract, ALU.mult)
                    v.tensor_tensor(sqd[:], sqd[:], cb1_b[:], ALU.mult)
                    cV.tick(v.tensor_tensor(h2_bf[:, T, :], sqd[:], sbv_b[:], ALU.add))
            h2_done = cV.n

            @E("sync")
            def _(sy, orow=orow, h2_done=h2_done):
                sy.wait_ge(cV.sem, h2_done)
                dma(sy, dS, d_xmid[orow:orow + SEG, :].rearrange("(T p) d -> p T d", p=128), SCR[:])
                dma(sy, dS, d_h2[orow:orow + SEG, :].rearrange("(T p) d -> p T d", p=128), h2_bf[:])

            if taps:
                @E("sync")
                def _(sy, orow=orow, i=i, fir_done=fir_done, stats_done=stats_done,
                      gate_done=gate_done, mix_done=mix_done, xmid_done=xmid_done, h2_done=h2_done):
                    rr = lambda d: d.rearrange("(T p) d -> p T d", p=128)
                    if "fir_s" in dbg:
                        sy.wait_ge(cV.sem, fir_done)
                        dma(sy, dS, rr(dbg["fir_s"][orow:orow + SEG, :]), TOKB[:])
                    if "fir_l" in dbg:
                        sy.wait_ge(cV.sem, fir_done)
                        dma(sy, dS, rr(dbg["fir_l"][orow:orow + SEG, :]), TOKC[:])
                    if "stat" in dbg:
                        sy.wait_ge(cV.sem, stats_done)
                        dma(sy, dS, dbg["stat"][2 * (i - own0):2 * (i - own0) + 2]
                            .rearrange("T p h d -> p T h d"), statv[:])
                    if "wgt" in dbg:
                        sy.wait_ge(cV.sem, gate_done)
                        dma(sy, dS, dbg["wgt"][2 * (i - own0):2 * (i - own0) + 2]
                            .rearrange("T p h e -> p T h e"), wgt[:])
                    if "o_mix" in dbg:
                        sy.wait_ge(cV.sem, mix_done)
                        dma(sy, dS, rr(dbg["o_mix"][orow:orow + SEG, :]), h_bf[:])
                    if "x_mid" in dbg:
                        sy.wait_ge(cV.sem, xmid_done)
                        dma(sy, dS, rr(dbg["x_mid"][orow:orow + SEG, :]), SCR[:])
                    if "h2" in dbg:
                        sy.wait_ge(cV.sem, h2_done)
                        dma(sy, dS, rr(dbg["h2"][orow:orow + SEG, :]), h2_bf[:])

            if i + 1 < nseg:
                @E("gpsimd")
                def _(g):
                    g.wait_ge(cT.sem, cT.n)
                    dma(g, dGm, W_ring[:, 0:8, :], KT(dWq))
                    dma(g, dGm, W_ring[:, 8:16, :], KT(dWk))
                    dma(g, dGm, W_ring[:, 16:24, :], KT(dWv))
                w1_ld[i + 1] = dGm.n

    paT, paV, paS, paG, paD, paGm = cT.n, cV.n, cS.n, cG.n, dS.n, dGm.n
    pa.close()

    pb = contextlib.ExitStack()

    def sbb(name, shape, dt=F32):
        return pb.enter_context(nc.sbuf_tensor(name, shape, dt))

    h2T = sbb("h2T", [128, 8, PB_TOK], BF16)
    acc = sbb("acc", [128, PB_TOK // 128, HID])
    wb1 = [sbb("wb1_0", [128, 8, EI], BF16), sbb("wb1_1", [128, 8, EI], BF16)]
    wb3 = [sbb("wb3_0", [128, 8, EI], BF16), sbb("wb3_1", [128, 8, EI], BF16)]
    wb2 = [sbb("wb2_0", [128, 6, HID], BF16), sbb("wb2_1", [128, 6, HID], BF16)]
    SCB1 = sbb("SCB1", [128, PB_TOK // 128, NE])
    SCB2 = sbb("SCB2", [128, PB_TOK // 128, NE])
    mrscr = sbb("mrscr", [128, NE])
    mx8 = sbb("mx8", [128, 8])
    aT = sbb("aT", [128, 6, PB_CH], BF16)
    gtmp = sbb("gtmp", [128, 2, PB_CH], BF16)
    yscr = sbb("yscr", [128, 2, 512])
    xmb = sbb("xmb", [128, 2, HID])
    obuf = sbb("obuf", [128, 2, HID])

    pG = [pb.enter_context(nc.psum_tensor("pG0", [128, 512], F32)),
          pb.enter_context(nc.psum_tensor("pG1", [128, 512], F32))]
    pU = [pb.enter_context(nc.psum_tensor("pU0", [128, 512], F32)),
          pb.enter_context(nc.psum_tensor("pU1", [128, 512], F32))]
    pO = [pb.enter_context(nc.psum_tensor("pO0", [128, 512], F32)),
          pb.enter_context(nc.psum_tensor("pO1", [128, 512], F32))]

    NPT = PB_TOK // 128
    exp_seq = []
    gu_free = {}
    po_free = {}
    at_free = {0: 0}

    for p in range(npass):
        pbase = p * PB_TOK

        @E("scalar")
        def _(s, p=p):
            if p == 0:
                s.wait_ge(cT.sem, paT)
                s.wait_ge(cV.sem, paV)
                s.wait_ge(cG.sem, paG)
                s.wait_ge(dS.sem, paD)
                s.wait_ge(dGm.sem, paGm)
            else:
                s.wait_ge(cV.sem, cV.n)
            for k in range(8):
                dma(s, dA, h2T[:, k, :], d_h2[pbase:pbase + PB_TOK, 128 * k:128 * (k + 1)],
                    transpose=True)
        h2t_ld = dA.n

        @E("tensor")
        def _(t, p=p):
            t.wait_ge(dA.sem, h2t_ld)
            if p == 0:
                t.wait_ge(cV.sem, paV)
                t.wait_ge(cS.sem, paS)
            for tt in range(NPT):
                for k in range(8):
                    ins = t.matmul(pO[0][:, NE * tt:NE * (tt + 1)], h2T[:, k, 128 * tt:128 * (tt + 1)],
                                   gatewT[:, k, :], start=(tt == 0 and k == 0),
                                   stop=(tt == NPT - 1 and k == 7))
            cT.tick(ins)

        @E("scalar")
        def _(s):
            s.wait_ge(cT.sem, cT.n)
            cS.tick(s.activation(SCB1[:].rearrange("p T e -> p (T e)"), pO[0][:, 0:NE * NPT], AF.Sigmoid))

        @E("vector")
        def _(v, p=p):
            v.wait_ge(cS.sem, cS.n)
            if p == 0:
                v.wait_ge(cG.sem, paG)
                v.wait_ge(dGm.sem, paGm)
            v.memset(mx8[:, 1:8], 3.0e38)
            for tt in range(NPT):
                v.tensor_tensor(SCB2[:, tt, :], SCB1[:, tt, :], ebias_b[:], ALU.add)
                v.drain()
                for r in range(4):
                    src = SCB2[:, tt, :] if r % 2 == 0 else mrscr[:]
                    dstm = mrscr[:] if r % 2 == 0 else SCB2[:, tt, :]
                    v.reduce_max(mx8[:, 0:1], src, axis=AX.X)
                    v.drain()
                    v.match_replace(dstm, mx8[:], src, -1e30)
                    v.drain()
                v.tensor_scalar(mrscr[:], SCB2[:, tt, :], -1e29, None, ALU.is_le)
                v.drain()
                v.tensor_tensor(mrscr[:], mrscr[:], SCB1[:, tt, :], ALU.mult)
                v.drain()
                v.reduce_sum(mx8[:, 0:1], mrscr[:], axis=AX.X)
                v.drain()
                v.reciprocal(mx8[:, 1:2], mx8[:, 0:1])
                v.drain()
                v.tensor_scalar(SCB1[:, tt, :], mrscr[:], mx8[:, 1:2], None, ALU.mult)
                ins = v.memset(mx8[:, 1:8], 3.0e38)
            cV.tick(ins)
        comb_done = cV.n

        if "combine" in dbg:
            @E("sync")
            def _(sy, pbase=pbase, comb_done=comb_done):
                sy.wait_ge(cV.sem, comb_done)
                dma(sy, dS, dbg["combine"][pbase:pbase + PB_TOK, :]
                    .rearrange("(T p) e -> p T e", p=128), SCB1[:])

        @E("vector")
        def _(v, comb_done=comb_done):
            v.wait_ge(cV.sem, comb_done)
            cV.tick(v.memset(acc[:], 0.0))
        acc_zero = cV.n

        for e in range(nexp):
            ebuf = e % 2

            @E("gpsimd")
            def _(g, p=p, e=e, ebuf=ebuf):
                if p == 0 and e == 0:
                    g.wait_ge(cV.sem, paV)
                    g.wait_ge(cS.sem, paS)
                    g.wait_ge(cT.sem, paT)
                if len(exp_seq) >= 2:
                    g.wait_ge(cT.sem, exp_seq[-2])
                dma(g, dGm, wb1[ebuf][:].rearrange("p k i -> p (k i)"), dw1[e])
                dma(g, dGm, wb3[ebuf][:].rearrange("p k i -> p (k i)"), dw3[e])
                dma(g, dGm, wb2[ebuf][:].rearrange("p k i -> p (k i)"), dw2[e])
            wexp = dGm.n

            for ch in range(PB_TOK // PB_CH):
                cbase = PB_CH * ch
                for ic in range(6):
                    gp = ic % 2

                    @E("tensor")
                    def _(t, e=e, ebuf=ebuf, ic=ic, gp=gp, cbase=cbase, wexp=wexp, acc_zero=acc_zero):
                        t.wait_ge(dGm.sem, wexp)
                        t.wait_ge(cV.sem, max(acc_zero, gu_free.get(gp, 0)))
                        for k in range(8):
                            t.matmul(pG[gp][:], wb1[ebuf][:, k, 128 * ic:128 * (ic + 1)],
                                     h2T[:, k, cbase:cbase + PB_CH], start=(k == 0), stop=(k == 7))
                        for k in range(8):
                            ins = t.matmul(pU[gp][:], wb3[ebuf][:, k, 128 * ic:128 * (ic + 1)],
                                           h2T[:, k, cbase:cbase + PB_CH], start=(k == 0), stop=(k == 7))
                        cT.tick(ins)
                    mm_done = cT.n

                    @E("scalar")
                    def _(s, gp=gp, mm_done=mm_done):
                        s.wait_ge(cT.sem, mm_done)
                        cS.tick(s.activation(gtmp[:, gp, :], pG[gp][:], AF.Silu))

                    @E("vector")
                    def _(v, ic=ic, gp=gp):
                        v.wait_ge(cS.sem, cS.n)
                        if ic == 0:
                            v.wait_ge(cT.sem, at_free[0])
                        cV.tick(v.tensor_tensor(aT[:, ic, :], gtmp[:, gp, :], pU[gp][:], ALU.mult))
                    gu_free[gp] = cV.n
                a_done = cV.n

                for Mt in range(PB_CH // 128):
                    tt = (cbase // 128) + Mt
                    for n in range(2):
                        ob = (Mt * 2 + n) % 2

                        @E("tensor")
                        def _(t, ebuf=ebuf, Mt=Mt, n=n, ob=ob, a_done=a_done):
                            t.wait_ge(cV.sem, max(a_done, po_free.get(ob, 0)))
                            for kc in range(6):
                                ins = t.matmul(pO[ob][:], aT[:, kc, 128 * Mt:128 * (Mt + 1)],
                                               wb2[ebuf][:, kc, 512 * n:512 * (n + 1)],
                                               start=(kc == 0), stop=(kc == 5))
                            cT.tick(ins)
                        w2_done = cT.n
                        at_free[0] = cT.n

                        @E("scalar")
                        def _(s, e=e, tt=tt, ob=ob, w2_done=w2_done):
                            s.wait_ge(cT.sem, w2_done)
                            cS.tick(s.activation(yscr[:, ob, :], pO[ob][:], AF.Copy,
                                                 scale=SCB1[:, tt, e:e + 1]))

                        @E("vector")
                        def _(v, tt=tt, n=n, ob=ob):
                            v.wait_ge(cS.sem, cS.n)
                            dst = acc[:, tt, 512 * n:512 * (n + 1)]
                            cV.tick(v.tensor_tensor(dst, dst, yscr[:, ob, :], ALU.add))
                        po_free[ob] = cV.n
            exp_seq.append(cT.n)
        moe_done = cV.n

        for g2 in range(NPT // 2):
            @E("sync")
            def _(sy, pbase=pbase, g2=g2, moe_done=moe_done):
                sy.wait_ge(cV.sem, moe_done if g2 == 0 else cV.n)
                r0 = pbase + 256 * g2
                dma(sy, dS, xmb[:], d_xmid[r0:r0 + 256, :].rearrange("(T p) d -> p T d", p=128))
            xm_ld = dS.n

            @E("vector")
            def _(v, g2=g2, xm_ld=xm_ld):
                v.wait_ge(dS.sem, xm_ld)
                for T in range(2):
                    tt = 2 * g2 + T
                    v.tensor_tensor(obuf[:, T, :], acc[:, tt, :], gb_b[:], ALU.mult)
                    ins = v.tensor_tensor(obuf[:, T, :], obuf[:, T, :], xmb[:, T, :], ALU.add)
                cV.tick(ins)

            @E("sync")
            def _(sy, pbase=pbase, g2=g2):
                sy.wait_ge(cV.sem, cV.n)
                r0 = pbase + 256 * g2
                dma(sy, dS, dout[r0:r0 + 256, :].rearrange("(T p) d -> p T d", p=128), obuf[:])

    @E("sync")
    def _(sy):
        sy.wait_ge(dS.sem, dS.n)

    pb.close()
    es.close()
    return nc


def _host_constants():
    import ml_dtypes
    ident = np.eye(128, dtype=ml_dtypes.bfloat16)
    identf = np.eye(128, dtype=np.float32)
    mA = np.zeros((128, 128), np.float32)
    mAU = np.zeros((128, 128), np.float32)
    mT = np.zeros((128, 128), np.float32)
    for c in range(4):
        s = slice(32 * c, 32 * (c + 1))
        tril = np.tril(np.ones((32, 32), np.float32), -1)
        mA[s, s] = -tril
        mAU[s, s] = -tril.T
        mT[s, s] = np.triu(np.ones((32, 32), np.float32))
    return ident, identf, mA, mAU, mT


def make_in_maps(inp, taps=()):
    ident, identf, mA, mAU, mT = _host_constants()
    f32 = lambda a: np.ascontiguousarray(np.asarray(a, np.float32))
    conv_taps = np.stack([f32(inp["L0_attn_qc_w"]), f32(inp["L0_attn_kc_w"]),
                          f32(inp["L0_attn_vc_w"])])
    convt = np.ascontiguousarray(conv_taps.reshape(3, 8, 128, 4).transpose(2, 0, 1, 3))
    firs = np.ascontiguousarray(f32(inp["L0_attn_fir_s"]).reshape(1024, 3)
                                .reshape(8, 128, 3).transpose(1, 0, 2))
    firl = np.ascontiguousarray(f32(inp["L0_attn_fir_l"]).reshape(1024, 31)
                                .reshape(8, 128, 31).transpose(1, 0, 2))
    shared = {
        "Wq": f32(inp["L0_attn_Wq"]), "Wk": f32(inp["L0_attn_Wk"]),
        "Wv": f32(inp["L0_attn_Wv"]), "Wb": f32(inp["L0_attn_Wb"]),
        "Wo": f32(inp["L0_attn_Wo"]),
        "convtaps": convt, "firs": firs, "firl": firl,
        "gW1": f32(inp["L0_attn_gW1"]), "gb1": f32(inp["L0_attn_gb1"]).reshape(1, 512),
        "gW2": f32(inp["L0_attn_gW2"]), "gb2": f32(inp["L0_attn_gb2"]).reshape(1, 4),
        "temp": f32(inp["L0_attn_temp"]).reshape(1, 4),
        "floorv": f32(inp["L0_attn_eps_floor"]).reshape(1, 16),
        "onorm": f32(inp["L0_attn_onorm_w"]).reshape(1, DV),
        "wmod": f32(inp["L0_wmod"]),
        "gate_wT": np.ascontiguousarray(f32(inp["L0_ffn_gate_w"]).T),
        "e_bias": f32(inp["L0_ffn_e_bias"]).reshape(1, NE),
        "w1": np.ascontiguousarray(f32(inp["L0_ffn_w1"]).reshape(NE, 8, 128, EI)
                                   .transpose(0, 2, 1, 3).reshape(NE, 128, 8 * EI)),
        "w3": np.ascontiguousarray(f32(inp["L0_ffn_w3"]).reshape(NE, 8, 128, EI)
                                   .transpose(0, 2, 1, 3).reshape(NE, 128, 8 * EI)),
        "w2": np.ascontiguousarray(f32(inp["L0_ffn_w2"]).reshape(NE, 6, 128, HID)
                                   .transpose(0, 2, 1, 3).reshape(NE, 128, 6 * HID)),
        "identity": ident, "identityf": identf, "maskA": mA, "maskAU": mAU, "maskT": mT,
    }
    x = f32(inp["L0_attn_x"])
    cfull = f32(inp["L0_c"])
    in_maps = []
    for core in range(8):
        b, s = core // 2, core % 2
        xb = x[b] if s == 1 else np.ascontiguousarray(np.roll(x[b], HALF, axis=0))
        mask = np.ones((L,), np.float32)
        if s == 0:
            mask[:HALF] = 0.0
        m = dict(shared)
        m["x"] = xb
        m["tokmask"] = np.ascontiguousarray(mask.reshape(L // 128, 128).T)
        m["c_bT"] = np.ascontiguousarray(cfull[b].reshape(8, 128).T)
        in_maps.append(m)
    return in_maps


def kernel(**inp):
    nc = build_program()
    in_maps = make_in_maps(inp)
    res = run_bass_kernel_spmd(nc, in_maps, core_ids=list(range(8)))
    out = np.empty((4, L, HID), np.float32)
    for core in range(8):
        b, s = core // 2, core % 2
        out[b, HALF * s:HALF * (s + 1)] = res.results[core]["out"]
    return out


# version 07
# speedup vs baseline: 1.9289x; 1.2484x over previous
import numpy as np
import contextlib
import concourse.bass as bass
import concourse.mybir as mybir
from concourse.bass_utils import run_bass_kernel_spmd

F32 = mybir.dt.float32
BF16 = mybir.dt.bfloat16
AF = mybir.ActivationFunctionType
ALU = mybir.AluOpType
AX = mybir.AxisListType

HID, NH, DK, DV = 1024, 4, 256, 256
L, HALF = 4096, 2048
SEG = 256
NSEG = L // SEG
NT = SEG // 128
NE, EI = 32, 768
TOK = HALF
PB_TOK = 1024
PB_CH = 512


class Ctr:
    def __init__(self, sem):
        self.sem = sem
        self.n = 0

    def tick(self, instr, k=1):
        instr.then_inc(self.sem, k)
        self.n += k
        return self.n


def build_program(nseg=NSEG, nexp=NE, taps=(), own0=NSEG // 2, npass=2):
    nc = bass.Bass("TRN2", target_bir_lowering=False, detect_race_conditions=False)

    dx = nc.dram_tensor("x", [L, HID], F32, kind="ExternalInput").ap()
    dmask = nc.dram_tensor("tokmask", [128, L // 128], F32, kind="ExternalInput").ap()
    dWq = nc.dram_tensor("Wq", [HID, HID], F32, kind="ExternalInput").ap()
    dWk = nc.dram_tensor("Wk", [HID, HID], F32, kind="ExternalInput").ap()
    dWv = nc.dram_tensor("Wv", [HID, HID], F32, kind="ExternalInput").ap()
    dWb = nc.dram_tensor("Wb", [HID, NH], F32, kind="ExternalInput").ap()
    dWo = nc.dram_tensor("Wo", [HID, HID], F32, kind="ExternalInput").ap()
    dconv = nc.dram_tensor("convtaps", [128, 3, 8, 4], F32, kind="ExternalInput").ap()
    dfirs = nc.dram_tensor("firs", [128, 8, 3], F32, kind="ExternalInput").ap()
    dfirl = nc.dram_tensor("firl", [128, 8, 31], F32, kind="ExternalInput").ap()
    dgW1 = nc.dram_tensor("gW1", [5120, 512], F32, kind="ExternalInput").ap()
    dgb1 = nc.dram_tensor("gb1", [1, 512], F32, kind="ExternalInput").ap()
    dgW2 = nc.dram_tensor("gW2", [512, 4], F32, kind="ExternalInput").ap()
    dgb2 = nc.dram_tensor("gb2", [1, 4], F32, kind="ExternalInput").ap()
    dtemp = nc.dram_tensor("temp", [1, 4], F32, kind="ExternalInput").ap()
    dfloor = nc.dram_tensor("floorv", [1, 16], F32, kind="ExternalInput").ap()
    donorm = nc.dram_tensor("onorm", [1, DV], F32, kind="ExternalInput").ap()
    dcT = nc.dram_tensor("c_bT", [128, 8], F32, kind="ExternalInput").ap()
    dwmod = nc.dram_tensor("wmod", [HID, 6 * HID], F32, kind="ExternalInput").ap()
    dgatew = nc.dram_tensor("gate_wT", [HID, NE], F32, kind="ExternalInput").ap()
    debias = nc.dram_tensor("e_bias", [1, NE], F32, kind="ExternalInput").ap()
    dw1 = nc.dram_tensor("w1", [NE, 128, 8 * EI], F32, kind="ExternalInput").ap()
    dw3 = nc.dram_tensor("w3", [NE, 128, 8 * EI], F32, kind="ExternalInput").ap()
    dw2 = nc.dram_tensor("w2", [NE, 128, 6 * HID], F32, kind="ExternalInput").ap()
    dident = nc.dram_tensor("identity", [128, 128], BF16, kind="ExternalInput").ap()
    didentf = nc.dram_tensor("identityf", [128, 128], F32, kind="ExternalInput").ap()
    dmaskA = nc.dram_tensor("maskA", [128, 128], F32, kind="ExternalInput").ap()
    dmaskAU = nc.dram_tensor("maskAU", [128, 128], F32, kind="ExternalInput").ap()
    dmaskT = nc.dram_tensor("maskT", [128, 128], F32, kind="ExternalInput").ap()
    dout = nc.dram_tensor("out", [TOK, HID], F32, kind="ExternalOutput").ap()
    d_xmid = nc.dram_tensor("xmid_scr", [TOK, HID], F32, kind="Internal").ap()
    d_h2 = nc.dram_tensor("h2_scr", [TOK, HID], BF16, kind="Internal").ap()

    TAPSPEC = {
        "h": ([L, HID], BF16), "q": ([L, HID], BF16), "k": ([L, HID], BF16),
        "v": ([L, HID], BF16), "beta": ([L, NH], F32),
        "qn": ([L, HID], BF16), "kn": ([L, HID], BF16),
        "uw": ([L // 128, 128, NH, 512], BF16), "delta_o": ([L, HID], BF16),
        "fir_s": ([TOK, HID], BF16), "fir_l": ([TOK, HID], BF16),
        "stat": ([TOK // 128, 128, NH, 12], F32), "wgt": ([TOK // 128, 128, NH, 4], F32),
        "o_mix": ([TOK, HID], BF16), "x_mid": ([TOK, HID], F32), "h2": ([TOK, HID], BF16),
        "combine": ([TOK, NE], F32),
        "setup": ([8, 128, HID], F32),
    }
    dbg = {}
    for t in taps:
        shp, dt = TAPSPEC[t]
        dbg[t] = nc.dram_tensor("dbg_" + t, shp, dt, kind="ExternalOutput").ap()

    es = contextlib.ExitStack()

    def sbt(name, shape, dt=F32):
        return es.enter_context(nc.sbuf_tensor(name, shape, dt))

    def mksem(name):
        return Ctr(es.enter_context(nc.semaphore(name)))

    cT = mksem("cT")
    cV = mksem("cV")
    cS = mksem("cS")
    cG = mksem("cG")
    dS = mksem("dS")
    dGm = mksem("dGm")
    dA = mksem("dA")

    ident = sbt("ident", [128, 128], BF16)
    identf = sbt("identf", [128, 128], F32)
    maskA_s = sbt("maskA_s", [128, 128])
    maskAU_s = sbt("maskAU_s", [128, 128])
    maskT_s = sbt("maskT_s", [128, 128])
    convt = sbt("convt", [128, 3, 8, 4])
    firs_t = sbt("firs_t", [128, 8, 3])
    firl_t = sbt("firl_t", [128, 8, 31])
    mask_sb = sbt("mask_sb", [128, L // 128])
    Wb_s = sbt("Wb_s", [128, 8, NH], BF16)
    W1s = sbt("W1s", [12, 512], BF16)
    gW2h = sbt("gW2h", [128, 4, 4 * NH], BF16)
    gb2h = sbt("gb2h", [1, 4 * NH], BF16)
    gb1_b = sbt("gb1_b", [1, 512], BF16)
    floorb = sbt("floorb", [128, NH * 4])
    onormb = sbt("onormb", [128, DV])
    ebias_b = sbt("ebias_b", [128, NE])
    gatewT = sbt("gatewT", [128, 8, NE], BF16)
    invt_c = sbt("invt_c", [128, 4])
    ones1 = sbt("ones1", [1, 512], BF16)
    onec = sbt("onec", [128, 1], BF16)
    zrow = sbt("zrow", [1, 512], BF16)
    sa_b = sbt("sa_b", [128, HID], BF16)
    ca1_b = sbt("ca1_b", [128, HID], BF16)
    ga_b = sbt("ga_b", [128, HID], BF16)
    sbv_b = sbt("sbv_b", [128, HID], BF16)
    cb1_b = sbt("cb1_b", [128, HID], BF16)
    gb_b = sbt("gb_b", [128, HID], BF16)

    block = es.enter_context(nc.Block())

    def E(name):
        def deco(f):
            getattr(block, name)(f)
        return deco

    def dma(eng, ctr, out, in_, **kw):
        ctr.tick(eng.dma_start(out=out, in_=in_, **kw), 16)

    KT = lambda d: d.rearrange("(k p) e -> p k e", p=128)

    sc = contextlib.ExitStack()

    def sbl(name, shape, dt=F32):
        return sc.enter_context(nc.sbuf_tensor(name, shape, dt))

    modrow = sbl("modrow", [1, 6 * HID])
    modrow_b = sbl("modrow_b", [1, 6 * HID], BF16)
    c_sb = sbl("c_sb", [128, 8])
    silc = sbl("silc", [128, 8], BF16)
    wmbuf = sbl("wmbuf", [128, 2, 8, 512], BF16)
    g1stg = sbl("g1stg", [128, 2, 512], BF16)
    gW2f = sbl("gW2f", [128, 4, 4])
    rows_s = sbl("rows_s", [1, 64])
    rows_b = sbl("rows_b", [1, 64], BF16)
    onr = sbl("onr", [1, DV])
    onr_b = sbl("onr_b", [1, DV], BF16)
    ebr = sbl("ebr", [1, NE])
    ebr_b = sbl("ebr_b", [1, NE], BF16)
    clampt = sbl("clampt", [128, HID])
    clampt2 = sbl("clampt2", [128, HID])
    w1row = sbl("w1row", [1, 512], BF16)
    pSu = sc.enter_context(nc.psum_tensor("pSu", [128, 512], F32))
    pRu = sc.enter_context(nc.psum_tensor("pRu", [1, 512], F32))

    @E("sync")
    def _(sy):
        for o, i in [(ident[:], dident), (identf[:], didentf), (maskA_s[:], dmaskA),
                     (maskAU_s[:], dmaskAU), (maskT_s[:], dmaskT), (convt[:], dconv),
                     (firs_t[:], dfirs), (firl_t[:], dfirl), (mask_sb[:], dmask),
                     (c_sb[:], dcT), (gW2f[:], KT(dgW2)),
                     (rows_s[0:1, 0:4], dgb2), (rows_s[0:1, 4:8], dtemp),
                     (rows_s[0:1, 16:32], dfloor), (onr[:], donorm), (ebr[:], debias)]:
            dma(sy, dS, o, i)
    ld_smalls = dS.n

    @E("gpsimd")
    def _(g):
        dma(g, dGm, Wb_s[:], KT(dWb))
        dma(g, dGm, gatewT[:], KT(dgatew))
        dma(g, dGm, gb1_b[:], dgb1)
    n_pre_g = 3

    @E("vector")
    def _(v):
        v.memset(ones1[:], 1.0)
        v.memset(onec[:], 1.0)
        cV.tick(v.memset(zrow[:], 0.0))
    ones_done = cV.n

    @E("scalar")
    def _(s):
        s.wait_ge(dS.sem, ld_smalls)
        cS.tick(s.activation(silc[:], c_sb[:], AF.Silu))
    silc_done = cS.n

    mod_evac = {}
    for n in range(12):
        @E("gpsimd")
        def _(g, n=n):
            if n >= 2:
                g.wait_ge(cS.sem, mod_evac[n - 2])
            dma(g, dGm, wmbuf[:, n % 2], KT(dwmod[:, 512 * n:512 * (n + 1)]))

        @E("tensor")
        def _(t, n=n):
            t.wait_ge(dGm.sem, dGm.n)
            if n == 0:
                t.wait_ge(cS.sem, silc_done)
            else:
                t.wait_ge(cS.sem, mod_evac[n - 1])
            for k in range(8):
                ins = t.matmul(pRu[:], silc[:, k:k + 1], wmbuf[:, n % 2, k, :],
                               start=(k == 0), stop=(k == 7))
            cT.tick(ins)

        @E("scalar")
        def _(s, n=n):
            s.wait_ge(cT.sem, cT.n)
            cS.tick(s.copy(modrow[0:1, 512 * n:512 * (n + 1)], pRu[:]))
        mod_evac[n] = cS.n

    @E("vector")
    def _(v):
        v.wait_ge(cS.sem, mod_evac[11])
        cV.tick(v.tensor_copy(modrow_b[:], modrow[:]))
    modb_done = cV.n

    vec_tiles = [sa_b, ca1_b, ga_b, sbv_b, cb1_b, gb_b]
    for vi in range(6):
        for half in range(2):
            @E("tensor")
            def _(t, vi=vi, half=half):
                t.wait_ge(cV.sem, modb_done if (vi == 0 and half == 0) else cV.n)
                cT.tick(t.matmul(pSu[:], ones1[0:1, 0:128],
                                 modrow_b[0:1, HID * vi + 512 * half: HID * vi + 512 * (half + 1)],
                                 start=True, stop=True))

            @E("vector")
            def _(v, vi=vi, half=half):
                v.wait_ge(cT.sem, cT.n)
                dst = vec_tiles[vi][:, 512 * half:512 * (half + 1)]
                if vi in (1, 4):
                    cV.tick(v.tensor_scalar(dst, pSu[:], 1.0, None, ALU.add))
                else:
                    cV.tick(v.tensor_copy(dst, pSu[:]))

    @E("vector")
    def _(v):
        v.wait_ge(dS.sem, ld_smalls)
        cV.tick(v.tensor_copy(rows_b[0:1, 32:33], rows_s[0:1, 16:17]))

    for gt in (ga_b, gb_b):
        @E("scalar")
        def _(s, gt=gt):
            s.wait_ge(cV.sem, cV.n)
            cS.tick(s.activation(clampt2[:], gt[:], AF.Abs))

        @E("vector")
        def _(v, gt=gt):
            v.wait_ge(cS.sem, cS.n)
            v.tensor_scalar(clampt[:], gt[:], 0.0, None, ALU.is_ge)
            v.tensor_scalar(clampt[:], clampt[:], 2.0, -1.0, ALU.mult, ALU.add)
            v.tensor_scalar(clampt2[:], clampt2[:], 0.25, None, ALU.max)
            cV.tick(v.tensor_tensor(gt[:], clampt[:], clampt2[:], ALU.mult))

    @E("vector")
    def _(v):
        v.wait_ge(cV.sem, cV.n)
        v.tensor_scalar(rows_s[0:1, 8:12], rows_s[0:1, 4:8], 0.2, 10.0, ALU.max, ALU.min)
        v.drain()
        v.reciprocal(rows_s[0:1, 12:16], rows_s[0:1, 8:12])
        v.drain()
        v.tensor_scalar(rows_s[0:1, 16:32], rows_s[0:1, 16:32], 1e-7, 0.1, ALU.max, ALU.min)
        v.drain()
        v.tensor_copy(rows_b[0:1, 0:32], rows_s[0:1, 0:32])
        v.tensor_copy(onr_b[:], onr[:])
        cV.tick(v.tensor_copy(ebr_b[:], ebr[:]))

    for src, dst, wid in [(rows_b[0:1, 12:16], invt_c[:], 4),
                          (rows_b[0:1, 16:32], floorb[:], 16),
                          (ebr_b[0:1, :], ebias_b[:], NE),
                          (onr_b[0:1, :], onormb[:], DV)]:
        @E("tensor")
        def _(t, src=src, wid=wid):
            t.wait_ge(cV.sem, cV.n)
            cT.tick(t.matmul(pSu[:, 0:wid], ones1[0:1, 0:128], src, start=True, stop=True))

        @E("vector")
        def _(v, dst=dst, wid=wid):
            v.wait_ge(cT.sem, cT.n)
            cV.tick(v.tensor_copy(dst, pSu[:, 0:wid]))

    @E("vector")
    def _(v):
        v.wait_ge(cV.sem, cV.n)
        for h in range(NH):
            v.tensor_scalar(gW2h[:, :, 4 * h:4 * (h + 1)], gW2f[:], invt_c[:, h:h + 1], None, ALU.mult)
            ins = v.tensor_scalar(gb2h[0:1, 4 * h:4 * (h + 1)], rows_s[0:1, 0:4],
                                  rows_s[0:1, 12 + h:13 + h], None, ALU.mult)
        cV.tick(ins)

    w1s_evac = {}
    for j in range(24):
        @E("gpsimd")
        def _(g, j=j):
            if j >= 2:
                g.wait_ge(cS.sem, w1s_evac[j - 2])
            dma(g, dGm, g1stg[:, j % 2, :], dgW1[HID + 128 * j: HID + 128 * (j + 1), :])

        @E("tensor")
        def _(t, j=j):
            t.wait_ge(dGm.sem, dGm.n)
            if j >= 2:
                t.wait_ge(cS.sem, w1s_evac[j - 1])
            cT.tick(t.matmul(pRu[:], onec[:], g1stg[:, j % 2, :], start=(j % 2 == 0), stop=(j % 2 == 1)))

        if j % 2 == 1:
            @E("scalar")
            def _(s, j=j):
                s.wait_ge(cT.sem, cT.n)
                if j >= 3:
                    s.wait_ge(dS.sem, dS.n)
                grp = j // 2
                scale = (1.0 / 256.0) if (grp % 3 == 0) else 1.0
                cS.tick(s.activation(w1row[:], pRu[:], AF.Copy, scale=scale))

            @E("sync")
            def _(sy, j=j):
                sy.wait_ge(cS.sem, cS.n)
                dma(sy, dS, W1s[j // 2:j // 2 + 1, :], w1row[:])
        w1s_evac[j] = cS.n

    if "setup" in dbg:
        @E("gpsimd")
        def _(sy):
            sy.wait_ge(cS.sem, cS.n)
            sy.wait_ge(cV.sem, cV.n)
            for j, tsr in enumerate([sa_b, ca1_b, ga_b, sbv_b, cb1_b, gb_b]):
                dma(sy, dGm, dbg["setup"][j], tsr[:])
            dma(sy, dGm, dbg["setup"][6, 0:12, 0:512], W1s[:])
            dma(sy, dGm, dbg["setup"][7, :, 0:4], invt_c[:])
            dma(sy, dGm, dbg["setup"][7, :, 4:20], floorb[:])
            dma(sy, dGm, dbg["setup"][7, :, 20:276], onormb[:])
            dma(sy, dGm, dbg["setup"][7, 0:1, 276:340], rows_s[:])
            dma(sy, dGm, dbg["setup"][7, 1:2, 276:340], rows_b[:])
    w1s_dma = dS.n
    sc.close()

    pa = contextlib.ExitStack()

    def sba(name, shape, dt=F32):
        return pa.enter_context(nc.sbuf_tensor(name, shape, dt))

    x_sb = [sba("x_sb0", [128, NT, HID]), sba("x_sb1", [128, NT, HID])]
    h_bf = sba("h_bf", [128, NT, HID], BF16)
    hT = sba("hT", [128, 8, SEG], BF16)
    SCR = sba("SCR", [128, NT, HID])
    CONVB = sba("CONVB", [128, 8, SEG])
    tail3 = sba("tail3", [128, 3, 8, 3])
    qTc = sba("qTc", [128, 8, SEG], BF16)
    kTc = sba("kTc", [128, 8, SEG], BF16)
    vTc = [sba("vTc0", [128, 8, SEG], BF16), sba("vTc1", [128, 8, SEG], BF16)]
    TOKA = sba("TOKA", [128, NT, HID], BF16)
    TOKB = sba("TOKB", [128, NT, HID], BF16)
    v_tok = sba("v_tok", [128, NT, HID], BF16)
    kn_tok = sba("kn_tok", [128, NT, HID], BF16)
    TOKC = sba("TOKC", [128, NT, HID], BF16)
    Xu = sba("Xu", [128, NT, NH, 2 * DV], BF16)
    qnT = sba("qnT", [128, 8, SEG], BF16)
    knT = sba("knT", [128, 8, SEG], BF16)
    kbnT = sba("kbnT", [128, 8, SEG], BF16)
    wT_sb = sba("wT_sb", [128, 8, SEG], BF16)
    attnT = sba("attnT", [128, NT * NH, 128], BF16)
    S_bf = sba("S_bf", [128, NH, 2 * DV], BF16)
    TpP = sba("TpP", [128, NT * NH, 128], BF16)
    TpB5 = sba("TpB5", [128, NT * NH, 5, 128], BF16)
    beta_sb = sba("beta_sb", [128, NT, NH])
    statv = sba("statv", [128, NT, NH, 12])
    svT = sba("svT", [12, NH, SEG], BF16)
    hmG = sba("hmG", [128, NH, 4, SEG], BF16)
    wgt = sba("wgt", [128, NT, NH, 4])
    stat_s = sba("stat_s", [128, 32])
    sqd = sba("sqd", [128, HID])
    vwin = sba("vwin", [128, 8, 286], BF16)
    W_ring = sba("W_ring", [128, 24, HID], BF16)
    ez_sb = sba("ez_sb", [128, NT * NH, 4])

    S_ps = [pa.enter_context(nc.psum_tensor(f"S_ps{h}", [128, 512], F32)) for h in range(NH)]
    pA = pa.enter_context(nc.psum_tensor("pA", [128, 512], F32))
    pB = pa.enter_context(nc.psum_tensor("pB", [128, 512], F32))

    @E("sync")
    def _(sy):
        dma(sy, dS, x_sb[0][:], dx[0:SEG, :].rearrange("(T p) d -> p T d", p=128))
    x_ld = {0: dS.n}

    @E("gpsimd")
    def _(g):
        dma(g, dGm, W_ring[:, 0:8, :], KT(dWq))
        dma(g, dGm, W_ring[:, 8:16, :], KT(dWk))
        dma(g, dGm, W_ring[:, 16:24, :], KT(dWv))
    w1_ld = {0: dGm.n}

    @E("vector")
    def _(v):
        v.memset(S_bf[:], 0.0)
        cV.tick(v.memset(tail3[:], 0.0))
    sbf_zero = cV.n

    setup_cS, setup_cV, setup_dS = cS.n, cV.n, dS.n

    @E("tensor")
    def _(t):
        t.wait_ge(cV.sem, max(ones_done, setup_cV))
        t.wait_ge(cS.sem, setup_cS)
        t.wait_ge(dS.sem, setup_dS)
        for h in range(NH):
            ins = t.matmul(S_ps[h][:], zrow[0:1, 0:128], zrow[0:1, 0:512],
                           start=True, stop=False, skip_group_check=True)
        cT.tick(ins)

    x_free = {0: (cV, 0), 1: (cV, 0)}
    scast = {h: (cS, 0) for h in range(NH)}
    scan_free = 0
    vwin_guard = 0
    w4_ld = {}

    for i in range(nseg):
        buf = i % 2
        own = i >= own0
        orow = (i - own0) * SEG

        if i + 1 < nseg:
            @E("sync")
            def _(sy, i=i):
                ctr, val = x_free[(i + 1) % 2]
                if val:
                    sy.wait_ge(ctr.sem, val)
                dma(sy, dS, x_sb[(i + 1) % 2][:],
                    dx[SEG * (i + 1):SEG * (i + 2), :].rearrange("(T p) d -> p T d", p=128))
            x_ld[i + 1] = dS.n

        @E("vector")
        def _(v, i=i):
            v.wait_ge(dS.sem, x_ld[i])
            if i == 0:
                v.wait_ge(cV.sem, sbf_zero)
            v.reduce_sum(stat_s[:, 0:1], x_sb[i % 2][:, 0, :], axis=AX.X)
            cV.tick(v.reduce_sum(stat_s[:, 1:2], x_sb[i % 2][:, 1, :], axis=AX.X))

        @E("scalar")
        def _(s, i=i):
            s.wait_ge(dS.sem, x_ld[i])
            s.wait_ge(cV.sem, cV.n)
            s.activation(sqd[:], x_sb[i % 2][:, 0, :], AF.Square, accum_out=stat_s[:, 2:3])
            cS.tick(s.activation(sqd[:], x_sb[i % 2][:, 1, :], AF.Square, accum_out=stat_s[:, 3:4]))

        @E("vector")
        def _(v):
            v.wait_ge(cS.sem, cS.n)
            v.tensor_scalar(stat_s[:, 4:6], stat_s[:, 0:2], 1.0 / HID, None, ALU.mult)
            v.drain()
            v.tensor_tensor(stat_s[:, 6:8], stat_s[:, 4:6], stat_s[:, 4:6], ALU.mult)
            v.tensor_scalar(stat_s[:, 8:10], stat_s[:, 2:4], 1.0 / HID, 1e-5, ALU.mult, ALU.add)
            v.drain()
            cV.tick(v.tensor_tensor(stat_s[:, 8:10], stat_s[:, 8:10], stat_s[:, 6:8], ALU.subtract))

        @E("scalar")
        def _(s):
            s.wait_ge(cV.sem, cV.n)
            cS.tick(s.activation(stat_s[:, 10:12], stat_s[:, 8:10], AF.Sqrt))

        @E("vector")
        def _(v, i=i):
            v.wait_ge(cS.sem, cS.n)
            v.reciprocal(stat_s[:, 12:14], stat_s[:, 10:12])
            v.drain()
            for T in range(NT):
                v.tensor_scalar(sqd[:], x_sb[i % 2][:, T, :], stat_s[:, 4 + T:5 + T],
                                stat_s[:, 12 + T:13 + T], ALU.subtract, ALU.mult)
                v.tensor_tensor(sqd[:], sqd[:], ca1_b[:], ALU.mult)
                v.tensor_tensor(sqd[:], sqd[:], sa_b[:], ALU.add)
                ins = v.tensor_scalar(h_bf[:, T, :], sqd[:],
                                      mask_sb[:, i * NT + T:i * NT + T + 1], None, ALU.mult)
            cV.tick(ins)
        h_done = cV.n

        pTt_cm = nc.psum_tensor("pTt%d" % i, [128, 1024], BF16)
        with pTt_cm as pTt:
            for T in range(NT):
                @E("tensor")
                def _(t, T=T):
                    t.wait_ge(cV.sem, h_done if T == 0 else cV.n)
                    for k in range(8):
                        ins = t.transpose(pTt[:, 128 * k:128 * (k + 1)],
                                          h_bf[:, T, 128 * k:128 * (k + 1)], ident[:])
                    cT.tick(ins)

                @E("vector")
                def _(v, T=T):
                    v.wait_ge(cT.sem, cT.n)
                    cV.tick(v.tensor_copy(hT[:, :, 128 * T:128 * (T + 1)],
                                          pTt[:].rearrange("p (k t) -> p k t", t=128)))
            hT_done = cV.n

            @E("tensor")
            def _(t, i=i):
                t.wait_ge(cV.sem, hT_done)
                t.wait_ge(dGm.sem, w1_ld[i])
                t.wait_ge(cS.sem, scan_free)
                for T in range(NT):
                    for k in range(8):
                        ins = t.matmul(pB[:, 4 * T:4 * (T + 1)], hT[:, k, 128 * T:128 * (T + 1)],
                                       Wb_s[:, k, :], start=(T == 0 and k == 0),
                                       stop=(T == NT - 1 and k == 7))
                cT.tick(ins)

            @E("scalar")
            def _(s):
                s.wait_ge(cT.sem, cT.n)
                cS.tick(s.activation(beta_sb[:].rearrange("p T h -> p (T h)"),
                                     pB[:, 0:4 * NT], AF.Sigmoid))
            beta_done = cS.n

            @E("vector")
            def _(v, beta_done=beta_done):
                v.wait_ge(cS.sem, beta_done)
                cV.tick(v.memset(stat_s[:, 30:31], 0.0))
            evac_par = {0: cV.n, 1: cV.n}

            q_on = own or (i == own0 - 1)
            tensors3 = [(qTc, 0, 0), (kTc, 8, 1), (vTc[buf], 16, 2)] if q_on else \
                       [(kTc, 8, 1), (vTc[buf], 16, 2)]
            conv_done_m = 0
            silu_work = []
            for dst, base, wi in tensors3:
                for f in range(8):
                    @E("tensor")
                    def _(t, base=base, f=f, evp=evac_par[f % 2]):
                        if evp:
                            t.wait_ge(cV.sem, evp)
                        pbank = pA if f % 2 == 0 else pB
                        for k in range(8):
                            ins = t.matmul(pbank[:, 0:256],
                                           W_ring[:, base + k, 128 * f:128 * (f + 1)],
                                           hT[:, k, :], start=(k == 0), stop=(k == 7))
                        cT.tick(ins)

                    @E("vector")
                    def _(v, f=f):
                        v.wait_ge(cT.sem, cT.n)
                        pbank = pA if f % 2 == 0 else pB
                        scr8 = SCR[:].rearrange("p T (k2 z) -> p (T k2) z", z=SEG)
                        cV.tick(v.tensor_copy(scr8[:, f, :], pbank[:, 0:256]))
                    evac_par[f % 2] = cV.n
                raw_done = cV.n

                @E("vector")
                def _(v, wi=wi, i=i, raw_done=raw_done):
                    scr8 = SCR[:].rearrange("p T (k2 z) -> p (T k2) z", z=SEG)
                    for dt_ in range(8):
                        tp = lambda j: convt[:, wi, dt_, j:j + 1]
                        v.tensor_scalar(CONVB[:, dt_, :], scr8[:, dt_, :], tp(3), None, ALU.mult)
                        for sh in (1, 2, 3):
                            v.scalar_tensor_tensor(CONVB[:, dt_, sh:SEG], scr8[:, dt_, 0:SEG - sh],
                                                   tp(3 - sh), CONVB[:, dt_, sh:SEG], ALU.mult, ALU.add)
                            v.scalar_tensor_tensor(CONVB[:, dt_, 0:sh], tail3[:, wi, dt_, 3 - sh:3],
                                                   tp(3 - sh), CONVB[:, dt_, 0:sh], ALU.mult, ALU.add)
                    cV.tick(v.tensor_copy(tail3[:, wi, :, :], scr8[:, :, SEG - 3:SEG]))
                conv_done_m = cV.n

                silu_work.append((dst, wi, cV.n))

            @E("scalar")
            def _(s, work=tuple(silu_work), vg=vwin_guard):
                for dst, wi, cvt in work:
                    s.wait_ge(cV.sem, cvt)
                    if wi == 2 and vg:
                        s.wait_ge(cV.sem, vg)
                    cS.tick(s.activation(dst[:].rearrange("p k z -> p (k z)"),
                                         CONVB[:].rearrange("p k z -> p (k z)"), AF.Silu))
            qkv_done = cS.n

            if own:
                @E("gpsimd")
                def _(g):
                    g.wait_ge(cS.sem, qkv_done)
                    dma(g, dGm, W_ring[:, 0:8, :], KT(dWo))
                    dma(g, dGm, W_ring[:, 8:16, 0:512], KT(dgW1[0:HID, :]))
                    dma(g, dGm, W_ring[:, 16:24, 0:512], KT(dgW1[HID + 3 * DV * NH:5120, :]))
                w4_ld[i] = dGm.n

            srcdst = ([(qTc, TOKA)] if own else []) + [(kTc, TOKB), (vTc[buf], v_tok)]
            for src, dst2 in srcdst:
                for T in range(NT):
                    @E("tensor")
                    def _(t, src=src, T=T):
                        t.wait_ge(cS.sem, qkv_done)
                        t.wait_ge(cV.sem, cV.n)
                        for k in range(8):
                            ins = t.transpose(pTt[:, 128 * k:128 * (k + 1)],
                                              src[:, k, 128 * T:128 * (T + 1)], ident[:])
                        cT.tick(ins)

                    @E("vector")
                    def _(v, dst2=dst2, T=T):
                        v.wait_ge(cT.sem, cT.n)
                        cV.tick(v.tensor_copy(dst2[:, T, :],
                                              pTt[:].rearrange("p (k t) -> p k t", t=128)))
            tok_done = cV.n

            @E("scalar")
            def _(s, own=own):
                s.wait_ge(cV.sem, tok_done)
                for T in range(NT):
                    for h in range(NH):
                        if own:
                            s.activation(sqd[:, 0:DV], TOKA[:, T, DV * h:DV * (h + 1)], AF.Square,
                                         accum_out=stat_s[:, 8 + 8 * T + h:9 + 8 * T + h])
                        ins = s.activation(sqd[:, 0:DV], TOKB[:, T, DV * h:DV * (h + 1)], AF.Square,
                                           accum_out=stat_s[:, 12 + 8 * T + h:13 + 8 * T + h])
                cS.tick(ins)

            @E("vector")
            def _(v, own=own):
                v.wait_ge(cS.sem, cS.n)
                if own:
                    cV.tick(v.tensor_scalar(stat_s[:, 8:24], stat_s[:, 8:24], 1e-6, None, ALU.add))
                else:
                    v.tensor_scalar(stat_s[:, 12:16], stat_s[:, 12:16], 1e-6, None, ALU.add)
                    cV.tick(v.tensor_scalar(stat_s[:, 20:24], stat_s[:, 20:24], 1e-6, None, ALU.add))

            @E("scalar")
            def _(s, own=own):
                s.wait_ge(cV.sem, cV.n)
                if own:
                    cS.tick(s.activation(stat_s[:, 8:24], stat_s[:, 8:24], AF.Sqrt))
                else:
                    s.activation(stat_s[:, 12:16], stat_s[:, 12:16], AF.Sqrt)
                    cS.tick(s.activation(stat_s[:, 20:24], stat_s[:, 20:24], AF.Sqrt))

            @E("vector")
            def _(v, own=own):
                v.wait_ge(cS.sem, cS.n)
                if own:
                    v.reciprocal(stat_s[:, 8:24], stat_s[:, 8:24])
                else:
                    v.reciprocal(stat_s[:, 12:16], stat_s[:, 12:16])
                    v.reciprocal(stat_s[:, 20:24], stat_s[:, 20:24])
                v.drain()
                for T in range(NT):
                    for h in range(NH):
                        if own:
                            v.tensor_scalar(TOKC[:, T, DV * h:DV * (h + 1)],
                                            TOKA[:, T, DV * h:DV * (h + 1)],
                                            stat_s[:, 8 + 8 * T + h:9 + 8 * T + h], None, ALU.mult)
                        v.tensor_scalar(kn_tok[:, T, DV * h:DV * (h + 1)],
                                        TOKB[:, T, DV * h:DV * (h + 1)],
                                        stat_s[:, 12 + 8 * T + h:13 + 8 * T + h], None, ALU.mult)
                        v.tensor_scalar(Xu[:, T, h, 0:DV], v_tok[:, T, DV * h:DV * (h + 1)],
                                        beta_sb[:, T, h:h + 1], None, ALU.mult)
                        ins = v.tensor_scalar(Xu[:, T, h, DV:2 * DV],
                                              kn_tok[:, T, DV * h:DV * (h + 1)],
                                              beta_sb[:, T, h:h + 1], None, ALU.mult)
                cV.tick(ins)
            norm_done = cV.n

            tlist = ([("qn", qnT)] if own else []) + [("kn", knT), ("kb", kbnT)]
            for srcf, dstt in tlist:
                for T in range(NT):
                    @E("tensor")
                    def _(t, srcf=srcf, T=T):
                        t.wait_ge(cV.sem, cV.n)
                        for hh in range(8):
                            h, half = hh // 2, hh % 2
                            if srcf == "qn":
                                src_ap = TOKC[:, T, DV * h + 128 * half:DV * h + 128 * (half + 1)]
                            elif srcf == "kn":
                                src_ap = kn_tok[:, T, DV * h + 128 * half:DV * h + 128 * (half + 1)]
                            else:
                                src_ap = Xu[:, T, h, DV + 128 * half:DV + 128 * (half + 1)]
                            ins = t.transpose(pTt[:, 128 * hh:128 * (hh + 1)], src_ap, ident[:])
                        cT.tick(ins)

                    @E("vector")
                    def _(v, dstt=dstt, T=T):
                        v.wait_ge(cT.sem, cT.n)
                        cV.tick(v.tensor_copy(dstt[:, :, 128 * T:128 * (T + 1)],
                                              pTt[:].rearrange("p (k t) -> p k t", t=128)))
            tposed_done = cV.n

        with nc.psum_tensor("pTb%d" % i, [128, 512], BF16) as pTb, \
             nc.psum_tensor("pP2%d" % i, [128, 512], F32) as pP2:
            chains = [(T, h) for T in range(NT) for h in range(NH)]
            lvl_mm = {}
            lvl_evac = {}
            lvl_tp = {}
            app_add = {}
            prev_mask = 0
            for ci, (T, h) in enumerate(chains):
                @E("tensor")
                def _(t, T=T, h=h, own=own, pm=prev_mask):
                    t.wait_ge(cV.sem, pm if pm else cV.n)
                    for half in range(2):
                        t.matmul(pA[:, 0:128], kbnT[:, 2 * h + half, 128 * T:128 * (T + 1)],
                                 knT[:, 2 * h + half, 128 * T:128 * (T + 1)],
                                 start=(half == 0), stop=(half == 1))
                    for half in range(2):
                        ins = t.matmul(pA[:, 128:256], knT[:, 2 * h + half, 128 * T:128 * (T + 1)],
                                       kbnT[:, 2 * h + half, 128 * T:128 * (T + 1)],
                                       start=(half == 0), stop=(half == 1))
                    if own:
                        for half in range(2):
                            ins = t.matmul(pA[:, 256:384], knT[:, 2 * h + half, 128 * T:128 * (T + 1)],
                                           qnT[:, 2 * h + half, 128 * T:128 * (T + 1)],
                                           start=(half == 0), stop=(half == 1))
                    cT.tick(ins)

                @E("vector")
                def _(v, T=T, h=h, ci=ci, own=own):
                    v.wait_ge(cT.sem, cT.n)
                    v.tensor_tensor(TpP[:, ci, :], pA[:, 0:128], maskA_s[:], ALU.mult)
                    ins = v.tensor_tensor(TpB5[:, ci, 0, :], pA[:, 128:256], maskAU_s[:], ALU.mult)
                    if own:
                        ins = v.tensor_tensor(attnT[:, NH * T + h, :], pA[:, 256:384],
                                              maskT_s[:], ALU.mult)
                    cV.tick(ins)
                prev_mask = cV.n
                lvl_tp[ci] = cV.n

            firq = []
            firrec = {}
            if own:
                def _fw_vwin(v, buf=buf, qkv_done=qkv_done):
                    v.wait_ge(cS.sem, qkv_done)
                    v.tensor_copy(vwin[:, :, 0:30], vTc[1 - buf][:, :, SEG - 30:SEG])
                    cV.tick(v.tensor_copy(vwin[:, :, 30:286], vTc[buf][:]))
                    firrec["vwin"] = cV.n
                firq.append(("vector", _fw_vwin))

                def _fw_conv(v, dts, ftap, ntap, j0, j1, wait_key=None, rec_key=None):
                    if wait_key:
                        v.wait_ge(cS.sem, firrec[wait_key])
                    base = 30 - (ntap - 1)
                    for dt_ in dts:
                        for j in range(j0, j1):
                            if j == 0:
                                ins = v.tensor_scalar(CONVB[:, dt_, :], vwin[:, dt_, base:base + SEG],
                                                      ftap[:, dt_, 0:1], None, ALU.mult)
                            else:
                                ins = v.scalar_tensor_tensor(CONVB[:, dt_, :],
                                                             vwin[:, dt_, base + j:base + j + SEG],
                                                             ftap[:, dt_, j:j + 1], CONVB[:, dt_, :],
                                                             ALU.mult, ALU.add)
                    cV.tick(ins)
                    if rec_key:
                        firrec[rec_key] = cV.n

                def _fw_cast(s, dstT, wait_key, rec_key):
                    s.wait_ge(cV.sem, firrec[wait_key])
                    cS.tick(s.copy(dstT[:].rearrange("p k z -> p (k z)"),
                                   CONVB[:].rearrange("p k z -> p (k z)")))
                    firrec[rec_key] = cS.n

                import functools
                P = functools.partial
                firq.append(("vector", P(_fw_conv, dts=range(0, 4), ftap=firs_t, ntap=3, j0=0, j1=3)))
                firq.append(("vector", P(_fw_conv, dts=range(4, 8), ftap=firs_t, ntap=3, j0=0, j1=3,
                                         rec_key="fs")))
                firq.append(("scalar", P(_fw_cast, dstT=knT, wait_key="fs", rec_key="c1")))
                for dt_ in range(8):
                    firq.append(("vector", P(_fw_conv, dts=[dt_], ftap=firl_t, ntap=31, j0=0, j1=16,
                                             wait_key="c1" if dt_ == 0 else None)))
                    firq.append(("vector", P(_fw_conv, dts=[dt_], ftap=firl_t, ntap=31, j0=16, j1=31,
                                             rec_key="fl" if dt_ == 7 else None)))
                firq.append(("scalar", P(_fw_cast, dstT=kbnT, wait_key="fl", rec_key="c2")))

            def fir_pump(n=1):
                for _ in range(n):
                    if not firq or firq[0][0] != "vector":
                        return
                    E("vector")(firq.pop(0)[1])

            def fir_pump_cast():
                while firq and firq[0][0] == "scalar":
                    E("scalar")(firq.pop(0)[1])

            pslot = lambda ci: (pB if ci % 2 == 0 else pP2)[:, 128 * ((ci // 2) % 2):
                                                            128 * ((ci // 2) % 2 + 1)]
            for j in range(1, 5):
                ev_base = cS.n
                mms_j = []
                for ci in range(len(chains)):
                    @E("tensor")
                    def _(t, ci=ci, j=j):
                        t.wait_ge(cV.sem, lvl_tp[ci])
                        ev = lvl_evac.get((j, ci - 2), 0) if ci >= 2 else lvl_evac.get((j - 1, ci + 6), 0)
                        if ev:
                            t.wait_ge(cS.sem, ev)
                        cT.tick(t.matmul(pslot(ci), TpB5[:, ci, j - 1, :], TpP[:, ci, :],
                                         start=True, stop=True, skip_group_check=True))
                    lvl_mm[(j, ci)] = cT.n
                    mms_j.append(cT.n)
                    lvl_evac[(j, ci)] = ev_base + ci + 1

                    @E("vector")
                    def _(v, ci=ci, j=j, w=lvl_evac[(j, ci)]):
                        v.wait_ge(cS.sem, w)
                        cV.tick(v.transpose(TpB5[:, ci, j, :], TpP[:, ci, :]))
                    lvl_tp[ci] = cV.n
                    fir_pump(1)

                @E("scalar")
                def _(s, mms=tuple(mms_j)):
                    for ci, mmv in enumerate(mms):
                        s.wait_ge(cT.sem, mmv)
                        cS.tick(s.copy(TpP[:, ci, :], pslot(ci)))
                fir_pump_cast()

            powers_done = cV.n
            bank_free = {0: 0, 1: 0}
            for j in range(4, -1, -1):
                for ci, (T, h) in enumerate(chains):
                    bank = pA if ci % 2 == 0 else pB
                    @E("tensor")
                    def _(t, ci=ci, j=j, T=T, h=h, bank=bank, bf=bank_free[ci % 2]):
                        t.wait_ge(cV.sem, max(powers_done, lvl_tp[ci], bf,
                                              app_add.get((j + 1, ci), 0)))
                        cT.tick(t.matmul(bank[:], TpB5[:, ci, j, :], Xu[:, T, h, :],
                                         start=True, stop=True))
                    mmv = cT.n

                    @E("vector")
                    def _(v, ci=ci, T=T, h=h, bank=bank, mmv=mmv):
                        v.wait_ge(cT.sem, mmv)
                        cV.tick(v.tensor_tensor(Xu[:, T, h, :], Xu[:, T, h, :], bank[:], ALU.add))
                    app_add[(j, ci)] = cV.n
                    bank_free[ci % 2] = cV.n
                    fir_pump(1)
                    fir_pump_cast()
            while firq:
                fir_pump(1)
                fir_pump_cast()
            vwin_guard = firrec.get("vwin", vwin_guard)
            tchain_done = cV.n

            for T in range(NT):
                for grp in range(2):
                    @E("tensor")
                    def _(t, T=T, grp=grp):
                        t.wait_ge(cV.sem, cV.n)
                        for j in range(4):
                            hh = 4 * grp + j
                            h, half = hh // 2, hh % 2
                            ins = t.transpose(pTb[:, 128 * j:128 * (j + 1)],
                                              Xu[:, T, h, DV + 128 * half:DV + 128 * (half + 1)],
                                              ident[:])
                        cT.tick(ins)

                    @E("vector")
                    def _(v, T=T, grp=grp):
                        v.wait_ge(cT.sem, cT.n)
                        cV.tick(v.tensor_copy(wT_sb[:, 4 * grp:4 * (grp + 1), 128 * T:128 * (T + 1)],
                                              pTb[:, 0:512].rearrange("p (k t) -> p k t", t=128)))
            wT_done = cV.n

        with nc.psum_tensor("wSa%d" % i, [128, 512], F32) as wSa, \
             nc.psum_tensor("wSb%d" % i, [128, 512], F32) as wSb:
            for T in range(NT):
                cast_base = cS.n
                cast_mms = []
                for c in range(4):
                    for h in range(NH):
                        wbank = wSa if h % 2 == 0 else wSb
                        obank = pA if h < 2 else pB
                        ocol = 256 * (h % 2)
                        r0, r1 = 32 * c, 32 * (c + 1)

                        @E("tensor")
                        def _(t, T=T, c=c, h=h, wbank=wbank, obank=obank, ocol=ocol,
                              r0=r0, r1=r1, own=own):
                            t.wait_ge(cS.sem, max(scast[h][1], cS.n if (c == 0 and h == 0) else 0))
                            if T == 0 and c == 0 and h == 0:
                                t.wait_ge(cV.sem, wT_done)
                            if h >= 2:
                                t.wait_ge(cV.sem, cV.n)
                            for half in range(2):
                                ins = t.matmul(wbank[r0:r1, 0:256],
                                               wT_sb[:, 2 * h + half, 128 * T + r0:128 * T + r1],
                                               S_bf[:, h, 256 * half:256 * (half + 1)],
                                               start=(half == 0), stop=(half == 1),
                                               tile_position=(0, r0), skip_group_check=True)
                            if own:
                                for half in range(2):
                                    ins = t.matmul(obank[r0:r1, ocol:ocol + 256],
                                                   qnT[:, 2 * h + half, 128 * T + r0:128 * T + r1],
                                                   S_bf[:, h, 256 * half:256 * (half + 1)],
                                                   start=(half == 0 and h % 2 == 0), stop=False,
                                                   tile_position=(0, r0), skip_group_check=True)
                            cT.tick(ins)

                        @E("vector")
                        def _(v, T=T, h=h, wbank=wbank, r0=r0, r1=r1):
                            v.wait_ge(cT.sem, cT.n)
                            cV.tick(v.tensor_tensor(Xu[r0:r1, T, h, 0:DV], Xu[r0:r1, T, h, 0:DV],
                                                    wbank[r0:r1, 0:256], ALU.subtract))

                        @E("tensor")
                        def _(t, T=T, c=c, h=h, r0=r0, r1=r1):
                            t.wait_ge(cV.sem, cV.n)
                            for mc in range(2):
                                ins = t.matmul(S_ps[h][:, 256 * mc:256 * (mc + 1)],
                                               kn_tok[r0:r1, T, DV * h + 128 * mc:DV * h + 128 * (mc + 1)],
                                               Xu[r0:r1, T, h, 0:DV],
                                               start=False, stop=False,
                                               tile_position=(r0, 0), skip_group_check=True)
                            cT.tick(ins)
                        cast_mms.append(cT.n)
                        scast[h] = (cS, cast_base + 4 * c + h + 1)

                delo_mms = []
                if own:
                    for h in range(NH):
                        obank = pA if h < 2 else pB
                        ocol = 256 * (h % 2)

                        @E("tensor")
                        def _(t, T=T, h=h, obank=obank, ocol=ocol):
                            t.wait_ge(cV.sem, cV.n)
                            cT.tick(t.matmul(obank[:, ocol:ocol + 256], attnT[:, NH * T + h, :],
                                             Xu[:, T, h, 0:DV], start=False, stop=(h % 2 == 1),
                                             skip_group_check=True))
                        if h % 2 == 1:
                            delo_mms.append((cT.n, h))

                @E("scalar")
                def _(s, T=T, mms=tuple(cast_mms), dmm=tuple(delo_mms)):
                    for idx, mmv in enumerate(mms):
                        h = idx % NH
                        s.wait_ge(cT.sem, mmv)
                        cS.tick(s.copy(S_bf[:, h, :], S_ps[h][:]))
                    for mmv, h in dmm:
                        s.wait_ge(cT.sem, mmv)
                        cS.tick(s.copy(TOKA[:, T, 512 * (h // 2):512 * (h // 2 + 1)],
                                       (pA if h < 2 else pB)[:]))
                assert cS.n == cast_base + len(cast_mms) + len(delo_mms)
            scan_done = cS.n
        scan_free = scan_done
        if not own:
            x_free[buf] = (cS, scan_done)

        if taps:
            @E("sync")
            def _(sy, i=i, h_done=h_done, tok_done=tok_done, beta_done=beta_done,
                  norm_done=norm_done, scan_done=scan_done, own=own):
                rr = lambda d: d.rearrange("(T p) d -> p T d", p=128)
                tap_rows = slice(SEG * i, SEG * (i + 1))
                if "h" in dbg:
                    sy.wait_ge(cV.sem, h_done)
                    dma(sy, dS, rr(dbg["h"][tap_rows, :]), h_bf[:])
                if "k" in dbg:
                    sy.wait_ge(cV.sem, tok_done)
                    dma(sy, dS, rr(dbg["k"][tap_rows, :]), TOKB[:])
                if "v" in dbg:
                    sy.wait_ge(cV.sem, tok_done)
                    dma(sy, dS, rr(dbg["v"][tap_rows, :]), v_tok[:])
                if "q" in dbg and own:
                    sy.wait_ge(cV.sem, tok_done)
                    dma(sy, dS, rr(dbg["q"][tap_rows, :]), TOKA[:])
                if "beta" in dbg:
                    sy.wait_ge(cS.sem, beta_done)
                    dma(sy, dS, dbg["beta"][tap_rows, :].rearrange("(T p) h -> p T h", p=128), beta_sb[:])
                if "kn" in dbg:
                    sy.wait_ge(cV.sem, norm_done)
                    dma(sy, dS, rr(dbg["kn"][tap_rows, :]), kn_tok[:])
                if "qn" in dbg and own:
                    sy.wait_ge(cV.sem, norm_done)
                    dma(sy, dS, rr(dbg["qn"][tap_rows, :]), TOKC[:])
                if "uw" in dbg:
                    sy.wait_ge(cS.sem, scan_done)
                    dma(sy, dS, dbg["uw"][2 * i:2 * i + 2].rearrange("T p h d -> p T h d"), Xu[:])
                if "delta_o" in dbg and own:
                    sy.wait_ge(cS.sem, scan_done)
                    dma(sy, dS, rr(dbg["delta_o"][tap_rows, :]), TOKA[:])

        if not own:
            w1_ld[i + 1] = w1_ld[i]
            continue

        with nc.psum_tensor("pTg%d" % i, [128, 1024], BF16) as pTg, \
             nc.psum_tensor("pCg%d" % i, [128, 512], F32) as pCg:
            for T in range(NT):
                @E("tensor")
                def _(t, T=T, scan_done=scan_done):
                    t.wait_ge(cS.sem, scan_done)
                    t.wait_ge(cV.sem, cV.n)
                    for hh in range(8):
                        h, half = hh // 2, hh % 2
                        ins = t.transpose(pTg[:, 128 * hh:128 * (hh + 1)],
                                          TOKA[:, T, DV * h + 128 * half:DV * h + 128 * (half + 1)],
                                          ident[:])
                    cT.tick(ins)

                @E("vector")
                def _(v, T=T):
                    v.wait_ge(cT.sem, cT.n)
                    cV.tick(v.tensor_copy(qTc[:, :, 128 * T:128 * (T + 1)],
                                          pTg[:].rearrange("p (k t) -> p k t", t=128)))
            delot_done = cV.n

            for dstT, dstTok, fc in [(knT, TOKB, firrec["c1"]), (kbnT, TOKC, firrec["c2"])]:
                for T in range(NT):
                    @E("tensor")
                    def _(t, dstT=dstT, T=T, fc=fc):
                        t.wait_ge(cS.sem, fc)
                        t.wait_ge(cV.sem, cV.n)
                        for k in range(8):
                            ins = t.transpose(pTg[:, 128 * k:128 * (k + 1)],
                                              dstT[:, k, 128 * T:128 * (T + 1)], ident[:])
                        cT.tick(ins)

                    @E("vector")
                    def _(v, dstTok=dstTok, T=T):
                        v.wait_ge(cT.sem, cT.n)
                        cV.tick(v.tensor_copy(dstTok[:, T, :],
                                              pTg[:].rearrange("p (k t) -> p k t", t=128)))
            fir_done = cV.n

            stat_ticks = []
            for T in range(NT):
                for h in range(NH):
                    @E("vector")
                    def _(v, T=T, h=h, fir_done=fir_done, scan_done=scan_done):
                        if T == 0 and h == 0:
                            v.wait_ge(cV.sem, fir_done)
                            v.wait_ge(cS.sem, scan_done)
                        else:
                            v.wait_ge(cV.sem, cV.n)
                        for bi, bsrc in enumerate([TOKB, TOKC, TOKA, v_tok]):
                            sl = bsrc[:, T, DV * h:DV * (h + 1)]
                            v.reduce_sum(statv[:, T, h, 3 * bi:3 * bi + 1], sl, axis=AX.X)
                            ins = v.tensor_reduce(statv[:, T, h, 3 * bi + 2:3 * bi + 3], sl,
                                                  axis=AX.X, op=ALU.max)
                        cV.tick(ins)
                    stat_ticks.append(cV.n)

            @E("scalar")
            def _(s, ticks=tuple(stat_ticks)):
                for idx, tv in enumerate(ticks):
                    T, h = idx // NH, idx % NH
                    s.wait_ge(cV.sem, tv)
                    for bi, bsrc in enumerate([TOKB, TOKC, TOKA, v_tok]):
                        ins = s.activation(sqd[:, 0:DV], bsrc[:, T, DV * h:DV * (h + 1)],
                                           AF.Square, accum_out=statv[:, T, h, 3 * bi + 1:3 * bi + 2])
                    cS.tick(ins)
            sq_done = cS.n

            @E("vector")
            def _(v, sq_done=sq_done):
                v.wait_ge(cS.sem, sq_done)
                rmsv = statv[:].rearrange("p T h d -> p (T h) d")[:, :, 1:11:3]
                cV.tick(v.tensor_scalar(rmsv, rmsv, 1.0 / DV, 1e-8, ALU.mult, ALU.max))

            @E("scalar")
            def _(s):
                s.wait_ge(cV.sem, cV.n)
                rmsv = statv[:].rearrange("p T h d -> p (T h) d")[:, :, 1:11:3]
                cS.tick(s.activation(rmsv, rmsv, AF.Sqrt))

            @E("vector")
            def _(v):
                v.wait_ge(cS.sem, cS.n)
                cV.tick(v.memset(stat_s[:, 31:32], 0.0))
            stats_done = cV.n

            for T in range(NT):
                for h in range(NH):
                    @E("tensor")
                    def _(t, T=T, h=h):
                        t.wait_ge(cV.sem, cV.n)
                        cT.tick(t.transpose(pCg[0:12, 0:128], statv[:, T, h, :], identf[:]))

                    @E("vector")
                    def _(v, T=T, h=h):
                        v.wait_ge(cT.sem, cT.n)
                        cV.tick(v.tensor_copy(svT[0:12, h, 128 * T:128 * (T + 1)], pCg[0:12, 0:128]))
            svt_done = cV.n

            g_base = cS.n
            gelu_mms = []
            for h in range(NH):
                for Mc in range(4):
                    idx = 4 * h + Mc
                    @E("tensor")
                    def _(t, i=i, h=h, Mc=Mc, svt_done=svt_done, idx=idx, g_base=g_base):
                        if idx == 0:
                            t.wait_ge(cV.sem, svt_done)
                            t.wait_ge(dGm.sem, w4_ld[i])
                            t.wait_ge(dS.sem, w1s_dma)
                        if idx >= 2:
                            t.wait_ge(cS.sem, g_base + idx - 1)
                        pg = pA if Mc % 2 == 0 else pB
                        t.matmul(pg[:, 0:256], W_ring[:, 8 + 0, 128 * Mc:128 * (Mc + 1)], hT[:, 0, :],
                                 start=True, stop=False)
                        for k in range(1, 8):
                            t.matmul(pg[:, 0:256], W_ring[:, 8 + k, 128 * Mc:128 * (Mc + 1)],
                                     hT[:, k, :], start=False, stop=False)
                        for bi, bT in enumerate([knT, kbnT, qTc, vTc[i % 2]]):
                            for half in range(2):
                                t.matmul(pg[:, 0:256], W_ring[:, 16 + 2 * bi + half, 128 * Mc:128 * (Mc + 1)],
                                         bT[:, 2 * h + half, :], start=False, stop=False)
                        t.matmul(pg[:, 0:256], W1s[0:12, 128 * Mc:128 * (Mc + 1)], svT[0:12, h, :],
                                 start=False, stop=False)
                        cT.tick(t.matmul(pg[:, 0:256], gb1_b[0:1, 128 * Mc:128 * (Mc + 1)],
                                         ones1[0:1, 0:SEG], start=False, stop=True))
                    gelu_mms.append(cT.n)

            @E("scalar")
            def _(s, mms=tuple(gelu_mms)):
                for idx, mmv in enumerate(mms):
                    h, Mc = idx // 4, idx % 4
                    s.wait_ge(cT.sem, mmv)
                    pg = pA if Mc % 2 == 0 else pB
                    cS.tick(s.activation(hmG[:, h, Mc, :], pg[:, 0:256], AF.Gelu))
            gelu_all = cS.n
            assert gelu_all == g_base + 16

            for h in range(NH):
                for T in range(NT):
                    @E("tensor")
                    def _(t, h=h, T=T, gelu_all=gelu_all):
                        if h == 0 and T == 0:
                            t.wait_ge(cS.sem, gelu_all)
                            t.wait_ge(cV.sem, cV.n)
                        g = NT * h + T
                        for Mc in range(4):
                            t.matmul(pCg[:, 4 * g:4 * (g + 1)], hmG[:, h, Mc, 128 * T:128 * (T + 1)],
                                     gW2h[:, Mc, 4 * h:4 * (h + 1)], start=(Mc == 0), stop=False,
                                     skip_group_check=True)
                        cT.tick(t.matmul(pCg[:, 4 * g:4 * (g + 1)], ones1[0:1, 128 * T:128 * (T + 1)],
                                         gb2h[0:1, 4 * h:4 * (h + 1)], start=False, stop=True,
                                         skip_group_check=True))
            logits_all = cT.n

            @E("vector")
            def _(v, logits_all=logits_all):
                v.wait_ge(cT.sem, logits_all)
                cV.tick(v.tensor_reduce(stat_s[:, 0:8], pCg[:, 0:4 * NT * NH]
                                        .rearrange("p (g e) -> p g e", e=4),
                                        axis=AX.X, op=ALU.max, negate=True))
            negmax_all = cV.n

            exp_base = cS.n

            @E("scalar")
            def _(s, negmax_all=negmax_all):
                s.wait_ge(cV.sem, negmax_all)
                for g in range(NT * NH):
                    cS.tick(s.activation(ez_sb[:, g, :], pCg[:, 4 * g:4 * (g + 1)], AF.Exp,
                                         bias=stat_s[:, g:g + 1], accum_out=stat_s[:, 8 + g:9 + g]))
            exp_all = cS.n

            for h in range(NH):
                for T in range(NT):
                    g = NT * h + T

                    @E("vector")
                    def _(v, T=T, h=h, g=g, exp_base=exp_base):
                        v.wait_ge(cS.sem, exp_base + g + 1)
                        v.reciprocal(stat_s[:, 16:17], stat_s[:, 8 + g:9 + g])
                        v.drain()
                        v.tensor_scalar(ez_sb[:, g, :], ez_sb[:, g, :], stat_s[:, 16:17], None, ALU.mult)
                        v.drain()
                        v.tensor_tensor(ez_sb[:, g, :], ez_sb[:, g, :], floorb[:, 4 * h:4 * (h + 1)], ALU.max)
                        v.drain()
                        v.reduce_sum(stat_s[:, 17:18], ez_sb[:, g, :], axis=AX.X)
                        v.drain()
                        v.reciprocal(stat_s[:, 18:19], stat_s[:, 17:18])
                        v.drain()
                        cV.tick(v.tensor_scalar(wgt[:, T, h, :], ez_sb[:, g, :], stat_s[:, 18:19],
                                                None, ALU.mult))
            gate_done = cV.n

            mix_ticks = []
            for T in range(NT):
                for h in range(NH):
                    @E("vector")
                    def _(v, T=T, h=h, gate_done=gate_done):
                        v.wait_ge(cV.sem, gate_done if (T == 0 and h == 0) else cV.n)
                        dst = SCR[:, T, DV * h:DV * (h + 1)]
                        v.tensor_scalar(dst, TOKB[:, T, DV * h:DV * (h + 1)], wgt[:, T, h, 0:1],
                                        None, ALU.mult)
                        v.scalar_tensor_tensor(dst, TOKC[:, T, DV * h:DV * (h + 1)], wgt[:, T, h, 1:2],
                                               dst, ALU.mult, ALU.add)
                        v.scalar_tensor_tensor(dst, TOKA[:, T, DV * h:DV * (h + 1)], wgt[:, T, h, 2:3],
                                               dst, ALU.mult, ALU.add)
                        cV.tick(v.scalar_tensor_tensor(dst, v_tok[:, T, DV * h:DV * (h + 1)],
                                                       wgt[:, T, h, 3:4], dst, ALU.mult, ALU.add))
                    mix_ticks.append(cV.n)

            @E("scalar")
            def _(s, ticks=tuple(mix_ticks)):
                for idx, tv in enumerate(ticks):
                    T, h = idx // NH, idx % NH
                    s.wait_ge(cV.sem, tv)
                    cS.tick(s.activation(sqd[:, 0:DV], SCR[:, T, DV * h:DV * (h + 1)], AF.Square,
                                         accum_out=stat_s[:, 16 + 4 * T + h:17 + 4 * T + h]))
            rms_sq = cS.n

            @E("vector")
            def _(v, rms_sq=rms_sq):
                v.wait_ge(cS.sem, rms_sq)
                cV.tick(v.tensor_scalar(stat_s[:, 16:24], stat_s[:, 16:24], 1.0 / DV, 1e-5,
                                        ALU.mult, ALU.add))

            @E("scalar")
            def _(s):
                s.wait_ge(cV.sem, cV.n)
                cS.tick(s.activation(stat_s[:, 16:24], stat_s[:, 16:24], AF.Sqrt))

            @E("vector")
            def _(v):
                v.wait_ge(cS.sem, cS.n)
                v.reciprocal(stat_s[:, 16:24], stat_s[:, 16:24])
                v.drain()
                for T in range(NT):
                    for h in range(NH):
                        ins = v.scalar_tensor_tensor(h_bf[:, T, DV * h:DV * (h + 1)],
                                                     SCR[:, T, DV * h:DV * (h + 1)],
                                                     stat_s[:, 16 + 4 * T + h:17 + 4 * T + h],
                                                     onormb[:], ALU.mult, ALU.mult)
                cV.tick(ins)
            mix_done = cV.n

            for T in range(NT):
                @E("tensor")
                def _(t, T=T, mix_done=mix_done):
                    t.wait_ge(cV.sem, mix_done if T == 0 else cV.n)
                    for k in range(8):
                        ins = t.transpose(pTg[:, 128 * k:128 * (k + 1)],
                                          h_bf[:, T, 128 * k:128 * (k + 1)], ident[:])
                    cT.tick(ins)

                @E("vector")
                def _(v, T=T):
                    v.wait_ge(cT.sem, cT.n)
                    cV.tick(v.tensor_copy(kTc[:, :, 128 * T:128 * (T + 1)],
                                          pTg[:].rearrange("p (k t) -> p k t", t=128)))
            oT_done = cV.n

            for T in range(NT):
                for n in range(2):
                    @E("tensor")
                    def _(t, T=T, n=n, oT_done=oT_done):
                        t.wait_ge(cV.sem, oT_done if (T == 0 and n == 0) else cV.n)
                        pg = pA if n == 0 else pB
                        for k in range(8):
                            ins = t.matmul(pg[:], kTc[:, k, 128 * T:128 * (T + 1)],
                                           W_ring[:, 0 + k, 512 * n:512 * (n + 1)],
                                           start=(k == 0), stop=(k == 7))
                        cT.tick(ins)

                    @E("vector")
                    def _(v, T=T, n=n, i=i):
                        v.wait_ge(cT.sem, cT.n)
                        pg = pA if n == 0 else pB
                        dst = SCR[:, T, 512 * n:512 * (n + 1)]
                        v.tensor_tensor(dst, pg[:], ga_b[:, 512 * n:512 * (n + 1)], ALU.mult)
                        cV.tick(v.tensor_tensor(dst, dst, x_sb[i % 2][:, T, 512 * n:512 * (n + 1)],
                                                ALU.add))
            xmid_done = cV.n
            x_free[buf] = (cV, xmid_done)

            @E("vector")
            def _(v, xmid_done=xmid_done):
                v.wait_ge(cV.sem, xmid_done)
                v.reduce_sum(stat_s[:, 0:1], SCR[:, 0, :], axis=AX.X)
                cV.tick(v.reduce_sum(stat_s[:, 1:2], SCR[:, 1, :], axis=AX.X))

            @E("scalar")
            def _(s):
                s.wait_ge(cV.sem, cV.n)
                s.activation(sqd[:], SCR[:, 0, :], AF.Square, accum_out=stat_s[:, 2:3])
                cS.tick(s.activation(sqd[:], SCR[:, 1, :], AF.Square, accum_out=stat_s[:, 3:4]))

            @E("vector")
            def _(v):
                v.wait_ge(cS.sem, cS.n)
                v.tensor_scalar(stat_s[:, 4:6], stat_s[:, 0:2], 1.0 / HID, None, ALU.mult)
                v.drain()
                v.tensor_tensor(stat_s[:, 6:8], stat_s[:, 4:6], stat_s[:, 4:6], ALU.mult)
                v.tensor_scalar(stat_s[:, 8:10], stat_s[:, 2:4], 1.0 / HID, 1e-5, ALU.mult, ALU.add)
                v.drain()
                cV.tick(v.tensor_tensor(stat_s[:, 8:10], stat_s[:, 8:10], stat_s[:, 6:8], ALU.subtract))

            @E("scalar")
            def _(s):
                s.wait_ge(cV.sem, cV.n)
                cS.tick(s.activation(stat_s[:, 10:12], stat_s[:, 8:10], AF.Sqrt))

            @E("vector")
            def _(v):
                v.wait_ge(cS.sem, cS.n)
                v.reciprocal(stat_s[:, 12:14], stat_s[:, 10:12])
                v.drain()
                for T in range(NT):
                    v.tensor_scalar(sqd[:], SCR[:, T, :], stat_s[:, 4 + T:5 + T],
                                    stat_s[:, 12 + T:13 + T], ALU.subtract, ALU.mult)
                    v.tensor_tensor(sqd[:], sqd[:], cb1_b[:], ALU.mult)
                    ins = v.tensor_tensor(TOKC[:, T, :], sqd[:], sbv_b[:], ALU.add)
                cV.tick(ins)
            h2_done = cV.n

            @E("sync")
            def _(sy, orow=orow, h2_done=h2_done):
                sy.wait_ge(cV.sem, h2_done)
                dma(sy, dS, d_xmid[orow:orow + SEG, :].rearrange("(T p) d -> p T d", p=128), SCR[:])
                dma(sy, dS, d_h2[orow:orow + SEG, :].rearrange("(T p) d -> p T d", p=128), TOKC[:])

            if taps:
                @E("sync")
                def _(sy, orow=orow, i=i, fir_done=fir_done, stats_done=stats_done,
                      gate_done=gate_done, mix_done=mix_done, xmid_done=xmid_done, h2_done=h2_done):
                    rr = lambda d: d.rearrange("(T p) d -> p T d", p=128)
                    if "fir_s" in dbg:
                        sy.wait_ge(cV.sem, fir_done)
                        dma(sy, dS, rr(dbg["fir_s"][orow:orow + SEG, :]), TOKB[:])
                    if "fir_l" in dbg:
                        sy.wait_ge(cV.sem, fir_done)
                        dma(sy, dS, rr(dbg["fir_l"][orow:orow + SEG, :]), TOKC[:])
                    if "stat" in dbg:
                        sy.wait_ge(cV.sem, stats_done)
                        dma(sy, dS, dbg["stat"][2 * (i - own0):2 * (i - own0) + 2]
                            .rearrange("T p h d -> p T h d"), statv[:])
                    if "wgt" in dbg:
                        sy.wait_ge(cV.sem, gate_done)
                        dma(sy, dS, dbg["wgt"][2 * (i - own0):2 * (i - own0) + 2]
                            .rearrange("T p h e -> p T h e"), wgt[:])
                    if "o_mix" in dbg:
                        sy.wait_ge(cV.sem, mix_done)
                        dma(sy, dS, rr(dbg["o_mix"][orow:orow + SEG, :]), h_bf[:])
                    if "x_mid" in dbg:
                        sy.wait_ge(cV.sem, xmid_done)
                        dma(sy, dS, rr(dbg["x_mid"][orow:orow + SEG, :]), SCR[:])
                    if "h2" in dbg:
                        sy.wait_ge(cV.sem, h2_done)
                        dma(sy, dS, rr(dbg["h2"][orow:orow + SEG, :]), TOKC[:])

            if i + 1 < nseg:
                @E("gpsimd")
                def _(g):
                    g.wait_ge(cT.sem, cT.n)
                    dma(g, dGm, W_ring[:, 0:8, :], KT(dWq))
                    dma(g, dGm, W_ring[:, 8:16, :], KT(dWk))
                    dma(g, dGm, W_ring[:, 16:24, :], KT(dWv))
                w1_ld[i + 1] = dGm.n

    paT, paV, paS, paG, paD, paGm = cT.n, cV.n, cS.n, cG.n, dS.n, dGm.n
    pa.close()

    pb = contextlib.ExitStack()

    def sbb(name, shape, dt=F32):
        return pb.enter_context(nc.sbuf_tensor(name, shape, dt))

    h2T = sbb("h2T", [128, 8, PB_TOK], BF16)
    acc = sbb("acc", [128, PB_TOK // 128, HID])
    wb1 = [sbb("wb1_0", [128, 8, EI], BF16), sbb("wb1_1", [128, 8, EI], BF16)]
    wb3 = [sbb("wb3_0", [128, 8, EI], BF16), sbb("wb3_1", [128, 8, EI], BF16)]
    wb2 = [sbb("wb2_0", [128, 6, HID], BF16), sbb("wb2_1", [128, 6, HID], BF16)]
    SCB1 = sbb("SCB1", [128, PB_TOK // 128, NE])
    SCB2 = sbb("SCB2", [128, PB_TOK // 128, NE])
    mrscr = sbb("mrscr", [128, NE])
    mx8 = sbb("mx8", [128, 8])
    aT = sbb("aT", [128, 6, PB_CH], BF16)
    gtmp = sbb("gtmp", [128, 2, PB_CH], BF16)
    yscr = sbb("yscr", [128, 2, 512])
    xmb = sbb("xmb", [128, 2, HID])
    obuf = sbb("obuf", [128, 2, HID])

    pG = [pb.enter_context(nc.psum_tensor("pG0", [128, 512], F32)),
          pb.enter_context(nc.psum_tensor("pG1", [128, 512], F32))]
    pU = [pb.enter_context(nc.psum_tensor("pU0", [128, 512], F32)),
          pb.enter_context(nc.psum_tensor("pU1", [128, 512], F32))]
    pO = [pb.enter_context(nc.psum_tensor("pO0", [128, 512], F32)),
          pb.enter_context(nc.psum_tensor("pO1", [128, 512], F32))]

    NPT = PB_TOK // 128
    exp_seq = []
    gu_free = {}
    po_free = {}
    at_free = {0: 0}

    for p in range(npass):
        pbase = p * PB_TOK

        @E("scalar")
        def _(s, p=p):
            if p == 0:
                s.wait_ge(cT.sem, paT)
                s.wait_ge(cV.sem, paV)
                s.wait_ge(cG.sem, paG)
                s.wait_ge(dS.sem, paD)
                s.wait_ge(dGm.sem, paGm)
            else:
                s.wait_ge(cV.sem, cV.n)
            for k in range(8):
                dma(s, dA, h2T[:, k, :], d_h2[pbase:pbase + PB_TOK, 128 * k:128 * (k + 1)],
                    transpose=True)
        h2t_ld = dA.n

        @E("tensor")
        def _(t, p=p):
            t.wait_ge(dA.sem, h2t_ld)
            if p == 0:
                t.wait_ge(cV.sem, paV)
                t.wait_ge(cS.sem, paS)
            for tt in range(NPT):
                for k in range(8):
                    ins = t.matmul(pO[0][:, NE * tt:NE * (tt + 1)], h2T[:, k, 128 * tt:128 * (tt + 1)],
                                   gatewT[:, k, :], start=(tt == 0 and k == 0),
                                   stop=(tt == NPT - 1 and k == 7))
            cT.tick(ins)

        @E("scalar")
        def _(s):
            s.wait_ge(cT.sem, cT.n)
            cS.tick(s.activation(SCB1[:].rearrange("p T e -> p (T e)"), pO[0][:, 0:NE * NPT], AF.Sigmoid))

        @E("vector")
        def _(v, p=p):
            v.wait_ge(cS.sem, cS.n)
            if p == 0:
                v.wait_ge(cG.sem, paG)
                v.wait_ge(dGm.sem, paGm)
            v.memset(mx8[:, 1:8], 3.0e38)
            for tt in range(NPT):
                v.tensor_tensor(SCB2[:, tt, :], SCB1[:, tt, :], ebias_b[:], ALU.add)
                v.drain()
                for r in range(4):
                    src = SCB2[:, tt, :] if r % 2 == 0 else mrscr[:]
                    dstm = mrscr[:] if r % 2 == 0 else SCB2[:, tt, :]
                    v.reduce_max(mx8[:, 0:1], src, axis=AX.X)
                    v.drain()
                    v.match_replace(dstm, mx8[:], src, -1e30)
                    v.drain()
                v.tensor_scalar(mrscr[:], SCB2[:, tt, :], -1e29, None, ALU.is_le)
                v.drain()
                v.tensor_tensor(mrscr[:], mrscr[:], SCB1[:, tt, :], ALU.mult)
                v.drain()
                v.reduce_sum(mx8[:, 0:1], mrscr[:], axis=AX.X)
                v.drain()
                v.reciprocal(mx8[:, 1:2], mx8[:, 0:1])
                v.drain()
                v.tensor_scalar(SCB1[:, tt, :], mrscr[:], mx8[:, 1:2], None, ALU.mult)
                ins = v.memset(mx8[:, 1:8], 3.0e38)
            cV.tick(ins)
        comb_done = cV.n

        if "combine" in dbg:
            @E("sync")
            def _(sy, pbase=pbase, comb_done=comb_done):
                sy.wait_ge(cV.sem, comb_done)
                dma(sy, dS, dbg["combine"][pbase:pbase + PB_TOK, :]
                    .rearrange("(T p) e -> p T e", p=128), SCB1[:])

        @E("vector")
        def _(v, comb_done=comb_done):
            v.wait_ge(cV.sem, comb_done)
            cV.tick(v.memset(acc[:], 0.0))
        acc_zero = cV.n

        for e in range(nexp):
            ebuf = e % 2

            @E("gpsimd")
            def _(g, p=p, e=e, ebuf=ebuf):
                if p == 0 and e == 0:
                    g.wait_ge(cV.sem, paV)
                    g.wait_ge(cS.sem, paS)
                    g.wait_ge(cT.sem, paT)
                    g.wait_ge(dS.sem, paD)
                if len(exp_seq) >= 2:
                    g.wait_ge(cT.sem, exp_seq[-2])
                dma(g, dGm, wb1[ebuf][:].rearrange("p k i -> p (k i)"), dw1[e])
                dma(g, dGm, wb3[ebuf][:].rearrange("p k i -> p (k i)"), dw3[e])
                dma(g, dGm, wb2[ebuf][:].rearrange("p k i -> p (k i)"), dw2[e])
            wexp = dGm.n

            for ch in range(PB_TOK // PB_CH):
                cbase = PB_CH * ch
                s_base = cS.n
                ch_mms = []
                for ic in range(6):
                    gp = ic % 2

                    @E("tensor")
                    def _(t, e=e, ebuf=ebuf, ic=ic, gp=gp, cbase=cbase, wexp=wexp, acc_zero=acc_zero):
                        t.wait_ge(dGm.sem, wexp)
                        t.wait_ge(cV.sem, max(acc_zero, gu_free.get(gp, 0)))
                        for k in range(8):
                            t.matmul(pG[gp][:], wb1[ebuf][:, k, 128 * ic:128 * (ic + 1)],
                                     h2T[:, k, cbase:cbase + PB_CH], start=(k == 0), stop=(k == 7))
                        for k in range(8):
                            ins = t.matmul(pU[gp][:], wb3[ebuf][:, k, 128 * ic:128 * (ic + 1)],
                                           h2T[:, k, cbase:cbase + PB_CH], start=(k == 0), stop=(k == 7))
                        cT.tick(ins)
                    ch_mms.append(cT.n)

                    @E("vector")
                    def _(v, ic=ic, gp=gp, w=s_base + ic + 1):
                        v.wait_ge(cS.sem, w)
                        if ic == 0:
                            v.wait_ge(cT.sem, at_free[0])
                        cV.tick(v.tensor_tensor(aT[:, ic, :], gtmp[:, gp, :], pU[gp][:], ALU.mult))
                    gu_free[gp] = cV.n
                a_done = cV.n

                @E("scalar")
                def _(s, mms=tuple(ch_mms)):
                    for ic, mmv in enumerate(mms):
                        s.wait_ge(cT.sem, mmv)
                        cS.tick(s.activation(gtmp[:, ic % 2, :], pG[ic % 2][:], AF.Silu))
                assert cS.n == s_base + 6

                for Mt in range(PB_CH // 128):
                    tt = (cbase // 128) + Mt
                    for n in range(2):
                        ob = (Mt * 2 + n) % 2

                        @E("tensor")
                        def _(t, ebuf=ebuf, Mt=Mt, n=n, ob=ob, a_done=a_done):
                            t.wait_ge(cV.sem, max(a_done, po_free.get(ob, 0)))
                            for kc in range(6):
                                ins = t.matmul(pO[ob][:], aT[:, kc, 128 * Mt:128 * (Mt + 1)],
                                               wb2[ebuf][:, kc, 512 * n:512 * (n + 1)],
                                               start=(kc == 0), stop=(kc == 5))
                            cT.tick(ins)
                        w2_done = cT.n
                        at_free[0] = cT.n

                        @E("vector")
                        def _(v, e=e, tt=tt, n=n, ob=ob, w2_done=w2_done):
                            v.wait_ge(cT.sem, w2_done)
                            dst = acc[:, tt, 512 * n:512 * (n + 1)]
                            cV.tick(v.scalar_tensor_tensor(dst, pO[ob][:], SCB1[:, tt, e:e + 1],
                                                           dst, ALU.mult, ALU.add))
                        po_free[ob] = cV.n
            exp_seq.append(cT.n)
        moe_done = cV.n

        for g2 in range(NPT // 2):
            @E("sync")
            def _(sy, pbase=pbase, g2=g2, moe_done=moe_done):
                sy.wait_ge(cV.sem, moe_done if g2 == 0 else cV.n)
                r0 = pbase + 256 * g2
                dma(sy, dS, xmb[:], d_xmid[r0:r0 + 256, :].rearrange("(T p) d -> p T d", p=128))
            xm_ld = dS.n

            @E("vector")
            def _(v, g2=g2, xm_ld=xm_ld):
                v.wait_ge(dS.sem, xm_ld)
                for T in range(2):
                    tt = 2 * g2 + T
                    v.tensor_tensor(obuf[:, T, :], acc[:, tt, :], gb_b[:], ALU.mult)
                    ins = v.tensor_tensor(obuf[:, T, :], obuf[:, T, :], xmb[:, T, :], ALU.add)
                cV.tick(ins)

            @E("sync")
            def _(sy, pbase=pbase, g2=g2):
                sy.wait_ge(cV.sem, cV.n)
                r0 = pbase + 256 * g2
                dma(sy, dS, dout[r0:r0 + 256, :].rearrange("(T p) d -> p T d", p=128), obuf[:])

    @E("sync")
    def _(sy):
        sy.wait_ge(dS.sem, dS.n)

    pb.close()
    es.close()
    return nc


def _host_constants():
    import ml_dtypes
    ident = np.eye(128, dtype=ml_dtypes.bfloat16)
    identf = np.eye(128, dtype=np.float32)
    mA = np.zeros((128, 128), np.float32)
    mAU = np.zeros((128, 128), np.float32)
    mT = np.zeros((128, 128), np.float32)
    for c in range(4):
        s = slice(32 * c, 32 * (c + 1))
        tril = np.tril(np.ones((32, 32), np.float32), -1)
        mA[s, s] = -tril
        mAU[s, s] = -tril.T
        mT[s, s] = np.triu(np.ones((32, 32), np.float32))
    return ident, identf, mA, mAU, mT


def make_in_maps(inp, taps=()):
    ident, identf, mA, mAU, mT = _host_constants()
    f32 = lambda a: np.ascontiguousarray(np.asarray(a, np.float32))
    conv_taps = np.stack([f32(inp["L0_attn_qc_w"]), f32(inp["L0_attn_kc_w"]),
                          f32(inp["L0_attn_vc_w"])])
    convt = np.ascontiguousarray(conv_taps.reshape(3, 8, 128, 4).transpose(2, 0, 1, 3))
    firs = np.ascontiguousarray(f32(inp["L0_attn_fir_s"]).reshape(1024, 3)
                                .reshape(8, 128, 3).transpose(1, 0, 2))
    firl = np.ascontiguousarray(f32(inp["L0_attn_fir_l"]).reshape(1024, 31)
                                .reshape(8, 128, 31).transpose(1, 0, 2))
    shared = {
        "Wq": f32(inp["L0_attn_Wq"]), "Wk": f32(inp["L0_attn_Wk"]),
        "Wv": f32(inp["L0_attn_Wv"]), "Wb": f32(inp["L0_attn_Wb"]),
        "Wo": f32(inp["L0_attn_Wo"]),
        "convtaps": convt, "firs": firs, "firl": firl,
        "gW1": f32(inp["L0_attn_gW1"]), "gb1": f32(inp["L0_attn_gb1"]).reshape(1, 512),
        "gW2": f32(inp["L0_attn_gW2"]), "gb2": f32(inp["L0_attn_gb2"]).reshape(1, 4),
        "temp": f32(inp["L0_attn_temp"]).reshape(1, 4),
        "floorv": f32(inp["L0_attn_eps_floor"]).reshape(1, 16),
        "onorm": f32(inp["L0_attn_onorm_w"]).reshape(1, DV),
        "wmod": f32(inp["L0_wmod"]),
        "gate_wT": np.ascontiguousarray(f32(inp["L0_ffn_gate_w"]).T),
        "e_bias": f32(inp["L0_ffn_e_bias"]).reshape(1, NE),
        "w1": np.ascontiguousarray(f32(inp["L0_ffn_w1"]).reshape(NE, 8, 128, EI)
                                   .transpose(0, 2, 1, 3).reshape(NE, 128, 8 * EI)),
        "w3": np.ascontiguousarray(f32(inp["L0_ffn_w3"]).reshape(NE, 8, 128, EI)
                                   .transpose(0, 2, 1, 3).reshape(NE, 128, 8 * EI)),
        "w2": np.ascontiguousarray(f32(inp["L0_ffn_w2"]).reshape(NE, 6, 128, HID)
                                   .transpose(0, 2, 1, 3).reshape(NE, 128, 6 * HID)),
        "identity": ident, "identityf": identf, "maskA": mA, "maskAU": mAU, "maskT": mT,
    }
    x = f32(inp["L0_attn_x"])
    cfull = f32(inp["L0_c"])
    in_maps = []
    for core in range(8):
        b, s = core // 2, core % 2
        xb = x[b] if s == 1 else np.ascontiguousarray(np.roll(x[b], HALF, axis=0))
        mask = np.ones((L,), np.float32)
        if s == 0:
            mask[:HALF] = 0.0
        m = dict(shared)
        m["x"] = xb
        m["tokmask"] = np.ascontiguousarray(mask.reshape(L // 128, 128).T)
        m["c_bT"] = np.ascontiguousarray(cfull[b].reshape(8, 128).T)
        in_maps.append(m)
    return in_maps


def kernel(**inp):
    nc = build_program()
    in_maps = make_in_maps(inp)
    res = run_bass_kernel_spmd(nc, in_maps, core_ids=list(range(8)))
    out = np.empty((4, L, HID), np.float32)
    for core in range(8):
        b, s = core // 2, core % 2
        out[b, HALF * s:HALF * (s + 1)] = res.results[core]["out"]
    return out


# version 08
# speedup vs baseline: 1.9459x; 1.0088x over previous
import numpy as np
import contextlib
import concourse.bass as bass
import concourse.mybir as mybir
from concourse.bass_utils import run_bass_kernel_spmd

F32 = mybir.dt.float32
BF16 = mybir.dt.bfloat16
AF = mybir.ActivationFunctionType
ALU = mybir.AluOpType
AX = mybir.AxisListType

HID, NH, DK, DV = 1024, 4, 256, 256
L, HALF = 4096, 2048
SEG = 256
NSEG = L // SEG
NT = SEG // 128
NE, EI = 32, 768
TOK = HALF
PB_TOK = 1024
PB_CH = 512


class Ctr:
    def __init__(self, sem):
        self.sem = sem
        self.n = 0

    def tick(self, instr, k=1):
        instr.then_inc(self.sem, k)
        self.n += k
        return self.n


def build_program(nseg=NSEG, nexp=NE, taps=(), own0=NSEG // 2, npass=2):
    nc = bass.Bass("TRN2", target_bir_lowering=False, detect_race_conditions=False)

    dx = nc.dram_tensor("x", [L, HID], F32, kind="ExternalInput").ap()
    dmask = nc.dram_tensor("tokmask", [128, L // 128], F32, kind="ExternalInput").ap()
    dWq = nc.dram_tensor("Wq", [HID, HID], F32, kind="ExternalInput").ap()
    dWk = nc.dram_tensor("Wk", [HID, HID], F32, kind="ExternalInput").ap()
    dWv = nc.dram_tensor("Wv", [HID, HID], F32, kind="ExternalInput").ap()
    dWb = nc.dram_tensor("Wb", [HID, NH], F32, kind="ExternalInput").ap()
    dWo = nc.dram_tensor("Wo", [HID, HID], F32, kind="ExternalInput").ap()
    dconv = nc.dram_tensor("convtaps", [128, 3, 8, 4], F32, kind="ExternalInput").ap()
    dfirs = nc.dram_tensor("firs", [128, 8, 3], F32, kind="ExternalInput").ap()
    dfirl = nc.dram_tensor("firl", [128, 8, 31], F32, kind="ExternalInput").ap()
    dgW1 = nc.dram_tensor("gW1", [5120, 512], F32, kind="ExternalInput").ap()
    dgb1 = nc.dram_tensor("gb1", [1, 512], F32, kind="ExternalInput").ap()
    dgW2 = nc.dram_tensor("gW2", [512, 4], F32, kind="ExternalInput").ap()
    dgb2 = nc.dram_tensor("gb2", [1, 4], F32, kind="ExternalInput").ap()
    dtemp = nc.dram_tensor("temp", [1, 4], F32, kind="ExternalInput").ap()
    dfloor = nc.dram_tensor("floorv", [1, 16], F32, kind="ExternalInput").ap()
    donorm = nc.dram_tensor("onorm", [1, DV], F32, kind="ExternalInput").ap()
    dcT = nc.dram_tensor("c_bT", [128, 8], F32, kind="ExternalInput").ap()
    dwmod = nc.dram_tensor("wmod", [HID, 6 * HID], F32, kind="ExternalInput").ap()
    dgatew = nc.dram_tensor("gate_wT", [HID, NE], F32, kind="ExternalInput").ap()
    debias = nc.dram_tensor("e_bias", [1, NE], F32, kind="ExternalInput").ap()
    dw1 = nc.dram_tensor("w1", [NE, 128, 8 * EI], F32, kind="ExternalInput").ap()
    dw3 = nc.dram_tensor("w3", [NE, 128, 8 * EI], F32, kind="ExternalInput").ap()
    dw2 = nc.dram_tensor("w2", [NE, 128, 6 * HID], F32, kind="ExternalInput").ap()
    dident = nc.dram_tensor("identity", [128, 128], BF16, kind="ExternalInput").ap()
    didentf = nc.dram_tensor("identityf", [128, 128], F32, kind="ExternalInput").ap()
    dmaskA = nc.dram_tensor("maskA", [128, 128], F32, kind="ExternalInput").ap()
    dmaskAU = nc.dram_tensor("maskAU", [128, 128], F32, kind="ExternalInput").ap()
    dmaskT = nc.dram_tensor("maskT", [128, 128], F32, kind="ExternalInput").ap()
    dout = nc.dram_tensor("out", [TOK, HID], F32, kind="ExternalOutput").ap()
    d_xmid = nc.dram_tensor("xmid_scr", [TOK, HID], F32, kind="Internal").ap()
    d_h2 = nc.dram_tensor("h2_scr", [TOK, HID], BF16, kind="Internal").ap()

    TAPSPEC = {
        "h": ([L, HID], BF16), "q": ([L, HID], BF16), "k": ([L, HID], BF16),
        "v": ([L, HID], BF16), "beta": ([L, NH], F32),
        "qn": ([L, HID], BF16), "kn": ([L, HID], BF16),
        "uw": ([L // 128, 128, NH, 512], BF16), "delta_o": ([L, HID], BF16),
        "fir_s": ([TOK, HID], BF16), "fir_l": ([TOK, HID], BF16),
        "stat": ([TOK // 128, 128, NH, 12], F32), "wgt": ([TOK // 128, 128, NH, 4], F32),
        "o_mix": ([TOK, HID], BF16), "x_mid": ([TOK, HID], F32), "h2": ([TOK, HID], BF16),
        "combine": ([TOK, NE], F32),
        "setup": ([8, 128, HID], F32),
    }
    dbg = {}
    for t in taps:
        shp, dt = TAPSPEC[t]
        dbg[t] = nc.dram_tensor("dbg_" + t, shp, dt, kind="ExternalOutput").ap()

    es = contextlib.ExitStack()

    def sbt(name, shape, dt=F32):
        return es.enter_context(nc.sbuf_tensor(name, shape, dt))

    def mksem(name):
        return Ctr(es.enter_context(nc.semaphore(name)))

    cT = mksem("cT")
    cV = mksem("cV")
    cS = mksem("cS")
    cG = mksem("cG")
    dS = mksem("dS")
    dGm = mksem("dGm")
    dA = mksem("dA")

    ident = sbt("ident", [128, 128], BF16)
    identf = sbt("identf", [128, 128], F32)
    maskA_s = sbt("maskA_s", [128, 128])
    maskAU_s = sbt("maskAU_s", [128, 128])
    maskT_s = sbt("maskT_s", [128, 128])
    convt = sbt("convt", [128, 3, 8, 4])
    firs_t = sbt("firs_t", [128, 8, 3])
    firl_t = sbt("firl_t", [128, 8, 31])
    mask_sb = sbt("mask_sb", [128, L // 128])
    Wb_s = sbt("Wb_s", [128, 8, NH], BF16)
    W1s = sbt("W1s", [12, 512], BF16)
    gW2h = sbt("gW2h", [128, 4, 4 * NH], BF16)
    gb2h = sbt("gb2h", [1, 4 * NH], BF16)
    gb1_b = sbt("gb1_b", [1, 512], BF16)
    floorb = sbt("floorb", [128, NH * 4])
    onormb = sbt("onormb", [128, DV])
    ebias_b = sbt("ebias_b", [128, NE])
    gatewT = sbt("gatewT", [128, 8, NE], BF16)
    invt_c = sbt("invt_c", [128, 4])
    ones1 = sbt("ones1", [1, 512], BF16)
    onec = sbt("onec", [128, 1], BF16)
    zrow = sbt("zrow", [1, 512], BF16)
    sa_b = sbt("sa_b", [128, HID], BF16)
    ca1_b = sbt("ca1_b", [128, HID], BF16)
    ga_b = sbt("ga_b", [128, HID], BF16)
    sbv_b = sbt("sbv_b", [128, HID], BF16)
    cb1_b = sbt("cb1_b", [128, HID], BF16)
    gb_b = sbt("gb_b", [128, HID], BF16)

    block = es.enter_context(nc.Block())

    def E(name):
        def deco(f):
            getattr(block, name)(f)
        return deco

    def dma(eng, ctr, out, in_, **kw):
        ctr.tick(eng.dma_start(out=out, in_=in_, **kw), 16)

    KT = lambda d: d.rearrange("(k p) e -> p k e", p=128)

    sc = contextlib.ExitStack()

    def sbl(name, shape, dt=F32):
        return sc.enter_context(nc.sbuf_tensor(name, shape, dt))

    modrow = sbl("modrow", [1, 6 * HID])
    modrow_b = sbl("modrow_b", [1, 6 * HID], BF16)
    c_sb = sbl("c_sb", [128, 8])
    silc = sbl("silc", [128, 8], BF16)
    wmbuf = sbl("wmbuf", [128, 2, 8, 512], BF16)
    g1stg = sbl("g1stg", [128, 2, 512], BF16)
    gW2f = sbl("gW2f", [128, 4, 4])
    rows_s = sbl("rows_s", [1, 64])
    rows_b = sbl("rows_b", [1, 64], BF16)
    onr = sbl("onr", [1, DV])
    onr_b = sbl("onr_b", [1, DV], BF16)
    ebr = sbl("ebr", [1, NE])
    ebr_b = sbl("ebr_b", [1, NE], BF16)
    clampt = sbl("clampt", [128, HID])
    clampt2 = sbl("clampt2", [128, HID])
    w1row = sbl("w1row", [1, 512], BF16)
    pSu = sc.enter_context(nc.psum_tensor("pSu", [128, 512], F32))
    pRu = sc.enter_context(nc.psum_tensor("pRu", [1, 512], F32))

    @E("sync")
    def _(sy):
        for o, i in [(ident[:], dident), (identf[:], didentf), (maskA_s[:], dmaskA),
                     (maskAU_s[:], dmaskAU), (maskT_s[:], dmaskT), (convt[:], dconv),
                     (firs_t[:], dfirs), (firl_t[:], dfirl), (mask_sb[:], dmask),
                     (c_sb[:], dcT), (gW2f[:], KT(dgW2)),
                     (rows_s[0:1, 0:4], dgb2), (rows_s[0:1, 4:8], dtemp),
                     (rows_s[0:1, 16:32], dfloor), (onr[:], donorm), (ebr[:], debias)]:
            dma(sy, dS, o, i)
    ld_smalls = dS.n

    @E("gpsimd")
    def _(g):
        dma(g, dGm, Wb_s[:], KT(dWb))
        dma(g, dGm, gatewT[:], KT(dgatew))
        dma(g, dGm, gb1_b[:], dgb1)
    n_pre_g = 3

    @E("vector")
    def _(v):
        v.memset(ones1[:], 1.0)
        v.memset(onec[:], 1.0)
        cV.tick(v.memset(zrow[:], 0.0))
    ones_done = cV.n

    @E("scalar")
    def _(s):
        s.wait_ge(dS.sem, ld_smalls)
        cS.tick(s.activation(silc[:], c_sb[:], AF.Silu))
    silc_done = cS.n

    mod_evac = {}
    for n in range(12):
        @E("gpsimd")
        def _(g, n=n):
            if n >= 2:
                g.wait_ge(cS.sem, mod_evac[n - 2])
            dma(g, dGm, wmbuf[:, n % 2], KT(dwmod[:, 512 * n:512 * (n + 1)]))

        @E("tensor")
        def _(t, n=n):
            t.wait_ge(dGm.sem, dGm.n)
            if n == 0:
                t.wait_ge(cS.sem, silc_done)
            else:
                t.wait_ge(cS.sem, mod_evac[n - 1])
            for k in range(8):
                ins = t.matmul(pRu[:], silc[:, k:k + 1], wmbuf[:, n % 2, k, :],
                               start=(k == 0), stop=(k == 7))
            cT.tick(ins)

        @E("scalar")
        def _(s, n=n):
            s.wait_ge(cT.sem, cT.n)
            cS.tick(s.copy(modrow[0:1, 512 * n:512 * (n + 1)], pRu[:]))
        mod_evac[n] = cS.n

    @E("vector")
    def _(v):
        v.wait_ge(cS.sem, mod_evac[11])
        cV.tick(v.tensor_copy(modrow_b[:], modrow[:]))
    modb_done = cV.n

    vec_tiles = [sa_b, ca1_b, ga_b, sbv_b, cb1_b, gb_b]
    for vi in range(6):
        for half in range(2):
            @E("tensor")
            def _(t, vi=vi, half=half):
                t.wait_ge(cV.sem, modb_done if (vi == 0 and half == 0) else cV.n)
                cT.tick(t.matmul(pSu[:], ones1[0:1, 0:128],
                                 modrow_b[0:1, HID * vi + 512 * half: HID * vi + 512 * (half + 1)],
                                 start=True, stop=True))

            @E("vector")
            def _(v, vi=vi, half=half):
                v.wait_ge(cT.sem, cT.n)
                dst = vec_tiles[vi][:, 512 * half:512 * (half + 1)]
                if vi in (1, 4):
                    cV.tick(v.tensor_scalar(dst, pSu[:], 1.0, None, ALU.add))
                else:
                    cV.tick(v.tensor_copy(dst, pSu[:]))

    @E("vector")
    def _(v):
        v.wait_ge(dS.sem, ld_smalls)
        cV.tick(v.tensor_copy(rows_b[0:1, 32:33], rows_s[0:1, 16:17]))

    for gt in (ga_b, gb_b):
        @E("scalar")
        def _(s, gt=gt):
            s.wait_ge(cV.sem, cV.n)
            cS.tick(s.activation(clampt2[:], gt[:], AF.Abs))

        @E("vector")
        def _(v, gt=gt):
            v.wait_ge(cS.sem, cS.n)
            v.tensor_scalar(clampt[:], gt[:], 0.0, None, ALU.is_ge)
            v.tensor_scalar(clampt[:], clampt[:], 2.0, -1.0, ALU.mult, ALU.add)
            v.tensor_scalar(clampt2[:], clampt2[:], 0.25, None, ALU.max)
            cV.tick(v.tensor_tensor(gt[:], clampt[:], clampt2[:], ALU.mult))

    @E("vector")
    def _(v):
        v.wait_ge(cV.sem, cV.n)
        v.tensor_scalar(rows_s[0:1, 8:12], rows_s[0:1, 4:8], 0.2, 10.0, ALU.max, ALU.min)
        v.drain()
        v.reciprocal(rows_s[0:1, 12:16], rows_s[0:1, 8:12])
        v.drain()
        v.tensor_scalar(rows_s[0:1, 16:32], rows_s[0:1, 16:32], 1e-7, 0.1, ALU.max, ALU.min)
        v.drain()
        v.tensor_copy(rows_b[0:1, 0:32], rows_s[0:1, 0:32])
        v.tensor_copy(onr_b[:], onr[:])
        cV.tick(v.tensor_copy(ebr_b[:], ebr[:]))

    for src, dst, wid in [(rows_b[0:1, 12:16], invt_c[:], 4),
                          (rows_b[0:1, 16:32], floorb[:], 16),
                          (ebr_b[0:1, :], ebias_b[:], NE),
                          (onr_b[0:1, :], onormb[:], DV)]:
        @E("tensor")
        def _(t, src=src, wid=wid):
            t.wait_ge(cV.sem, cV.n)
            cT.tick(t.matmul(pSu[:, 0:wid], ones1[0:1, 0:128], src, start=True, stop=True))

        @E("vector")
        def _(v, dst=dst, wid=wid):
            v.wait_ge(cT.sem, cT.n)
            cV.tick(v.tensor_copy(dst, pSu[:, 0:wid]))

    @E("vector")
    def _(v):
        v.wait_ge(cV.sem, cV.n)
        for h in range(NH):
            v.tensor_scalar(gW2h[:, :, 4 * h:4 * (h + 1)], gW2f[:], invt_c[:, h:h + 1], None, ALU.mult)
            ins = v.tensor_scalar(gb2h[0:1, 4 * h:4 * (h + 1)], rows_s[0:1, 0:4],
                                  rows_s[0:1, 12 + h:13 + h], None, ALU.mult)
        cV.tick(ins)

    w1s_evac = {}
    for j in range(24):
        @E("gpsimd")
        def _(g, j=j):
            if j >= 2:
                g.wait_ge(cS.sem, w1s_evac[j - 2])
            dma(g, dGm, g1stg[:, j % 2, :], dgW1[HID + 128 * j: HID + 128 * (j + 1), :])

        @E("tensor")
        def _(t, j=j):
            t.wait_ge(dGm.sem, dGm.n)
            if j >= 2:
                t.wait_ge(cS.sem, w1s_evac[j - 1])
            cT.tick(t.matmul(pRu[:], onec[:], g1stg[:, j % 2, :], start=(j % 2 == 0), stop=(j % 2 == 1)))

        if j % 2 == 1:
            @E("scalar")
            def _(s, j=j):
                s.wait_ge(cT.sem, cT.n)
                if j >= 3:
                    s.wait_ge(dS.sem, dS.n)
                grp = j // 2
                scale = (1.0 / 256.0) if (grp % 3 == 0) else 1.0
                cS.tick(s.activation(w1row[:], pRu[:], AF.Copy, scale=scale))

            @E("sync")
            def _(sy, j=j):
                sy.wait_ge(cS.sem, cS.n)
                dma(sy, dS, W1s[j // 2:j // 2 + 1, :], w1row[:])
        w1s_evac[j] = cS.n

    if "setup" in dbg:
        @E("gpsimd")
        def _(sy):
            sy.wait_ge(cS.sem, cS.n)
            sy.wait_ge(cV.sem, cV.n)
            for j, tsr in enumerate([sa_b, ca1_b, ga_b, sbv_b, cb1_b, gb_b]):
                dma(sy, dGm, dbg["setup"][j], tsr[:])
            dma(sy, dGm, dbg["setup"][6, 0:12, 0:512], W1s[:])
            dma(sy, dGm, dbg["setup"][7, :, 0:4], invt_c[:])
            dma(sy, dGm, dbg["setup"][7, :, 4:20], floorb[:])
            dma(sy, dGm, dbg["setup"][7, :, 20:276], onormb[:])
            dma(sy, dGm, dbg["setup"][7, 0:1, 276:340], rows_s[:])
            dma(sy, dGm, dbg["setup"][7, 1:2, 276:340], rows_b[:])
    w1s_dma = dS.n
    sc.close()

    pa = contextlib.ExitStack()

    def sba(name, shape, dt=F32):
        return pa.enter_context(nc.sbuf_tensor(name, shape, dt))

    x_sb = [sba("x_sb0", [128, NT, HID]), sba("x_sb1", [128, NT, HID])]
    h_bf = sba("h_bf", [128, NT, HID], BF16)
    hT = sba("hT", [128, 8, SEG], BF16)
    SCR = sba("SCR", [128, NT, HID])
    CONVB = sba("CONVB", [128, 8, SEG])
    tail3 = sba("tail3", [128, 3, 8, 3])
    qTc = sba("qTc", [128, 8, SEG], BF16)
    kTc = sba("kTc", [128, 8, SEG], BF16)
    vTc = [sba("vTc0", [128, 8, SEG], BF16), sba("vTc1", [128, 8, SEG], BF16)]
    TOKA = sba("TOKA", [128, NT, HID], BF16)
    TOKB = sba("TOKB", [128, NT, HID], BF16)
    v_tok = sba("v_tok", [128, NT, HID], BF16)
    kn_tok = sba("kn_tok", [128, NT, HID], BF16)
    TOKC = sba("TOKC", [128, NT, HID], BF16)
    Xu = sba("Xu", [128, NT, NH, 2 * DV], BF16)
    qnT = sba("qnT", [128, 8, SEG], BF16)
    knT = sba("knT", [128, 8, SEG], BF16)
    kbnT = sba("kbnT", [128, 8, SEG], BF16)
    wT_sb = sba("wT_sb", [128, 8, SEG], BF16)
    attnT = sba("attnT", [128, NT * NH, 128], BF16)
    S_bf = sba("S_bf", [128, NH, 2 * DV], BF16)
    TpP = sba("TpP", [128, NT * NH, 128], BF16)
    TpB5 = sba("TpB5", [128, NT * NH, 5, 128], BF16)
    beta_sb = sba("beta_sb", [128, NT, NH])
    statv = sba("statv", [128, NT, NH, 12])
    svT = sba("svT", [12, NH, SEG], BF16)
    hmG = sba("hmG", [128, NH, 4, SEG], BF16)
    wgt = sba("wgt", [128, NT, NH, 4])
    stat_s = sba("stat_s", [128, 32])
    sqd = sba("sqd", [128, HID])
    vwin = sba("vwin", [128, 8, 286], BF16)
    W_ring = sba("W_ring", [128, 24, HID], BF16)
    ez_sb = sba("ez_sb", [128, NT * NH, 4])

    S_ps = [pa.enter_context(nc.psum_tensor(f"S_ps{h}", [128, 512], F32)) for h in range(NH)]
    pA = pa.enter_context(nc.psum_tensor("pA", [128, 512], F32))
    pB = pa.enter_context(nc.psum_tensor("pB", [128, 512], F32))

    @E("sync")
    def _(sy):
        dma(sy, dS, x_sb[0][:], dx[0:SEG, :].rearrange("(T p) d -> p T d", p=128))
    x_ld = {0: dS.n}

    @E("gpsimd")
    def _(g):
        dma(g, dGm, W_ring[:, 0:8, :], KT(dWq))
        dma(g, dGm, W_ring[:, 8:16, :], KT(dWk))
        dma(g, dGm, W_ring[:, 16:24, :], KT(dWv))
    w1_ld = {0: dGm.n}

    @E("vector")
    def _(v):
        v.memset(S_bf[:], 0.0)
        cV.tick(v.memset(tail3[:], 0.0))
    sbf_zero = cV.n

    setup_cS, setup_cV, setup_dS = cS.n, cV.n, dS.n

    @E("tensor")
    def _(t):
        t.wait_ge(cV.sem, max(ones_done, setup_cV))
        t.wait_ge(cS.sem, setup_cS)
        t.wait_ge(dS.sem, setup_dS)
        for h in range(NH):
            ins = t.matmul(S_ps[h][:], zrow[0:1, 0:128], zrow[0:1, 0:512],
                           start=True, stop=False, skip_group_check=True)
        cT.tick(ins)

    x_free = {0: (cV, 0), 1: (cV, 0)}
    scast = {h: (cS, 0) for h in range(NH)}
    scan_free = 0
    vwin_guard = 0
    w4_ld = {}

    for i in range(nseg):
        buf = i % 2
        own = i >= own0
        orow = (i - own0) * SEG

        if i + 1 < nseg:
            @E("sync")
            def _(sy, i=i):
                ctr, val = x_free[(i + 1) % 2]
                if val:
                    sy.wait_ge(ctr.sem, val)
                dma(sy, dS, x_sb[(i + 1) % 2][:],
                    dx[SEG * (i + 1):SEG * (i + 2), :].rearrange("(T p) d -> p T d", p=128))
            x_ld[i + 1] = dS.n

        @E("vector")
        def _(v, i=i):
            v.wait_ge(dS.sem, x_ld[i])
            if i == 0:
                v.wait_ge(cV.sem, sbf_zero)
            v.reduce_sum(stat_s[:, 0:1], x_sb[i % 2][:, 0, :], axis=AX.X)
            cV.tick(v.reduce_sum(stat_s[:, 1:2], x_sb[i % 2][:, 1, :], axis=AX.X))

        @E("scalar")
        def _(s, i=i):
            s.wait_ge(dS.sem, x_ld[i])
            s.wait_ge(cV.sem, cV.n)
            s.activation(sqd[:], x_sb[i % 2][:, 0, :], AF.Square, accum_out=stat_s[:, 2:3])
            cS.tick(s.activation(sqd[:], x_sb[i % 2][:, 1, :], AF.Square, accum_out=stat_s[:, 3:4]))

        @E("vector")
        def _(v):
            v.wait_ge(cS.sem, cS.n)
            v.tensor_scalar(stat_s[:, 4:6], stat_s[:, 0:2], 1.0 / HID, None, ALU.mult)
            v.drain()
            v.tensor_tensor(stat_s[:, 6:8], stat_s[:, 4:6], stat_s[:, 4:6], ALU.mult)
            v.tensor_scalar(stat_s[:, 8:10], stat_s[:, 2:4], 1.0 / HID, 1e-5, ALU.mult, ALU.add)
            v.drain()
            cV.tick(v.tensor_tensor(stat_s[:, 8:10], stat_s[:, 8:10], stat_s[:, 6:8], ALU.subtract))

        @E("scalar")
        def _(s):
            s.wait_ge(cV.sem, cV.n)
            cS.tick(s.activation(stat_s[:, 10:12], stat_s[:, 8:10], AF.Sqrt))

        @E("vector")
        def _(v, i=i):
            v.wait_ge(cS.sem, cS.n)
            v.reciprocal(stat_s[:, 12:14], stat_s[:, 10:12])
            v.drain()
            for T in range(NT):
                v.tensor_scalar(sqd[:], x_sb[i % 2][:, T, :], stat_s[:, 4 + T:5 + T],
                                stat_s[:, 12 + T:13 + T], ALU.subtract, ALU.mult)
                v.tensor_tensor(sqd[:], sqd[:], ca1_b[:], ALU.mult)
                v.tensor_tensor(sqd[:], sqd[:], sa_b[:], ALU.add)
                ins = v.tensor_scalar(h_bf[:, T, :], sqd[:],
                                      mask_sb[:, i * NT + T:i * NT + T + 1], None, ALU.mult)
            cV.tick(ins)
        h_done = cV.n

        pTt_cm = nc.psum_tensor("pTt%d" % i, [128, 1024], BF16)
        with pTt_cm as pTt:
            for T in range(NT):
                @E("tensor")
                def _(t, T=T):
                    t.wait_ge(cV.sem, h_done if T == 0 else cV.n)
                    for k in range(8):
                        ins = t.transpose(pTt[:, 128 * k:128 * (k + 1)],
                                          h_bf[:, T, 128 * k:128 * (k + 1)], ident[:])
                    cT.tick(ins)

                @E("vector")
                def _(v, T=T):
                    v.wait_ge(cT.sem, cT.n)
                    cV.tick(v.tensor_copy(hT[:, :, 128 * T:128 * (T + 1)],
                                          pTt[:].rearrange("p (k t) -> p k t", t=128)))
            hT_done = cV.n

            @E("tensor")
            def _(t, i=i):
                t.wait_ge(cV.sem, hT_done)
                t.wait_ge(dGm.sem, w1_ld[i])
                t.wait_ge(cS.sem, scan_free)
                for T in range(NT):
                    for k in range(8):
                        ins = t.matmul(pB[:, 4 * T:4 * (T + 1)], hT[:, k, 128 * T:128 * (T + 1)],
                                       Wb_s[:, k, :], start=(T == 0 and k == 0),
                                       stop=(T == NT - 1 and k == 7))
                cT.tick(ins)

            @E("scalar")
            def _(s):
                s.wait_ge(cT.sem, cT.n)
                cS.tick(s.activation(beta_sb[:].rearrange("p T h -> p (T h)"),
                                     pB[:, 0:4 * NT], AF.Sigmoid))
            beta_done = cS.n

            @E("vector")
            def _(v, beta_done=beta_done):
                v.wait_ge(cS.sem, beta_done)
                cV.tick(v.memset(stat_s[:, 30:31], 0.0))
            evac_par = {0: cV.n, 1: cV.n}

            q_on = own or (i == own0 - 1)
            tensors3 = [(qTc, 0, 0), (kTc, 8, 1), (vTc[buf], 16, 2)] if q_on else \
                       [(kTc, 8, 1), (vTc[buf], 16, 2)]
            conv_done_m = 0
            silu_work = []
            for dst, base, wi in tensors3:
                for f in range(8):
                    @E("tensor")
                    def _(t, base=base, f=f, evp=evac_par[f % 2]):
                        if evp:
                            t.wait_ge(cV.sem, evp)
                        pbank = pA if f % 2 == 0 else pB
                        for k in range(8):
                            ins = t.matmul(pbank[:, 0:256],
                                           W_ring[:, base + k, 128 * f:128 * (f + 1)],
                                           hT[:, k, :], start=(k == 0), stop=(k == 7))
                        cT.tick(ins)

                    @E("vector")
                    def _(v, f=f):
                        v.wait_ge(cT.sem, cT.n)
                        pbank = pA if f % 2 == 0 else pB
                        scr8 = SCR[:].rearrange("p T (k2 z) -> p (T k2) z", z=SEG)
                        cV.tick(v.tensor_copy(scr8[:, f, :], pbank[:, 0:256]))
                    evac_par[f % 2] = cV.n
                raw_done = cV.n

                @E("vector")
                def _(v, wi=wi, i=i, raw_done=raw_done):
                    scr8 = SCR[:].rearrange("p T (k2 z) -> p (T k2) z", z=SEG)
                    for dt_ in range(8):
                        tp = lambda j: convt[:, wi, dt_, j:j + 1]
                        v.tensor_scalar(CONVB[:, dt_, :], scr8[:, dt_, :], tp(3), None, ALU.mult)
                        for sh in (1, 2, 3):
                            v.scalar_tensor_tensor(CONVB[:, dt_, sh:SEG], scr8[:, dt_, 0:SEG - sh],
                                                   tp(3 - sh), CONVB[:, dt_, sh:SEG], ALU.mult, ALU.add)
                            v.scalar_tensor_tensor(CONVB[:, dt_, 0:sh], tail3[:, wi, dt_, 3 - sh:3],
                                                   tp(3 - sh), CONVB[:, dt_, 0:sh], ALU.mult, ALU.add)
                    cV.tick(v.tensor_copy(tail3[:, wi, :, :], scr8[:, :, SEG - 3:SEG]))
                conv_done_m = cV.n

                silu_work.append((dst, wi, cV.n))

            @E("scalar")
            def _(s, work=tuple(silu_work), vg=vwin_guard):
                for dst, wi, cvt in work:
                    s.wait_ge(cV.sem, cvt)
                    if wi == 2 and vg:
                        s.wait_ge(cV.sem, vg)
                    cS.tick(s.activation(dst[:].rearrange("p k z -> p (k z)"),
                                         CONVB[:].rearrange("p k z -> p (k z)"), AF.Silu))
            qkv_done = cS.n

            if own:
                @E("gpsimd")
                def _(g):
                    g.wait_ge(cS.sem, qkv_done)
                    dma(g, dGm, W_ring[:, 0:8, :], KT(dWo))
                    dma(g, dGm, W_ring[:, 8:16, 0:512], KT(dgW1[0:HID, :]))
                    dma(g, dGm, W_ring[:, 16:24, 0:512], KT(dgW1[HID + 3 * DV * NH:5120, :]))
                w4_ld[i] = dGm.n

            srcdst = ([(qTc, TOKA)] if own else []) + [(kTc, TOKB), (vTc[buf], v_tok)]
            for src, dst2 in srcdst:
                for T in range(NT):
                    @E("tensor")
                    def _(t, src=src, T=T):
                        t.wait_ge(cS.sem, qkv_done)
                        t.wait_ge(cV.sem, cV.n)
                        for k in range(8):
                            ins = t.transpose(pTt[:, 128 * k:128 * (k + 1)],
                                              src[:, k, 128 * T:128 * (T + 1)], ident[:])
                        cT.tick(ins)

                    @E("vector")
                    def _(v, dst2=dst2, T=T):
                        v.wait_ge(cT.sem, cT.n)
                        cV.tick(v.tensor_copy(dst2[:, T, :],
                                              pTt[:].rearrange("p (k t) -> p k t", t=128)))
            tok_done = cV.n

            @E("scalar")
            def _(s, own=own):
                s.wait_ge(cV.sem, tok_done)
                for T in range(NT):
                    for h in range(NH):
                        if own:
                            s.activation(sqd[:, 0:DV], TOKA[:, T, DV * h:DV * (h + 1)], AF.Square,
                                         accum_out=stat_s[:, 8 + 8 * T + h:9 + 8 * T + h])
                        ins = s.activation(sqd[:, 0:DV], TOKB[:, T, DV * h:DV * (h + 1)], AF.Square,
                                           accum_out=stat_s[:, 12 + 8 * T + h:13 + 8 * T + h])
                cS.tick(ins)

            @E("vector")
            def _(v, own=own):
                v.wait_ge(cS.sem, cS.n)
                if own:
                    cV.tick(v.tensor_scalar(stat_s[:, 8:24], stat_s[:, 8:24], 1e-6, None, ALU.add))
                else:
                    v.tensor_scalar(stat_s[:, 12:16], stat_s[:, 12:16], 1e-6, None, ALU.add)
                    cV.tick(v.tensor_scalar(stat_s[:, 20:24], stat_s[:, 20:24], 1e-6, None, ALU.add))

            @E("scalar")
            def _(s, own=own):
                s.wait_ge(cV.sem, cV.n)
                if own:
                    cS.tick(s.activation(stat_s[:, 8:24], stat_s[:, 8:24], AF.Sqrt))
                else:
                    s.activation(stat_s[:, 12:16], stat_s[:, 12:16], AF.Sqrt)
                    cS.tick(s.activation(stat_s[:, 20:24], stat_s[:, 20:24], AF.Sqrt))

            @E("vector")
            def _(v, own=own):
                v.wait_ge(cS.sem, cS.n)
                if own:
                    v.reciprocal(stat_s[:, 8:24], stat_s[:, 8:24])
                else:
                    v.reciprocal(stat_s[:, 12:16], stat_s[:, 12:16])
                    v.reciprocal(stat_s[:, 20:24], stat_s[:, 20:24])
                v.drain()
                for T in range(NT):
                    for h in range(NH):
                        if own:
                            v.tensor_scalar(TOKC[:, T, DV * h:DV * (h + 1)],
                                            TOKA[:, T, DV * h:DV * (h + 1)],
                                            stat_s[:, 8 + 8 * T + h:9 + 8 * T + h], None, ALU.mult)
                        v.tensor_scalar(kn_tok[:, T, DV * h:DV * (h + 1)],
                                        TOKB[:, T, DV * h:DV * (h + 1)],
                                        stat_s[:, 12 + 8 * T + h:13 + 8 * T + h], None, ALU.mult)
                        v.tensor_scalar(Xu[:, T, h, 0:DV], v_tok[:, T, DV * h:DV * (h + 1)],
                                        beta_sb[:, T, h:h + 1], None, ALU.mult)
                        ins = v.tensor_scalar(Xu[:, T, h, DV:2 * DV],
                                              kn_tok[:, T, DV * h:DV * (h + 1)],
                                              beta_sb[:, T, h:h + 1], None, ALU.mult)
                cV.tick(ins)
            norm_done = cV.n

            tlist = ([("qn", qnT)] if own else []) + [("kn", knT), ("kb", kbnT)]
            for srcf, dstt in tlist:
                for T in range(NT):
                    @E("tensor")
                    def _(t, srcf=srcf, T=T):
                        t.wait_ge(cV.sem, cV.n)
                        for hh in range(8):
                            h, half = hh // 2, hh % 2
                            if srcf == "qn":
                                src_ap = TOKC[:, T, DV * h + 128 * half:DV * h + 128 * (half + 1)]
                            elif srcf == "kn":
                                src_ap = kn_tok[:, T, DV * h + 128 * half:DV * h + 128 * (half + 1)]
                            else:
                                src_ap = Xu[:, T, h, DV + 128 * half:DV + 128 * (half + 1)]
                            ins = t.transpose(pTt[:, 128 * hh:128 * (hh + 1)], src_ap, ident[:])
                        cT.tick(ins)

                    @E("vector")
                    def _(v, dstt=dstt, T=T):
                        v.wait_ge(cT.sem, cT.n)
                        cV.tick(v.tensor_copy(dstt[:, :, 128 * T:128 * (T + 1)],
                                              pTt[:].rearrange("p (k t) -> p k t", t=128)))
            tposed_done = cV.n

        with nc.psum_tensor("pTb%d" % i, [128, 512], BF16) as pTb, \
             nc.psum_tensor("pP2%d" % i, [128, 512], F32) as pP2:
            chains = [(T, h) for T in range(NT) for h in range(NH)]
            lvl_mm = {}
            lvl_evac = {}
            lvl_tp = {}
            app_add = {}
            prev_mask = 0
            for ci, (T, h) in enumerate(chains):
                @E("tensor")
                def _(t, T=T, h=h, own=own, pm=prev_mask):
                    t.wait_ge(cV.sem, pm if pm else cV.n)
                    for half in range(2):
                        t.matmul(pA[:, 0:128], kbnT[:, 2 * h + half, 128 * T:128 * (T + 1)],
                                 knT[:, 2 * h + half, 128 * T:128 * (T + 1)],
                                 start=(half == 0), stop=(half == 1))
                    for half in range(2):
                        ins = t.matmul(pA[:, 128:256], knT[:, 2 * h + half, 128 * T:128 * (T + 1)],
                                       kbnT[:, 2 * h + half, 128 * T:128 * (T + 1)],
                                       start=(half == 0), stop=(half == 1))
                    if own:
                        for half in range(2):
                            ins = t.matmul(pA[:, 256:384], knT[:, 2 * h + half, 128 * T:128 * (T + 1)],
                                           qnT[:, 2 * h + half, 128 * T:128 * (T + 1)],
                                           start=(half == 0), stop=(half == 1))
                    cT.tick(ins)

                @E("vector")
                def _(v, T=T, h=h, ci=ci, own=own):
                    v.wait_ge(cT.sem, cT.n)
                    v.tensor_tensor(TpP[:, ci, :], pA[:, 0:128], maskA_s[:], ALU.mult)
                    ins = v.tensor_tensor(TpB5[:, ci, 0, :], pA[:, 128:256], maskAU_s[:], ALU.mult)
                    if own:
                        ins = v.tensor_tensor(attnT[:, NH * T + h, :], pA[:, 256:384],
                                              maskT_s[:], ALU.mult)
                    cV.tick(ins)
                prev_mask = cV.n
                lvl_tp[ci] = cV.n

            pslot = lambda ci: (pB if ci % 2 == 0 else pP2)[:, 128 * ((ci // 2) % 2):
                                                            128 * ((ci // 2) % 2 + 1)]
            for j in range(1, 5):
                ev_base = cS.n
                mms_j = []
                for ci in range(len(chains)):
                    @E("tensor")
                    def _(t, ci=ci, j=j):
                        t.wait_ge(cV.sem, lvl_tp[ci])
                        ev = lvl_evac.get((j, ci - 2), 0) if ci >= 2 else lvl_evac.get((j - 1, ci + 6), 0)
                        if ev:
                            t.wait_ge(cS.sem, ev)
                        cT.tick(t.matmul(pslot(ci), TpB5[:, ci, j - 1, :], TpP[:, ci, :],
                                         start=True, stop=True, skip_group_check=True))
                    lvl_mm[(j, ci)] = cT.n
                    mms_j.append(cT.n)
                    lvl_evac[(j, ci)] = ev_base + ci + 1

                    @E("vector")
                    def _(v, ci=ci, j=j, w=lvl_evac[(j, ci)]):
                        v.wait_ge(cS.sem, w)
                        cV.tick(v.transpose(TpB5[:, ci, j, :], TpP[:, ci, :]))
                    lvl_tp[ci] = cV.n

                @E("scalar")
                def _(s, mms=tuple(mms_j)):
                    for ci, mmv in enumerate(mms):
                        s.wait_ge(cT.sem, mmv)
                        cS.tick(s.copy(TpP[:, ci, :], pslot(ci)))

            powers_done = cV.n
            bank_free = {0: 0, 1: 0}
            for j in range(4, -1, -1):
                for ci, (T, h) in enumerate(chains):
                    bank = pA if ci % 2 == 0 else pB
                    @E("tensor")
                    def _(t, ci=ci, j=j, T=T, h=h, bank=bank, bf=bank_free[ci % 2]):
                        t.wait_ge(cV.sem, max(powers_done, lvl_tp[ci], bf,
                                              app_add.get((j + 1, ci), 0)))
                        cT.tick(t.matmul(bank[:], TpB5[:, ci, j, :], Xu[:, T, h, :],
                                         start=True, stop=True))
                    mmv = cT.n

                    @E("vector")
                    def _(v, ci=ci, T=T, h=h, bank=bank, mmv=mmv):
                        v.wait_ge(cT.sem, mmv)
                        cV.tick(v.tensor_tensor(Xu[:, T, h, :], Xu[:, T, h, :], bank[:], ALU.add))
                    app_add[(j, ci)] = cV.n
                    bank_free[ci % 2] = cV.n
            tchain_done = cV.n

            for T in range(NT):
                for grp in range(2):
                    @E("tensor")
                    def _(t, T=T, grp=grp):
                        t.wait_ge(cV.sem, cV.n)
                        for j in range(4):
                            hh = 4 * grp + j
                            h, half = hh // 2, hh % 2
                            ins = t.transpose(pTb[:, 128 * j:128 * (j + 1)],
                                              Xu[:, T, h, DV + 128 * half:DV + 128 * (half + 1)],
                                              ident[:])
                        cT.tick(ins)

                    @E("vector")
                    def _(v, T=T, grp=grp):
                        v.wait_ge(cT.sem, cT.n)
                        cV.tick(v.tensor_copy(wT_sb[:, 4 * grp:4 * (grp + 1), 128 * T:128 * (T + 1)],
                                              pTb[:, 0:512].rearrange("p (k t) -> p k t", t=128)))
            wT_done = cV.n

        with nc.psum_tensor("wSa%d" % i, [128, 512], F32) as wSa, \
             nc.psum_tensor("wSb%d" % i, [128, 512], F32) as wSb:
            for T in range(NT):
                cast_base = cS.n
                cast_mms = []
                for c in range(4):
                    for h in range(NH):
                        wbank = wSa if h % 2 == 0 else wSb
                        obank = pA if h < 2 else pB
                        ocol = 256 * (h % 2)
                        r0, r1 = 32 * c, 32 * (c + 1)

                        @E("tensor")
                        def _(t, T=T, c=c, h=h, wbank=wbank, obank=obank, ocol=ocol,
                              r0=r0, r1=r1, own=own):
                            t.wait_ge(cS.sem, max(scast[h][1], cS.n if (c == 0 and h == 0) else 0))
                            if T == 0 and c == 0 and h == 0:
                                t.wait_ge(cV.sem, wT_done)
                            if h >= 2:
                                t.wait_ge(cV.sem, cV.n)
                            for half in range(2):
                                ins = t.matmul(wbank[r0:r1, 0:256],
                                               wT_sb[:, 2 * h + half, 128 * T + r0:128 * T + r1],
                                               S_bf[:, h, 256 * half:256 * (half + 1)],
                                               start=(half == 0), stop=(half == 1),
                                               tile_position=(0, r0), skip_group_check=True)
                            if own:
                                for half in range(2):
                                    ins = t.matmul(obank[r0:r1, ocol:ocol + 256],
                                                   qnT[:, 2 * h + half, 128 * T + r0:128 * T + r1],
                                                   S_bf[:, h, 256 * half:256 * (half + 1)],
                                                   start=(half == 0 and h % 2 == 0), stop=False,
                                                   tile_position=(0, r0), skip_group_check=True)
                            cT.tick(ins)

                        @E("vector")
                        def _(v, T=T, h=h, wbank=wbank, r0=r0, r1=r1):
                            v.wait_ge(cT.sem, cT.n)
                            cV.tick(v.tensor_tensor(Xu[r0:r1, T, h, 0:DV], Xu[r0:r1, T, h, 0:DV],
                                                    wbank[r0:r1, 0:256], ALU.subtract))

                        @E("tensor")
                        def _(t, T=T, c=c, h=h, r0=r0, r1=r1):
                            t.wait_ge(cV.sem, cV.n)
                            for mc in range(2):
                                ins = t.matmul(S_ps[h][:, 256 * mc:256 * (mc + 1)],
                                               kn_tok[r0:r1, T, DV * h + 128 * mc:DV * h + 128 * (mc + 1)],
                                               Xu[r0:r1, T, h, 0:DV],
                                               start=False, stop=False,
                                               tile_position=(r0, 0), skip_group_check=True)
                            cT.tick(ins)
                        cast_mms.append(cT.n)
                        scast[h] = (cS, cast_base + 4 * c + h + 1)

                delo_mms = []
                if own:
                    for h in range(NH):
                        obank = pA if h < 2 else pB
                        ocol = 256 * (h % 2)

                        @E("tensor")
                        def _(t, T=T, h=h, obank=obank, ocol=ocol):
                            t.wait_ge(cV.sem, cV.n)
                            cT.tick(t.matmul(obank[:, ocol:ocol + 256], attnT[:, NH * T + h, :],
                                             Xu[:, T, h, 0:DV], start=False, stop=(h % 2 == 1),
                                             skip_group_check=True))
                        if h % 2 == 1:
                            delo_mms.append((cT.n, h))

                @E("scalar")
                def _(s, T=T, mms=tuple(cast_mms), dmm=tuple(delo_mms)):
                    for idx, mmv in enumerate(mms):
                        h = idx % NH
                        s.wait_ge(cT.sem, mmv)
                        cS.tick(s.copy(S_bf[:, h, :], S_ps[h][:]))
                    for mmv, h in dmm:
                        s.wait_ge(cT.sem, mmv)
                        cS.tick(s.copy(TOKA[:, T, 512 * (h // 2):512 * (h // 2 + 1)],
                                       (pA if h < 2 else pB)[:]))
                assert cS.n == cast_base + len(cast_mms) + len(delo_mms)
            scan_done = cS.n
        scan_free = scan_done
        if not own:
            x_free[buf] = (cS, scan_done)

        if taps:
            @E("sync")
            def _(sy, i=i, h_done=h_done, tok_done=tok_done, beta_done=beta_done,
                  norm_done=norm_done, scan_done=scan_done, own=own):
                rr = lambda d: d.rearrange("(T p) d -> p T d", p=128)
                tap_rows = slice(SEG * i, SEG * (i + 1))
                if "h" in dbg:
                    sy.wait_ge(cV.sem, h_done)
                    dma(sy, dS, rr(dbg["h"][tap_rows, :]), h_bf[:])
                if "k" in dbg:
                    sy.wait_ge(cV.sem, tok_done)
                    dma(sy, dS, rr(dbg["k"][tap_rows, :]), TOKB[:])
                if "v" in dbg:
                    sy.wait_ge(cV.sem, tok_done)
                    dma(sy, dS, rr(dbg["v"][tap_rows, :]), v_tok[:])
                if "q" in dbg and own:
                    sy.wait_ge(cV.sem, tok_done)
                    dma(sy, dS, rr(dbg["q"][tap_rows, :]), TOKA[:])
                if "beta" in dbg:
                    sy.wait_ge(cS.sem, beta_done)
                    dma(sy, dS, dbg["beta"][tap_rows, :].rearrange("(T p) h -> p T h", p=128), beta_sb[:])
                if "kn" in dbg:
                    sy.wait_ge(cV.sem, norm_done)
                    dma(sy, dS, rr(dbg["kn"][tap_rows, :]), kn_tok[:])
                if "qn" in dbg and own:
                    sy.wait_ge(cV.sem, norm_done)
                    dma(sy, dS, rr(dbg["qn"][tap_rows, :]), TOKC[:])
                if "uw" in dbg:
                    sy.wait_ge(cS.sem, scan_done)
                    dma(sy, dS, dbg["uw"][2 * i:2 * i + 2].rearrange("T p h d -> p T h d"), Xu[:])
                if "delta_o" in dbg and own:
                    sy.wait_ge(cS.sem, scan_done)
                    dma(sy, dS, rr(dbg["delta_o"][tap_rows, :]), TOKA[:])

        if not own:
            w1_ld[i + 1] = w1_ld[i]
            continue

        with nc.psum_tensor("pTg%d" % i, [128, 1024], BF16) as pTg, \
             nc.psum_tensor("pCg%d" % i, [128, 512], F32) as pCg:
            for T in range(NT):
                @E("tensor")
                def _(t, T=T, scan_done=scan_done):
                    t.wait_ge(cS.sem, scan_done)
                    t.wait_ge(cV.sem, cV.n)
                    for hh in range(8):
                        h, half = hh // 2, hh % 2
                        ins = t.transpose(pTg[:, 128 * hh:128 * (hh + 1)],
                                          TOKA[:, T, DV * h + 128 * half:DV * h + 128 * (half + 1)],
                                          ident[:])
                    cT.tick(ins)

                @E("vector")
                def _(v, T=T):
                    v.wait_ge(cT.sem, cT.n)
                    cV.tick(v.tensor_copy(qTc[:, :, 128 * T:128 * (T + 1)],
                                          pTg[:].rearrange("p (k t) -> p k t", t=128)))
            delot_done = cV.n

            @E("vector")
            def _(v, i=i, buf=buf, qkv_done=qkv_done):
                v.wait_ge(cS.sem, qkv_done)
                v.tensor_copy(vwin[:, :, 0:30], vTc[1 - buf][:, :, SEG - 30:SEG])
                cV.tick(v.tensor_copy(vwin[:, :, 30:286], vTc[buf][:]))
            vwin_done = cV.n
            vwin_guard = vwin_done

            fir_base = cS.n
            fir_casts = []
            for fi, (ntap, ftap, dstT, dstTok) in enumerate(
                    [(3, firs_t, knT, TOKB), (31, firl_t, kbnT, TOKC)]):
                @E("vector")
                def _(v, ntap=ntap, ftap=ftap, fi=fi, fir_base=fir_base):
                    v.wait_ge(cV.sem, vwin_done)
                    v.wait_ge(cS.sem, fir_base + fi)
                    for dt_ in range(8):
                        base = 30 - (ntap - 1)
                        v.tensor_scalar(CONVB[:, dt_, :], vwin[:, dt_, base:base + SEG],
                                        ftap[:, dt_, 0:1], None, ALU.mult)
                        for j in range(1, ntap):
                            ins = v.scalar_tensor_tensor(CONVB[:, dt_, :],
                                                         vwin[:, dt_, base + j:base + j + SEG],
                                                         ftap[:, dt_, j:j + 1], CONVB[:, dt_, :],
                                                         ALU.mult, ALU.add)
                    cV.tick(ins)
                fir_casts.append((dstT, cV.n))
                fcast = fir_base + fi + 1

                for T in range(NT):
                    @E("tensor")
                    def _(t, dstT=dstT, T=T, fcast=fcast):
                        t.wait_ge(cS.sem, fcast)
                        t.wait_ge(cV.sem, cV.n)
                        for k in range(8):
                            ins = t.transpose(pTg[:, 128 * k:128 * (k + 1)],
                                              dstT[:, k, 128 * T:128 * (T + 1)], ident[:])
                        cT.tick(ins)

                    @E("vector")
                    def _(v, dstTok=dstTok, T=T):
                        v.wait_ge(cT.sem, cT.n)
                        cV.tick(v.tensor_copy(dstTok[:, T, :],
                                              pTg[:].rearrange("p (k t) -> p k t", t=128)))
            fir_done = cV.n

            @E("scalar")
            def _(s, casts=tuple(fir_casts)):
                for dstT, cvt in casts:
                    s.wait_ge(cV.sem, cvt)
                    cS.tick(s.copy(dstT[:].rearrange("p k z -> p (k z)"),
                                   CONVB[:].rearrange("p k z -> p (k z)")))
            assert cS.n == fir_base + 2

            stat_ticks = []
            for T in range(NT):
                for h in range(NH):
                    @E("vector")
                    def _(v, T=T, h=h, fir_done=fir_done, scan_done=scan_done):
                        if T == 0 and h == 0:
                            v.wait_ge(cV.sem, fir_done)
                            v.wait_ge(cS.sem, scan_done)
                        else:
                            v.wait_ge(cV.sem, cV.n)
                        for bi, bsrc in enumerate([TOKB, TOKC, TOKA, v_tok]):
                            sl = bsrc[:, T, DV * h:DV * (h + 1)]
                            v.reduce_sum(statv[:, T, h, 3 * bi:3 * bi + 1], sl, axis=AX.X)
                            ins = v.tensor_reduce(statv[:, T, h, 3 * bi + 2:3 * bi + 3], sl,
                                                  axis=AX.X, op=ALU.max)
                        cV.tick(ins)
                    stat_ticks.append(cV.n)

            @E("scalar")
            def _(s, ticks=tuple(stat_ticks)):
                for idx, tv in enumerate(ticks):
                    T, h = idx // NH, idx % NH
                    s.wait_ge(cV.sem, tv)
                    for bi, bsrc in enumerate([TOKB, TOKC, TOKA, v_tok]):
                        ins = s.activation(sqd[:, 0:DV], bsrc[:, T, DV * h:DV * (h + 1)],
                                           AF.Square, accum_out=statv[:, T, h, 3 * bi + 1:3 * bi + 2])
                    cS.tick(ins)
            sq_done = cS.n

            @E("vector")
            def _(v, sq_done=sq_done):
                v.wait_ge(cS.sem, sq_done)
                rmsv = statv[:].rearrange("p T h d -> p (T h) d")[:, :, 1:11:3]
                cV.tick(v.tensor_scalar(rmsv, rmsv, 1.0 / DV, 1e-8, ALU.mult, ALU.max))

            @E("scalar")
            def _(s):
                s.wait_ge(cV.sem, cV.n)
                rmsv = statv[:].rearrange("p T h d -> p (T h) d")[:, :, 1:11:3]
                cS.tick(s.activation(rmsv, rmsv, AF.Sqrt))

            @E("vector")
            def _(v):
                v.wait_ge(cS.sem, cS.n)
                cV.tick(v.memset(stat_s[:, 31:32], 0.0))
            stats_done = cV.n

            for T in range(NT):
                for h in range(NH):
                    @E("tensor")
                    def _(t, T=T, h=h):
                        t.wait_ge(cV.sem, cV.n)
                        cT.tick(t.transpose(pCg[0:12, 0:128], statv[:, T, h, :], identf[:]))

                    @E("vector")
                    def _(v, T=T, h=h):
                        v.wait_ge(cT.sem, cT.n)
                        cV.tick(v.tensor_copy(svT[0:12, h, 128 * T:128 * (T + 1)], pCg[0:12, 0:128]))
            svt_done = cV.n

            g_base = cS.n
            gelu_mms = []
            for h in range(NH):
                for Mc in range(4):
                    idx = 4 * h + Mc
                    @E("tensor")
                    def _(t, i=i, h=h, Mc=Mc, svt_done=svt_done, idx=idx, g_base=g_base):
                        if idx == 0:
                            t.wait_ge(cV.sem, svt_done)
                            t.wait_ge(dGm.sem, w4_ld[i])
                            t.wait_ge(dS.sem, w1s_dma)
                        if idx >= 2:
                            t.wait_ge(cS.sem, g_base + idx - 1)
                        pg = pA if Mc % 2 == 0 else pB
                        t.matmul(pg[:, 0:256], W_ring[:, 8 + 0, 128 * Mc:128 * (Mc + 1)], hT[:, 0, :],
                                 start=True, stop=False)
                        for k in range(1, 8):
                            t.matmul(pg[:, 0:256], W_ring[:, 8 + k, 128 * Mc:128 * (Mc + 1)],
                                     hT[:, k, :], start=False, stop=False)
                        for bi, bT in enumerate([knT, kbnT, qTc, vTc[i % 2]]):
                            for half in range(2):
                                t.matmul(pg[:, 0:256], W_ring[:, 16 + 2 * bi + half, 128 * Mc:128 * (Mc + 1)],
                                         bT[:, 2 * h + half, :], start=False, stop=False)
                        t.matmul(pg[:, 0:256], W1s[0:12, 128 * Mc:128 * (Mc + 1)], svT[0:12, h, :],
                                 start=False, stop=False)
                        cT.tick(t.matmul(pg[:, 0:256], gb1_b[0:1, 128 * Mc:128 * (Mc + 1)],
                                         ones1[0:1, 0:SEG], start=False, stop=True))
                    gelu_mms.append(cT.n)

            @E("scalar")
            def _(s, mms=tuple(gelu_mms)):
                for idx, mmv in enumerate(mms):
                    h, Mc = idx // 4, idx % 4
                    s.wait_ge(cT.sem, mmv)
                    pg = pA if Mc % 2 == 0 else pB
                    cS.tick(s.activation(hmG[:, h, Mc, :], pg[:, 0:256], AF.Gelu))
            gelu_all = cS.n
            assert gelu_all == g_base + 16

            for h in range(NH):
                for T in range(NT):
                    @E("tensor")
                    def _(t, h=h, T=T, gelu_all=gelu_all):
                        if h == 0 and T == 0:
                            t.wait_ge(cS.sem, gelu_all)
                            t.wait_ge(cV.sem, cV.n)
                        g = NT * h + T
                        for Mc in range(4):
                            t.matmul(pCg[:, 4 * g:4 * (g + 1)], hmG[:, h, Mc, 128 * T:128 * (T + 1)],
                                     gW2h[:, Mc, 4 * h:4 * (h + 1)], start=(Mc == 0), stop=False,
                                     skip_group_check=True)
                        cT.tick(t.matmul(pCg[:, 4 * g:4 * (g + 1)], ones1[0:1, 128 * T:128 * (T + 1)],
                                         gb2h[0:1, 4 * h:4 * (h + 1)], start=False, stop=True,
                                         skip_group_check=True))
            logits_all = cT.n

            @E("vector")
            def _(v, logits_all=logits_all):
                v.wait_ge(cT.sem, logits_all)
                cV.tick(v.tensor_reduce(stat_s[:, 0:8], pCg[:, 0:4 * NT * NH]
                                        .rearrange("p (g e) -> p g e", e=4),
                                        axis=AX.X, op=ALU.max, negate=True))
            negmax_all = cV.n

            exp_base = cS.n

            @E("scalar")
            def _(s, negmax_all=negmax_all):
                s.wait_ge(cV.sem, negmax_all)
                for g in range(NT * NH):
                    cS.tick(s.activation(ez_sb[:, g, :], pCg[:, 4 * g:4 * (g + 1)], AF.Exp,
                                         bias=stat_s[:, g:g + 1], accum_out=stat_s[:, 8 + g:9 + g]))
            exp_all = cS.n

            for h in range(NH):
                for T in range(NT):
                    g = NT * h + T

                    @E("vector")
                    def _(v, T=T, h=h, g=g, exp_base=exp_base):
                        v.wait_ge(cS.sem, exp_base + g + 1)
                        v.reciprocal(stat_s[:, 16:17], stat_s[:, 8 + g:9 + g])
                        v.drain()
                        v.tensor_scalar(ez_sb[:, g, :], ez_sb[:, g, :], stat_s[:, 16:17], None, ALU.mult)
                        v.drain()
                        v.tensor_tensor(ez_sb[:, g, :], ez_sb[:, g, :], floorb[:, 4 * h:4 * (h + 1)], ALU.max)
                        v.drain()
                        v.reduce_sum(stat_s[:, 17:18], ez_sb[:, g, :], axis=AX.X)
                        v.drain()
                        v.reciprocal(stat_s[:, 18:19], stat_s[:, 17:18])
                        v.drain()
                        cV.tick(v.tensor_scalar(wgt[:, T, h, :], ez_sb[:, g, :], stat_s[:, 18:19],
                                                None, ALU.mult))
            gate_done = cV.n

            mix_ticks = []
            for T in range(NT):
                for h in range(NH):
                    @E("vector")
                    def _(v, T=T, h=h, gate_done=gate_done):
                        v.wait_ge(cV.sem, gate_done if (T == 0 and h == 0) else cV.n)
                        dst = SCR[:, T, DV * h:DV * (h + 1)]
                        v.tensor_scalar(dst, TOKB[:, T, DV * h:DV * (h + 1)], wgt[:, T, h, 0:1],
                                        None, ALU.mult)
                        v.scalar_tensor_tensor(dst, TOKC[:, T, DV * h:DV * (h + 1)], wgt[:, T, h, 1:2],
                                               dst, ALU.mult, ALU.add)
                        v.scalar_tensor_tensor(dst, TOKA[:, T, DV * h:DV * (h + 1)], wgt[:, T, h, 2:3],
                                               dst, ALU.mult, ALU.add)
                        cV.tick(v.scalar_tensor_tensor(dst, v_tok[:, T, DV * h:DV * (h + 1)],
                                                       wgt[:, T, h, 3:4], dst, ALU.mult, ALU.add))
                    mix_ticks.append(cV.n)

            @E("scalar")
            def _(s, ticks=tuple(mix_ticks)):
                for idx, tv in enumerate(ticks):
                    T, h = idx // NH, idx % NH
                    s.wait_ge(cV.sem, tv)
                    cS.tick(s.activation(sqd[:, 0:DV], SCR[:, T, DV * h:DV * (h + 1)], AF.Square,
                                         accum_out=stat_s[:, 16 + 4 * T + h:17 + 4 * T + h]))
            rms_sq = cS.n

            @E("vector")
            def _(v, rms_sq=rms_sq):
                v.wait_ge(cS.sem, rms_sq)
                cV.tick(v.tensor_scalar(stat_s[:, 16:24], stat_s[:, 16:24], 1.0 / DV, 1e-5,
                                        ALU.mult, ALU.add))

            @E("scalar")
            def _(s):
                s.wait_ge(cV.sem, cV.n)
                cS.tick(s.activation(stat_s[:, 16:24], stat_s[:, 16:24], AF.Sqrt))

            @E("vector")
            def _(v):
                v.wait_ge(cS.sem, cS.n)
                v.reciprocal(stat_s[:, 16:24], stat_s[:, 16:24])
                v.drain()
                for T in range(NT):
                    for h in range(NH):
                        ins = v.scalar_tensor_tensor(h_bf[:, T, DV * h:DV * (h + 1)],
                                                     SCR[:, T, DV * h:DV * (h + 1)],
                                                     stat_s[:, 16 + 4 * T + h:17 + 4 * T + h],
                                                     onormb[:], ALU.mult, ALU.mult)
                cV.tick(ins)
            mix_done = cV.n

            for T in range(NT):
                @E("tensor")
                def _(t, T=T, mix_done=mix_done):
                    t.wait_ge(cV.sem, mix_done if T == 0 else cV.n)
                    for k in range(8):
                        ins = t.transpose(pTg[:, 128 * k:128 * (k + 1)],
                                          h_bf[:, T, 128 * k:128 * (k + 1)], ident[:])
                    cT.tick(ins)

                @E("vector")
                def _(v, T=T):
                    v.wait_ge(cT.sem, cT.n)
                    cV.tick(v.tensor_copy(kTc[:, :, 128 * T:128 * (T + 1)],
                                          pTg[:].rearrange("p (k t) -> p k t", t=128)))
            oT_done = cV.n

            for T in range(NT):
                for n in range(2):
                    @E("tensor")
                    def _(t, T=T, n=n, oT_done=oT_done):
                        t.wait_ge(cV.sem, oT_done if (T == 0 and n == 0) else cV.n)
                        pg = pA if n == 0 else pB
                        for k in range(8):
                            ins = t.matmul(pg[:], kTc[:, k, 128 * T:128 * (T + 1)],
                                           W_ring[:, 0 + k, 512 * n:512 * (n + 1)],
                                           start=(k == 0), stop=(k == 7))
                        cT.tick(ins)

                    @E("vector")
                    def _(v, T=T, n=n, i=i):
                        v.wait_ge(cT.sem, cT.n)
                        pg = pA if n == 0 else pB
                        dst = SCR[:, T, 512 * n:512 * (n + 1)]
                        v.tensor_tensor(dst, pg[:], ga_b[:, 512 * n:512 * (n + 1)], ALU.mult)
                        cV.tick(v.tensor_tensor(dst, dst, x_sb[i % 2][:, T, 512 * n:512 * (n + 1)],
                                                ALU.add))
            xmid_done = cV.n
            x_free[buf] = (cV, xmid_done)

            @E("vector")
            def _(v, xmid_done=xmid_done):
                v.wait_ge(cV.sem, xmid_done)
                v.reduce_sum(stat_s[:, 0:1], SCR[:, 0, :], axis=AX.X)
                cV.tick(v.reduce_sum(stat_s[:, 1:2], SCR[:, 1, :], axis=AX.X))

            @E("scalar")
            def _(s):
                s.wait_ge(cV.sem, cV.n)
                s.activation(sqd[:], SCR[:, 0, :], AF.Square, accum_out=stat_s[:, 2:3])
                cS.tick(s.activation(sqd[:], SCR[:, 1, :], AF.Square, accum_out=stat_s[:, 3:4]))

            @E("vector")
            def _(v):
                v.wait_ge(cS.sem, cS.n)
                v.tensor_scalar(stat_s[:, 4:6], stat_s[:, 0:2], 1.0 / HID, None, ALU.mult)
                v.drain()
                v.tensor_tensor(stat_s[:, 6:8], stat_s[:, 4:6], stat_s[:, 4:6], ALU.mult)
                v.tensor_scalar(stat_s[:, 8:10], stat_s[:, 2:4], 1.0 / HID, 1e-5, ALU.mult, ALU.add)
                v.drain()
                cV.tick(v.tensor_tensor(stat_s[:, 8:10], stat_s[:, 8:10], stat_s[:, 6:8], ALU.subtract))

            @E("scalar")
            def _(s):
                s.wait_ge(cV.sem, cV.n)
                cS.tick(s.activation(stat_s[:, 10:12], stat_s[:, 8:10], AF.Sqrt))

            @E("vector")
            def _(v):
                v.wait_ge(cS.sem, cS.n)
                v.reciprocal(stat_s[:, 12:14], stat_s[:, 10:12])
                v.drain()
                for T in range(NT):
                    v.tensor_scalar(sqd[:], SCR[:, T, :], stat_s[:, 4 + T:5 + T],
                                    stat_s[:, 12 + T:13 + T], ALU.subtract, ALU.mult)
                    v.tensor_tensor(sqd[:], sqd[:], cb1_b[:], ALU.mult)
                    ins = v.tensor_tensor(TOKC[:, T, :], sqd[:], sbv_b[:], ALU.add)
                cV.tick(ins)
            h2_done = cV.n

            @E("sync")
            def _(sy, orow=orow, h2_done=h2_done):
                sy.wait_ge(cV.sem, h2_done)
                dma(sy, dS, d_xmid[orow:orow + SEG, :].rearrange("(T p) d -> p T d", p=128), SCR[:])
                dma(sy, dS, d_h2[orow:orow + SEG, :].rearrange("(T p) d -> p T d", p=128), TOKC[:])

            if taps:
                @E("sync")
                def _(sy, orow=orow, i=i, fir_done=fir_done, stats_done=stats_done,
                      gate_done=gate_done, mix_done=mix_done, xmid_done=xmid_done, h2_done=h2_done):
                    rr = lambda d: d.rearrange("(T p) d -> p T d", p=128)
                    if "fir_s" in dbg:
                        sy.wait_ge(cV.sem, fir_done)
                        dma(sy, dS, rr(dbg["fir_s"][orow:orow + SEG, :]), TOKB[:])
                    if "fir_l" in dbg:
                        sy.wait_ge(cV.sem, fir_done)
                        dma(sy, dS, rr(dbg["fir_l"][orow:orow + SEG, :]), TOKC[:])
                    if "stat" in dbg:
                        sy.wait_ge(cV.sem, stats_done)
                        dma(sy, dS, dbg["stat"][2 * (i - own0):2 * (i - own0) + 2]
                            .rearrange("T p h d -> p T h d"), statv[:])
                    if "wgt" in dbg:
                        sy.wait_ge(cV.sem, gate_done)
                        dma(sy, dS, dbg["wgt"][2 * (i - own0):2 * (i - own0) + 2]
                            .rearrange("T p h e -> p T h e"), wgt[:])
                    if "o_mix" in dbg:
                        sy.wait_ge(cV.sem, mix_done)
                        dma(sy, dS, rr(dbg["o_mix"][orow:orow + SEG, :]), h_bf[:])
                    if "x_mid" in dbg:
                        sy.wait_ge(cV.sem, xmid_done)
                        dma(sy, dS, rr(dbg["x_mid"][orow:orow + SEG, :]), SCR[:])
                    if "h2" in dbg:
                        sy.wait_ge(cV.sem, h2_done)
                        dma(sy, dS, rr(dbg["h2"][orow:orow + SEG, :]), TOKC[:])

            if i + 1 < nseg:
                @E("gpsimd")
                def _(g):
                    g.wait_ge(cT.sem, cT.n)
                    dma(g, dGm, W_ring[:, 0:8, :], KT(dWq))
                    dma(g, dGm, W_ring[:, 8:16, :], KT(dWk))
                    dma(g, dGm, W_ring[:, 16:24, :], KT(dWv))
                w1_ld[i + 1] = dGm.n

    paT, paV, paS, paG, paD, paGm = cT.n, cV.n, cS.n, cG.n, dS.n, dGm.n
    pa.close()

    pb = contextlib.ExitStack()

    def sbb(name, shape, dt=F32):
        return pb.enter_context(nc.sbuf_tensor(name, shape, dt))

    h2T = sbb("h2T", [128, 8, PB_TOK], BF16)
    h2T2 = sbb("h2T2", [128, 8, PB_TOK], BF16)
    acc = sbb("acc", [128, PB_TOK // 128, HID])
    wb1 = [sbb("wb1_0", [128, 8, EI], BF16), sbb("wb1_1", [128, 8, EI], BF16)]
    wb3 = [sbb("wb3_0", [128, 8, EI], BF16), sbb("wb3_1", [128, 8, EI], BF16)]
    wb2 = [sbb("wb2_0", [128, 6, HID], BF16), sbb("wb2_1", [128, 6, HID], BF16)]
    SCB1 = sbb("SCB1", [128, 2 * (PB_TOK // 128), NE])
    SCB2 = sbb("SCB2", [128, 2 * (PB_TOK // 128), NE])
    mrscr = sbb("mrscr", [128, NE])
    mx8 = sbb("mx8", [128, 8])
    aT = sbb("aT", [128, 6, PB_CH], BF16)
    gtmp = sbb("gtmp", [128, 2, PB_CH], BF16)
    yscr = sbb("yscr", [128, 2, 512])
    xmb = sbb("xmb", [128, 2, HID])
    obuf = sbb("obuf", [128, 2, HID])

    pG = [pb.enter_context(nc.psum_tensor("pG0", [128, 512], F32)),
          pb.enter_context(nc.psum_tensor("pG1", [128, 512], F32))]
    pU = [pb.enter_context(nc.psum_tensor("pU0", [128, 512], F32)),
          pb.enter_context(nc.psum_tensor("pU1", [128, 512], F32))]
    pO = [pb.enter_context(nc.psum_tensor("pO0", [128, 512], F32)),
          pb.enter_context(nc.psum_tensor("pO1", [128, 512], F32))]

    NPT = PB_TOK // 128
    exp_seq = []
    gu_free = {}
    po_free = {}
    at_free = {0: 0}
    h2Tb = [h2T, h2T2]

    h2t_ld = []
    for p in range(npass):
        @E("scalar")
        def _(s, p=p):
            if p == 0:
                s.wait_ge(cT.sem, paT)
                s.wait_ge(cV.sem, paV)
                s.wait_ge(cG.sem, paG)
                s.wait_ge(dS.sem, paD)
                s.wait_ge(dGm.sem, paGm)
            for k in range(8):
                dma(s, dA, h2Tb[p][:, k, :],
                    d_h2[p * PB_TOK:(p + 1) * PB_TOK, 128 * k:128 * (k + 1)], transpose=True)
        h2t_ld.append(dA.n)

    comb_done = []
    for p in range(npass):
        @E("tensor")
        def _(t, p=p):
            t.wait_ge(dA.sem, h2t_ld[p])
            if p == 0:
                t.wait_ge(cV.sem, paV)
                t.wait_ge(cS.sem, paS)
            else:
                t.wait_ge(cS.sem, cS.n)
            for tt in range(NPT):
                for k in range(8):
                    ins = t.matmul(pO[0][:, NE * tt:NE * (tt + 1)],
                                   h2Tb[p][:, k, 128 * tt:128 * (tt + 1)],
                                   gatewT[:, k, :], start=(tt == 0 and k == 0),
                                   stop=(tt == NPT - 1 and k == 7))
            cT.tick(ins)

        @E("scalar")
        def _(s, p=p):
            s.wait_ge(cT.sem, cT.n)
            cS.tick(s.activation(SCB1[:, p * NPT:(p + 1) * NPT, :].rearrange("p T e -> p (T e)"),
                                 pO[0][:, 0:NE * NPT], AF.Sigmoid))

        @E("vector")
        def _(v, p=p):
            v.wait_ge(cS.sem, cS.n)
            if p == 0:
                v.wait_ge(cG.sem, paG)
                v.wait_ge(dGm.sem, paGm)
            v.memset(mx8[:, 1:8], 3.0e38)
            for tt in range(p * NPT, (p + 1) * NPT):
                v.tensor_tensor(SCB2[:, tt, :], SCB1[:, tt, :], ebias_b[:], ALU.add)
                v.drain()
                for r in range(4):
                    src = SCB2[:, tt, :] if r % 2 == 0 else mrscr[:]
                    dstm = mrscr[:] if r % 2 == 0 else SCB2[:, tt, :]
                    v.reduce_max(mx8[:, 0:1], src, axis=AX.X)
                    v.drain()
                    v.match_replace(dstm, mx8[:], src, -1e30)
                    v.drain()
                v.tensor_scalar(mrscr[:], SCB2[:, tt, :], -1e29, None, ALU.is_le)
                v.drain()
                v.tensor_tensor(mrscr[:], mrscr[:], SCB1[:, tt, :], ALU.mult)
                v.drain()
                v.reduce_sum(mx8[:, 0:1], mrscr[:], axis=AX.X)
                v.drain()
                v.reciprocal(mx8[:, 1:2], mx8[:, 0:1])
                v.drain()
                v.tensor_scalar(SCB1[:, tt, :], mrscr[:], mx8[:, 1:2], None, ALU.mult)
                ins = v.memset(mx8[:, 1:8], 3.0e38)
            cV.tick(ins)
        comb_done.append(cV.n)

        if "combine" in dbg:
            @E("sync")
            def _(sy, p=p, cd=comb_done[p]):
                sy.wait_ge(cV.sem, cd)
                dma(sy, dS, dbg["combine"][p * PB_TOK:(p + 1) * PB_TOK, :]
                    .rearrange("(T p) e -> p T e", p=128), SCB1[:, p * NPT:(p + 1) * NPT, :])

    for p in range(npass):
        pbase = p * PB_TOK
        h2p = h2Tb[p]

        @E("vector")
        def _(v, cd=comb_done[p]):
            v.wait_ge(cV.sem, max(cd, cV.n))
            cV.tick(v.memset(acc[:], 0.0))
        acc_zero = cV.n

        for e in range(nexp):
            ebuf = e % 2

            @E("gpsimd")
            def _(g, p=p, e=e, ebuf=ebuf):
                if p == 0 and e == 0:
                    g.wait_ge(cV.sem, paV)
                    g.wait_ge(cS.sem, paS)
                    g.wait_ge(cT.sem, paT)
                    g.wait_ge(dS.sem, paD)
                if len(exp_seq) >= 2:
                    g.wait_ge(cT.sem, exp_seq[-2])
                dma(g, dGm, wb1[ebuf][:].rearrange("p k i -> p (k i)"), dw1[e])
                dma(g, dGm, wb3[ebuf][:].rearrange("p k i -> p (k i)"), dw3[e])
                dma(g, dGm, wb2[ebuf][:].rearrange("p k i -> p (k i)"), dw2[e])
            wexp = dGm.n

            for ch in range(PB_TOK // PB_CH):
                cbase = PB_CH * ch
                s_base = cS.n
                ch_mms = []
                for ic in range(6):
                    gp = ic % 2

                    @E("tensor")
                    def _(t, e=e, ebuf=ebuf, ic=ic, gp=gp, cbase=cbase, wexp=wexp,
                          acc_zero=acc_zero, h2p=h2p):
                        t.wait_ge(dGm.sem, wexp)
                        t.wait_ge(cV.sem, max(acc_zero, gu_free.get(gp, 0)))
                        for k in range(8):
                            t.matmul(pG[gp][:], wb1[ebuf][:, k, 128 * ic:128 * (ic + 1)],
                                     h2p[:, k, cbase:cbase + PB_CH], start=(k == 0), stop=(k == 7))
                        for k in range(8):
                            ins = t.matmul(pU[gp][:], wb3[ebuf][:, k, 128 * ic:128 * (ic + 1)],
                                           h2p[:, k, cbase:cbase + PB_CH], start=(k == 0), stop=(k == 7))
                        cT.tick(ins)
                    ch_mms.append(cT.n)

                    @E("vector")
                    def _(v, ic=ic, gp=gp, w=s_base + ic + 1):
                        v.wait_ge(cS.sem, w)
                        if ic == 0:
                            v.wait_ge(cT.sem, at_free[0])
                        cV.tick(v.tensor_tensor(aT[:, ic, :], gtmp[:, gp, :], pU[gp][:], ALU.mult))
                    gu_free[gp] = cV.n
                a_done = cV.n

                @E("scalar")
                def _(s, mms=tuple(ch_mms)):
                    for ic, mmv in enumerate(mms):
                        s.wait_ge(cT.sem, mmv)
                        cS.tick(s.activation(gtmp[:, ic % 2, :], pG[ic % 2][:], AF.Silu))
                assert cS.n == s_base + 6

                for Mt in range(PB_CH // 128):
                    tt = (cbase // 128) + Mt
                    for n in range(2):
                        ob = (Mt * 2 + n) % 2

                        @E("tensor")
                        def _(t, ebuf=ebuf, Mt=Mt, n=n, ob=ob, a_done=a_done):
                            t.wait_ge(cV.sem, max(a_done, po_free.get(ob, 0)))
                            for kc in range(6):
                                ins = t.matmul(pO[ob][:], aT[:, kc, 128 * Mt:128 * (Mt + 1)],
                                               wb2[ebuf][:, kc, 512 * n:512 * (n + 1)],
                                               start=(kc == 0), stop=(kc == 5))
                            cT.tick(ins)
                        w2_done = cT.n
                        at_free[0] = cT.n

                        @E("vector")
                        def _(v, e=e, tt=tt, n=n, ob=ob, w2_done=w2_done, p=p):
                            v.wait_ge(cT.sem, w2_done)
                            dst = acc[:, tt, 512 * n:512 * (n + 1)]
                            cV.tick(v.scalar_tensor_tensor(dst, pO[ob][:],
                                                           SCB1[:, p * NPT + tt, e:e + 1],
                                                           dst, ALU.mult, ALU.add))
                        po_free[ob] = cV.n
            exp_seq.append(cT.n)
        moe_done = cV.n

        for g2 in range(NPT // 2):
            @E("sync")
            def _(sy, pbase=pbase, g2=g2, moe_done=moe_done):
                sy.wait_ge(cV.sem, moe_done if g2 == 0 else cV.n)
                r0 = pbase + 256 * g2
                dma(sy, dS, xmb[:], d_xmid[r0:r0 + 256, :].rearrange("(T p) d -> p T d", p=128))
            xm_ld = dS.n

            @E("vector")
            def _(v, g2=g2, xm_ld=xm_ld):
                v.wait_ge(dS.sem, xm_ld)
                for T in range(2):
                    tt = 2 * g2 + T
                    v.tensor_tensor(obuf[:, T, :], acc[:, tt, :], gb_b[:], ALU.mult)
                    ins = v.tensor_tensor(obuf[:, T, :], obuf[:, T, :], xmb[:, T, :], ALU.add)
                cV.tick(ins)

            @E("sync")
            def _(sy, pbase=pbase, g2=g2):
                sy.wait_ge(cV.sem, cV.n)
                r0 = pbase + 256 * g2
                dma(sy, dS, dout[r0:r0 + 256, :].rearrange("(T p) d -> p T d", p=128), obuf[:])

    @E("sync")
    def _(sy):
        sy.wait_ge(dS.sem, dS.n)

    pb.close()
    es.close()
    return nc


def _host_constants():
    import ml_dtypes
    ident = np.eye(128, dtype=ml_dtypes.bfloat16)
    identf = np.eye(128, dtype=np.float32)
    mA = np.zeros((128, 128), np.float32)
    mAU = np.zeros((128, 128), np.float32)
    mT = np.zeros((128, 128), np.float32)
    for c in range(4):
        s = slice(32 * c, 32 * (c + 1))
        tril = np.tril(np.ones((32, 32), np.float32), -1)
        mA[s, s] = -tril
        mAU[s, s] = -tril.T
        mT[s, s] = np.triu(np.ones((32, 32), np.float32))
    return ident, identf, mA, mAU, mT


def make_in_maps(inp, taps=()):
    ident, identf, mA, mAU, mT = _host_constants()
    f32 = lambda a: np.ascontiguousarray(np.asarray(a, np.float32))
    conv_taps = np.stack([f32(inp["L0_attn_qc_w"]), f32(inp["L0_attn_kc_w"]),
                          f32(inp["L0_attn_vc_w"])])
    convt = np.ascontiguousarray(conv_taps.reshape(3, 8, 128, 4).transpose(2, 0, 1, 3))
    firs = np.ascontiguousarray(f32(inp["L0_attn_fir_s"]).reshape(1024, 3)
                                .reshape(8, 128, 3).transpose(1, 0, 2))
    firl = np.ascontiguousarray(f32(inp["L0_attn_fir_l"]).reshape(1024, 31)
                                .reshape(8, 128, 31).transpose(1, 0, 2))
    shared = {
        "Wq": f32(inp["L0_attn_Wq"]), "Wk": f32(inp["L0_attn_Wk"]),
        "Wv": f32(inp["L0_attn_Wv"]), "Wb": f32(inp["L0_attn_Wb"]),
        "Wo": f32(inp["L0_attn_Wo"]),
        "convtaps": convt, "firs": firs, "firl": firl,
        "gW1": f32(inp["L0_attn_gW1"]), "gb1": f32(inp["L0_attn_gb1"]).reshape(1, 512),
        "gW2": f32(inp["L0_attn_gW2"]), "gb2": f32(inp["L0_attn_gb2"]).reshape(1, 4),
        "temp": f32(inp["L0_attn_temp"]).reshape(1, 4),
        "floorv": f32(inp["L0_attn_eps_floor"]).reshape(1, 16),
        "onorm": f32(inp["L0_attn_onorm_w"]).reshape(1, DV),
        "wmod": f32(inp["L0_wmod"]),
        "gate_wT": np.ascontiguousarray(f32(inp["L0_ffn_gate_w"]).T),
        "e_bias": f32(inp["L0_ffn_e_bias"]).reshape(1, NE),
        "w1": np.ascontiguousarray(f32(inp["L0_ffn_w1"]).reshape(NE, 8, 128, EI)
                                   .transpose(0, 2, 1, 3).reshape(NE, 128, 8 * EI)),
        "w3": np.ascontiguousarray(f32(inp["L0_ffn_w3"]).reshape(NE, 8, 128, EI)
                                   .transpose(0, 2, 1, 3).reshape(NE, 128, 8 * EI)),
        "w2": np.ascontiguousarray(f32(inp["L0_ffn_w2"]).reshape(NE, 6, 128, HID)
                                   .transpose(0, 2, 1, 3).reshape(NE, 128, 6 * HID)),
        "identity": ident, "identityf": identf, "maskA": mA, "maskAU": mAU, "maskT": mT,
    }
    x = f32(inp["L0_attn_x"])
    cfull = f32(inp["L0_c"])
    in_maps = []
    for core in range(8):
        b, s = core // 2, core % 2
        xb = x[b] if s == 1 else np.ascontiguousarray(np.roll(x[b], HALF, axis=0))
        mask = np.ones((L,), np.float32)
        if s == 0:
            mask[:HALF] = 0.0
        m = dict(shared)
        m["x"] = xb
        m["tokmask"] = np.ascontiguousarray(mask.reshape(L // 128, 128).T)
        m["c_bT"] = np.ascontiguousarray(cfull[b].reshape(8, 128).T)
        in_maps.append(m)
    return in_maps


def kernel(**inp):
    nc = build_program()
    in_maps = make_in_maps(inp)
    res = run_bass_kernel_spmd(nc, in_maps, core_ids=list(range(8)))
    out = np.empty((4, L, HID), np.float32)
    for core in range(8):
        b, s = core // 2, core % 2
        out[b, HALF * s:HALF * (s + 1)] = res.results[core]["out"]
    return out
